# version 05
# baseline (speedup 1.0000x reference)
import numpy as np
import concourse.bass as bass
import concourse.mybir as mybir
from concourse.alu_op_type import AluOpType as A
from concourse.bass_utils import run_bass_kernel_spmd

F32 = mybir.dt.float32
BF16 = mybir.dt.bfloat16
AF = mybir.ActivationFunctionType
AX = mybir.AxisListType

NCORES = 8
T = 2048
D = 1024
NT = T // 128
ND = D // 128
H = 16
DK = 64
NPAIR = 8
FF = 4096
NJ = FF // 128

C32 = 1.5 * 2.0 ** 23
C64 = 1.5 * 2.0 ** 52
E_NEG10 = float(np.float32(np.exp(np.float64(-10.0))))
E_POS10 = float(np.float32(np.exp(np.float64(10.0))))
LN_EPS = 1e-5
INV_D = 1.0 / D
QA = 32767.0
QW = 127.0


class Sem:
    def __init__(self, handle):
        self.h = handle
        self.n = 0

    def inc(self, k=1):
        self.n += k
        return self.n


class Rot:
    def __init__(self, nslots):
        self.k = nslots
        self.marks = []

    def acquire(self):
        u = len(self.marks)
        wait = self.marks[u - self.k] if u >= self.k else None
        self.marks.append(None)
        return u, u % self.k, wait

    def release(self, u, mark):
        self.marks[u] = mark


def _wait(eng, w):
    if w is not None:
        eng.wait_ge(w[0].h, w[1])


class Prog:
    ENGINES = ("sync", "tensor", "vector", "scalar", "gpsimd")

    def __init__(self, nc):
        self.nc = nc
        self.ops = {e: [] for e in self.ENGINES}
        self.sems = {}

    def sem(self, name):
        if name not in self.sems:
            h = self.nc.ctx.enter_context(self.nc.semaphore(name))
            self.sems[name] = Sem(h)
        return self.sems[name]

    def on(self, eng, fn):
        self.ops[eng].append(fn)

    def attach(self):
        cm = self.nc.Block()
        block = cm.__enter__()

        @block.sync
        def _(eng):
            for fn in self.ops["sync"]:
                fn(eng)

        @block.tensor
        def _(eng):
            for fn in self.ops["tensor"]:
                fn(eng)

        @block.vector
        def _(eng):
            for fn in self.ops["vector"]:
                fn(eng)

        @block.scalar
        def _(eng):
            for fn in self.ops["scalar"]:
                fn(eng)

        @block.gpsimd
        def _(eng):
            for fn in self.ops["gpsimd"]:
                fn(eng)

        cm.__exit__(None, None, None)


def sb(nc, name, shape, dt):
    return nc.ctx.enter_context(nc.sbuf_tensor(name, list(shape), dt))


def psum(nc, name, shape, dt):
    return nc.ctx.enter_context(nc.psum_tensor(name, list(shape), dt))



def emit_kappa(P, E):
    vv = P.sem("vv")

    def go(v, KAP=E["KAP"], KC64=E["KC64"], DD=E["DD"]):
        v.memset(KAP[:], 1.0)
        v.drain()
        v.tensor_scalar(out=KAP[:], in0=KAP[:], scalar1=C64, scalar2=None, op0=A.add)
        v.drain()
        v.tensor_scalar(out=KAP[:], in0=KAP[:], scalar1=-C64, scalar2=None, op0=A.add)
        v.drain()
        v.tensor_scalar(out=KC64[:], in0=KAP[:], scalar1=C64, scalar2=None, op0=A.mult)
        v.drain()
        v.tensor_scalar(out=DD[:], in0=KC64[:], scalar1=C32, scalar2=-1.0,
                        op0=A.add, op1=A.mult)
        v.drain().then_inc(vv.h, 1)
    P.on("vector", go)
    return ("vv", vv.inc())


def emit_ln(P, E, X, XLN, waits):
    vv, ev = P.sem("vv"), P.sem("ev")
    SU, SQ, M, V, R, MSQ, SQS = (E[k] for k in ("SU", "SQ", "M", "V", "R", "MSQ", "SQS"))

    ev0 = P.sem("ev")

    def sq1(s, X=X, waits=tuple(waits)):
        for w in waits:
            _wait(s, w)
        for tt in range(NT):
            s.activation(out=SQS[:], in_=X[:, 1024 * tt:1024 * (tt + 1)],
                         func=AF.Square, accum_out=SQ[:, tt:tt + 1])
            s.drain()
        s.drain().then_inc(ev0.h, 1)
    P.on("scalar", sq1)
    m_sq = ev0.inc()

    def d1(v, X=X, waits=tuple(waits), m_sq=m_sq):
        for w in waits:
            _wait(v, w)
        for tt in range(NT):
            xs = X[:, 1024 * tt:1024 * (tt + 1)]
            v.reduce_sum(out=SU[:, tt:tt + 1], in_=xs, axis=AX.X)
        v.wait_ge(ev0.h, m_sq)
        v.drain()
        v.tensor_scalar(out=M[:], in0=SU[:], scalar1=INV_D, scalar2=None, op0=A.mult)
        v.tensor_scalar(out=V[:], in0=SQ[:], scalar1=INV_D, scalar2=None, op0=A.mult)
        v.drain()
        v.tensor_tensor(out=MSQ[:], in0=M[:], in1=M[:], op=A.mult)
        v.drain()
        v.scalar_tensor_tensor(out=V[:], in0=V[:], scalar=LN_EPS, in1=MSQ[:],
                               op0=A.add, op1=A.subtract)
        v.drain().then_inc(vv.h, 1)
    P.on("vector", d1)
    m_var = vv.inc()

    def s1(s, m_var=m_var):
        s.wait_ge(vv.h, m_var)
        s.activation(out=R[:], in_=V[:], func=AF.Sqrt)
        s.drain().then_inc(ev.h, 1)
    P.on("scalar", s1)
    m_sd = ev.inc()

    def d2(v, X=X, XLN=XLN, m_sd=m_sd):
        v.wait_ge(ev.h, m_sd)
        v.reciprocal(out=R[:], in_=R[:])
        v.drain()
        for tt in range(NT):
            v.tensor_scalar(out=XLN[:, 1024 * tt:1024 * (tt + 1)],
                            in0=X[:, 1024 * tt:1024 * (tt + 1)],
                            scalar1=M[:, tt:tt + 1], scalar2=R[:, tt:tt + 1],
                            op0=A.subtract, op1=A.mult)
        v.drain().then_inc(vv.h, 1)
    P.on("vector", d2)
    return (vv, vv.inc())


def emit_tm2fm(P, E, SRC, DST, wait, rot_psT, n_src_cols=1024, bf=True):
    mm, ev = P.sem("mm"), P.sem("ev")
    psT = E["psT"]
    nch = n_src_cols // 128
    last = None
    first = True
    for dc in range(nch):
        for g4 in range(NT // 4):
            u, slot, w = rot_psT.acquire()

            def pe(t, dc=dc, g4=g4, slot=slot, w=w, SRC=SRC, first=first,
                   wait=wait, ncols=n_src_cols):
                if first and wait is not None:
                    _wait(t, wait)
                _wait(t, w)
                for j in range(4):
                    tt = 4 * g4 + j
                    ins = t.transpose(psT[:, 1024 * slot + 128 * j:1024 * slot + 128 * (j + 1)],
                                      SRC[:, ncols * tt + 128 * dc:ncols * tt + 128 * (dc + 1)],
                                      E["IDB"][:, :])
                ins.then_inc(mm.h, 1)
            P.on("tensor", pe)
            first = False
            m1 = mm.inc()

            def sc(s, dc=dc, g4=g4, slot=slot, m1=m1, DST=DST):
                s.wait_ge(mm.h, m1)
                s.copy(out=DST[:, 2048 * dc + 512 * g4:2048 * dc + 512 * (g4 + 1)],
                       in_=psT[:, 1024 * slot:1024 * slot + 512])
                s.drain().then_inc(ev.h, 1)
            P.on("scalar", sc)
            m2 = ev.inc()
            rot_psT.release(u, (ev, m2))
            last = (ev, m2)
    return last


def emit_amax_partial(P, E, XLN, OUT11, wait):
    mm, ev, vv = P.sem("mm"), P.sem("ev"), P.sem("vv")
    AM, AMR, TR1 = E["AM16"], E["AMR"], E["TR1"]

    def d1(v, XLN=XLN, wait=wait):
        _wait(v, wait)
        for tt in range(NT):
            v.tensor_reduce(out=AM[:, tt:tt + 1], in_=XLN[:, 1024 * tt:1024 * (tt + 1)],
                            axis=AX.X, op=A.max, apply_absolute_value=True)
        v.drain()
        v.tensor_reduce(out=AMR[:], in_=AM[:], axis=AX.X, op=A.max,
                        apply_absolute_value=True)
        v.drain().then_inc(vv.h, 1)
    P.on("vector", d1)
    m0 = vv.inc()

    def pe(t, m0=m0):
        t.wait_ge(vv.h, m0)
        t.transpose(E["psS"][0:1, 0:128], AMR[:], E["IDF"][:, :]).then_inc(mm.h, 1)
    P.on("tensor", pe)
    m1 = mm.inc()

    def sc(s, m1=m1):
        s.wait_ge(mm.h, m1)
        s.copy(out=TR1[:], in_=E["psS"][0:1, 0:128])
        s.drain().then_inc(ev.h, 1)
    P.on("scalar", sc)
    m2 = ev.inc()

    def d2(v, m2=m2, OUT11=OUT11):
        v.wait_ge(ev.h, m2)
        v.tensor_reduce(out=OUT11[0:1, 0:1], in_=TR1[:], axis=AX.X, op=A.max,
                        apply_absolute_value=True)
        v.drain().then_inc(vv.h, 1)
    P.on("vector", d2)
    return (vv, vv.inc())


def _chunks_for_tkb(tkb):
    out = []
    c = 128 * tkb
    while c < T:
        e = min(T, (c // 512 + 1) * 512)
        out.append((c, e))
        c = e
    return out


def emit_attention(P, E, kind, Wg, XLT, AFOUT, xlt_wait, extras=None):
    nc = P.nc
    mm, ev, vv, gd = P.sem("mm"), P.sem("ev"), P.sem("vv"), P.sem("gd")
    psS, psO, psT = E["psS"], E["psO"], E["psT"]
    WPR, V65, VT, PT = E["WPR"], E["V65"], E["VT"], E["PT"]
    ZF, ZR, OF, ONESF = E["ZF"], E["ZR"], E["OF"], E["ONESF"]
    rot_ps1 = Rot(2)
    rot_wpr = Rot(2)
    rot_psT = Rot(2)
    rot_mrow = Rot(2)
    is_b = (kind == "b")
    if is_b:
        M16, MC4, MT = E["M16"], E["MC4"], E["MT"]
        HMG, HB = extras["HMG"], extras["HB"]
        MROW = extras["MROW"]
        QP = KP = None
    else:
        QP, KP = E["QP"], E["KP"]
        CM = E["CM"]

    pair_done = {}
    last_of_mark = None
    kh_ready = None
    if is_b:
        def khm(v, xlt_wait=xlt_wait):
            if xlt_wait is not None:
                _wait(v, xlt_wait)
            v.memset(E["KH0"][64:65, :], 1.0)
            v.memset(E["KH1"][64:65, :], 1.0)
            v.drain().then_inc(vv.h, 1)
        P.on("vector", khm)
        kh_ready = (vv, vv.inc())

    for p in range(NPAIR):
        uW, wslot, wwait = rot_wpr.acquire()
        wbase = 4096 * wslot

        def ld(g, p=p, wbase=wbase, wwait=wwait, Wg=Wg, pre=gd.n,
               xlt_wait=xlt_wait):
            g.wait_ge(gd.h, pre)
            if p < 2 and xlt_wait is not None:
                _wait(g, xlt_wait)
            _wait(g, wwait)
            for k in range(ND):
                g.dma_start(out=WPR[:, wbase + 512 * k:wbase + 512 * (k + 1)],
                            in_=Wg[128 * k:128 * (k + 1), 512 * p:512 * (p + 1)]
                            ).then_inc(gd.h, 16)
        P.on("gpsimd", ld)
        gmark = gd.inc(16 * ND)

        secs = [("q", 0, 128), ("k", 128, 128), ("v0", 256, 65), ("v1", 384, 65)]
        for sec, off, mw in secs:
            for n in range(4):
                u1, s1_, w1 = rot_ps1.acquire()
                pdst = psS[:, 512 * s1_:512 * (s1_ + 1)] if mw == 128 else \
                    psS[0:65, 512 * s1_:512 * s1_ + 512]

                def pe(t, p=p, off=off, mw=mw, n=n, s1_=s1_, w1=w1, gmark=gmark,
                       wbase=wbase, XLT=XLT, xlt_wait=xlt_wait, pdst=pdst):
                    t.wait_ge(gd.h, gmark)
                    if xlt_wait is not None:
                        _wait(t, xlt_wait)
                    _wait(t, w1)
                    for k in range(ND):
                        ins = t.matmul(pdst,
                                       WPR[:, wbase + 512 * k + off:wbase + 512 * k + off + mw],
                                       XLT[:, 2048 * k + 512 * n:2048 * k + 512 * (n + 1)],
                                       start=(k == 0), stop=(k == ND - 1))
                    ins.then_inc(mm.h, 1)
                P.on("tensor", pe)
                m1 = mm.inc()

                def sc(s, p=p, sec=sec, n=n, s1_=s1_, m1=m1):
                    s.wait_ge(mm.h, m1)
                    src128 = psS[:, 512 * s1_:512 * (s1_ + 1)]
                    src65 = psS[0:65, 512 * s1_:512 * s1_ + 512]
                    cols = slice(512 * n, 512 * (n + 1))
                    if not is_b:
                        if sec == "q":
                            s.copy(out=QP[:, cols], in_=src128)
                        elif sec == "k":
                            s.copy(out=KP[:, cols], in_=src128)
                        else:
                            hh = 0 if sec == "v0" else 1
                            s.activation(out=V65[:, 2048 * hh + 512 * n:2048 * hh + 512 * (n + 1)],
                                         in_=src65, func=AF.Identity,
                                         bias=E["VB"][:, :], scale=1.0)
                    else:
                        ci = 4 * p
                        if sec in ("q", "k"):
                            dsts = (E["QH0"], E["QH1"]) if sec == "q" else (E["KH0"], E["KH1"])
                            co = ci + (0 if sec == "q" else 1)
                            for hh in (0, 1):
                                s.activation(
                                    out=dsts[hh][0:64, cols],
                                    in_=psS[64 * hh:64 * hh + 64, 512 * s1_:512 * s1_ + 512],
                                    func=AF.Identity,
                                    scale=HMG[64 * hh:64 * hh + 64, co:co + 1],
                                    bias=HB[64 * hh:64 * hh + 64, co:co + 1])
                        else:
                            hh = 0 if sec == "v0" else 1
                            co = ci + 2 + hh
                            s.activation(out=V65[:, 2048 * hh + 512 * n:2048 * hh + 512 * (n + 1)],
                                         in_=src65, func=AF.Identity,
                                         scale=HMG[0:65, co:co + 1],
                                         bias=HB[0:65, co:co + 1])
                    s.drain().then_inc(ev.h, 1)
                P.on("scalar", sc)
                m2 = ev.inc()
                rot_ps1.release(u1, (ev, m2))
        qkv_done = (ev, ev.n)

        for hh in (0, 1):
            for c4 in range(4):
                u2, s2_, w2 = rot_psT.acquire()

                def pe(t, hh=hh, c4=c4, s2_=s2_, w2=w2, qkv_done=qkv_done):
                    _wait(t, qkv_done)
                    _wait(t, w2)
                    for j in range(4):
                        c = 4 * c4 + j
                        ins = t.transpose(psT[:, 1024 * s2_ + 128 * j:1024 * s2_ + 128 * j + 65],
                                          V65[0:65, 2048 * hh + 128 * c:2048 * hh + 128 * (c + 1)],
                                          E["IDB"][0:65, 0:65])
                    ins.then_inc(mm.h, 1)
                P.on("tensor", pe)
                m1 = mm.inc()

                def sc(s, hh=hh, c4=c4, s2_=s2_, m1=m1):
                    s.wait_ge(mm.h, m1)
                    src = psT[:, 1024 * s2_:1024 * s2_ + 512].rearrange(
                        "p (j c) -> p j c", j=4)[:, :, 0:65]
                    dst = VT[:, 2080 * hh + 260 * c4:2080 * hh + 260 * (c4 + 1)].rearrange(
                        "p (j c) -> p j c", j=4)
                    s.copy(out=dst, in_=src)
                    s.drain().then_inc(ev.h, 1)
                P.on("scalar", sc)
                m2 = ev.inc()
                rot_psT.release(u2, (ev, m2))
        vt_done = (ev, ev.n)

        for hh in (0, 1):
            h = 2 * p + hh
            if is_b:
                QHh = E["QH0"] if hh == 0 else E["QH1"]
                KHh = E["KH0"] if hh == 0 else E["KH1"]
                for tqb in range(NT):
                    for cc in range(4):
                        u1, s1_, w1 = rot_ps1.acquire()

                        def pe(t, tqb=tqb, cc=cc, s1_=s1_, w1=w1, QHh=QHh, KHh=KHh,
                               vt_done=vt_done):
                            _wait(t, vt_done)
                            _wait(t, w1)
                            t.matmul(psS[:, 512 * s1_:512 * (s1_ + 1)],
                                     QHh[0:64, 128 * tqb:128 * (tqb + 1)],
                                     KHh[0:64, 512 * cc:512 * (cc + 1)],
                                     start=True, stop=True).then_inc(mm.h, 1)
                        P.on("tensor", pe)
                        m1 = mm.inc()

                        def dv(v, cc=cc, s1_=s1_, m1=m1):
                            v.wait_ge(mm.h, m1)
                            v.tensor_reduce(out=MC4[:, cc:cc + 1],
                                            in_=psS[:, 512 * s1_:512 * (s1_ + 1)],
                                            axis=AX.X, op=A.max)
                            v.drain().then_inc(vv.h, 1)
                        P.on("vector", dv)
                        m2 = vv.inc()
                        rot_ps1.release(u1, (vv, m2))

                    def dv2(v, tqb=tqb):
                        v.tensor_reduce(out=M16[:, tqb:tqb + 1], in_=MC4[:],
                                        axis=AX.X, op=A.max)
                        v.drain()
                    P.on("vector", dv2)

                def dv3(v):
                    v.tensor_scalar(out=M16[:], in0=M16[:], scalar1=-1.0,
                                    scalar2=None, op0=A.mult)
                    v.drain().then_inc(vv.h, 1)
                P.on("vector", dv3)
                m3 = vv.inc()

                def pe2(t, m3=m3):
                    t.wait_ge(vv.h, m3)
                    t.transpose(psO[0:16, 0:128], M16[:], E["IDF"][:, :]).then_inc(mm.h, 1)
                P.on("tensor", pe2)
                m4 = mm.inc()

                def sc2(s, m4=m4):
                    s.wait_ge(mm.h, m4)
                    s.copy(out=MT[:], in_=psO[0:16, 0:128])
                    s.drain().then_inc(ev.h, 1)
                P.on("scalar", sc2)
                m5 = ev.inc()
                uM, mslot, wM = rot_mrow.acquire()

                def g1(g, m5=m5, mslot=mslot, wM=wM, MROW=MROW, pre=gd.n):
                    g.wait_ge(gd.h, pre)
                    g.wait_ge(ev.h, m5)
                    _wait(g, wM)
                    g.dma_start(out=MROW[mslot], in_=MT[:]).then_inc(gd.h, 16)
                P.on("gpsimd", g1)
                m6 = gd.inc(16)

                def g2(g, m6=m6, mslot=mslot, MROW=MROW, QHh=QHh, pre=gd.n):
                    g.wait_ge(gd.h, max(m6, pre))
                    g.dma_start(out=QHh[64:65, :],
                                in_=MROW[mslot].rearrange("(o a) b -> o (a b)", o=1)
                                ).then_inc(gd.h, 16)
                P.on("gpsimd", g2)
                m7 = gd.inc(16)
                rot_mrow.release(uM, (gd, m7))
                row_ready = (gd, m7)
            else:
                row_ready = vt_done

            tkbs = range(NT)
            av_started = False
            for tkb in tkbs:
                chunks = _chunks_for_tkb(tkb) if not is_b else [
                    (512 * i, 512 * (i + 1)) for i in range(4)]
                for (cs, ce) in chunks:
                    u1, s1_, w1 = rot_ps1.acquire()
                    plen = ce - cs
                    poff = 512 * s1_ + (cs % 512)

                    def pe(t, tkb=tkb, cs=cs, ce=ce, s1_=s1_, w1=w1, poff=poff,
                           hh=hh, row_ready=row_ready, kh_ready=kh_ready):
                        _wait(t, row_ready)
                        _wait(t, kh_ready)
                        _wait(t, w1)
                        if is_b:
                            lhs = (E["KH0"] if hh == 0 else E["KH1"])[0:65, 128 * tkb:128 * (tkb + 1)]
                            rhs = (E["QH0"] if hh == 0 else E["QH1"])[0:65, cs:ce]
                        else:
                            hb = 64 * hh
                            lhs = KP[hb:hb + 64, 128 * tkb:128 * (tkb + 1)]
                            rhs = QP[hb:hb + 64, cs:ce]
                        t.matmul(psS[:, poff:poff + (ce - cs)], lhs, rhs,
                                 start=True, stop=True).then_inc(mm.h, 1)
                    P.on("tensor", pe)
                    m1 = mm.inc()

                    def sc(s, cs=cs, ce=ce, poff=poff, m1=m1):
                        s.wait_ge(mm.h, m1)
                        s.activation(out=PT[:, cs:ce], in_=psS[:, poff:poff + (ce - cs)],
                                     func=AF.Exp, scale=0.125)
                        s.drain().then_inc(ev.h, 1)
                    P.on("scalar", sc)
                    m2 = ev.inc()
                    rot_ps1.release(u1, (ev, m2))

                    def dv(v, tkb=tkb, cs=cs, ce=ce, m2=m2, diag=(cs == 128 * tkb)):
                        v.wait_ge(ev.h, m2)
                        if not is_b:
                            v.tensor_scalar(out=PT[:, cs:ce], in0=PT[:, cs:ce],
                                            scalar1=E_NEG10, scalar2=E_POS10,
                                            op0=A.max, op1=A.min)
                            if diag:
                                v.drain()
                                v.tensor_tensor(out=PT[:, cs:cs + 128],
                                                in0=PT[:, cs:cs + 128],
                                                in1=E["CM"][:, :], op=A.mult)
                        v.drain().then_inc(vv.h, 1)
                    P.on("vector", dv)
                    m3 = vv.inc()

                    def peav(t, tkb=tkb, cs=cs, ce=ce, m3=m3, hh=hh,
                             startflag=(tkb == 0)):
                        t.wait_ge(vv.h, m3)
                        t.matmul(psO[0:65, cs:ce],
                                 VT[:, 2080 * hh + 65 * tkb:2080 * hh + 65 * (tkb + 1)],
                                 PT[:, cs:ce], start=startflag, stop=(tkb == NT - 1),
                                 skip_group_check=True).then_inc(mm.h, 1)
                    P.on("tensor", peav)
                    mm.inc()
                    av_started = av_started or True
                av_started = True
            av_done = (mm, mm.n)

            def sc3(s, av_done=av_done):
                _wait(s, av_done)
                s.copy(out=ZF[:], in_=psO[64:65, :])
                s.drain().then_inc(ev.h, 1)
            P.on("scalar", sc3)
            m8 = ev.inc()

            def dv4(v, m8=m8):
                v.wait_ge(ev.h, m8)
                v.reciprocal(out=ZR[:], in_=ZF[:])
                v.drain().then_inc(vv.h, 1)
            P.on("vector", dv4)
            m9 = vv.inc()

            for n in range(4):
                u1, s1_, w1 = rot_ps1.acquire()

                def pe3(t, n=n, s1_=s1_, w1=w1, m9=m9):
                    t.wait_ge(vv.h, m9)
                    _wait(t, w1)
                    t.matmul(psS[0:64, 512 * s1_:512 * s1_ + 512],
                             ONESF[0:1, 0:64], ZR[0:1, 512 * n:512 * (n + 1)],
                             start=True, stop=True).then_inc(mm.h, 1)
                P.on("tensor", pe3)
                m10 = mm.inc()

                def sc4(s, n=n, m10=m10):
                    s.wait_ge(mm.h, m10)
                    s.copy(out=OF[:, 512 * n:512 * (n + 1)], in_=psO[0:64, 512 * n:512 * (n + 1)])
                    s.drain().then_inc(ev.h, 1)
                P.on("scalar", sc4)
                m11 = ev.inc()

                def dv5(v, n=n, h=h, s1_=s1_, m10=m10, m11=m11, AFOUT=AFOUT):
                    v.wait_ge(mm.h, m10)
                    v.wait_ge(ev.h, m11)
                    rb = 64 * (h % 2)
                    v.tensor_tensor(out=AFOUT[rb:rb + 64, 2048 * (h // 2) + 512 * n:
                                              2048 * (h // 2) + 512 * (n + 1)],
                                    in0=OF[:, 512 * n:512 * (n + 1)],
                                    in1=psS[0:64, 512 * s1_:512 * s1_ + 512],
                                    op=A.mult)
                    v.drain().then_inc(vv.h, 1)
                P.on("vector", dv5)
                m12 = vv.inc()
                rot_ps1.release(u1, (vv, m12))
            head_done = (vv, vv.n)
            pair_done[p] = (mm, mm.n)
            def fence(t, head_done=head_done):
                t.wait_ge(head_done[0].h, head_done[1])
            P.on("tensor", fence)
        rot_wpr.release(uW, (mm, mm.n))
        last_of_mark = (vv, vv.n)
    return last_of_mark


def emit_proj_residual(P, E, Wt, AFIN, X, af_wait, nk=ND, WOt=None, BIAS=None):
    mm, ev, vv, gd = P.sem("mm"), P.sem("ev"), P.sem("vv"), P.sem("gd")
    psS, psO, PRF = E["psS"], E["psO"], E["PRF"]
    WO = WOt if WOt is not None else E["WO"]
    rot_wo = Rot(2)
    rot_ps2 = Rot(4)
    rot_prf = Rot(2)
    rot_ps1 = Rot(2)
    last = None
    for ec in range(ND):
        uW, wslot, wwait = rot_wo.acquire()

        def ld(g, ec=ec, wslot=wslot, wwait=wwait, Wt=Wt, pre=gd.n):
            g.wait_ge(gd.h, pre)
            _wait(g, wwait)
            for k in range(nk):
                g.dma_start(out=WO[:, 1024 * wslot + 128 * k:1024 * wslot + 128 * (k + 1)],
                            in_=Wt[128 * k:128 * (k + 1), 128 * ec:128 * (ec + 1)]
                            ).then_inc(gd.h, 16)
        P.on("gpsimd", ld)
        gmark = gd.inc(16 * nk)

        for n in range(4):
            u2, s2_, w2 = rot_ps2.acquire()

            def pe(t, ec=ec, n=n, s2_=s2_, w2=w2, gmark=gmark, wslot=wslot,
                   AFIN=AFIN, af_wait=af_wait):
                t.wait_ge(gd.h, gmark)
                if af_wait is not None:
                    _wait(t, af_wait)
                _wait(t, w2)
                for k in range(nk):
                    ins = t.matmul(psO[:, 512 * s2_:512 * (s2_ + 1)],
                                   WO[:, 1024 * wslot + 128 * k:1024 * wslot + 128 * (k + 1)],
                                   AFIN[:, 2048 * k + 512 * n:2048 * k + 512 * (n + 1)],
                                   start=(k == 0), stop=(k == nk - 1))
                ins.then_inc(mm.h, 1)
            P.on("tensor", pe)
            m1 = mm.inc()

            uP, sp, wp = rot_prf.acquire()

            def sc(s, ec=ec, s2_=s2_, sp=sp, wp=wp, m1=m1, BIAS=BIAS):
                s.wait_ge(mm.h, m1)
                _wait(s, wp)
                if BIAS is None:
                    s.copy(out=PRF[:, 512 * sp:512 * (sp + 1)],
                           in_=psO[:, 512 * s2_:512 * (s2_ + 1)])
                else:
                    s.activation(out=PRF[:, 512 * sp:512 * (sp + 1)],
                                 in_=psO[:, 512 * s2_:512 * (s2_ + 1)],
                                 func=AF.Identity, bias=BIAS[:, ec:ec + 1], scale=1.0)
                s.drain().then_inc(ev.h, 1)
            P.on("scalar", sc)
            m2 = ev.inc()
            rot_ps2.release(u2, (ev, m2))

            u1, s1_, w1 = rot_ps1.acquire()

            def pe2(t, sp=sp, s1_=s1_, w1=w1, m2=m2):
                t.wait_ge(ev.h, m2)
                _wait(t, w1)
                for j in range(4):
                    ins = t.transpose(psS[:, 512 * s1_ + 128 * j:512 * s1_ + 128 * (j + 1)],
                                      PRF[:, 512 * sp + 128 * j:512 * sp + 128 * (j + 1)],
                                      E["IDF"][:, :])
                ins.then_inc(mm.h, 1)
            P.on("tensor", pe2)
            m3 = mm.inc()

            def dv(v, ec=ec, n=n, s1_=s1_, m3=m3, X=X):
                v.wait_ge(mm.h, m3)
                for j in range(4):
                    tt = 4 * n + j
                    xsl = X[:, 1024 * tt + 128 * ec:1024 * tt + 128 * (ec + 1)]
                    v.tensor_tensor(out=xsl, in0=psS[:, 512 * s1_ + 128 * j:512 * s1_ + 128 * (j + 1)],
                                    in1=xsl, op=A.add)
                v.drain().then_inc(vv.h, 1)
            P.on("vector", dv)
            m4 = vv.inc()
            rot_prf.release(uP, (mm, m3))
            rot_ps1.release(u1, (vv, m4))
            last = (vv, m4)
        rot_wo.release(uW, (mm, mm.n))
    return last



def _common_tiles(nc, E, need_attn=True, attn_b=False, skip_tr1=False):
    E["KAP"] = sb(nc, "KAP", [128, 1], F32)
    E["KC64"] = sb(nc, "KC64", [128, 1], F32)
    E["DD"] = sb(nc, "DD", [128, 1], F32)
    for k in ("SU", "SQ", "M", "V", "R", "MSQ", "AM16"):
        E[k] = sb(nc, k, [128, 16], F32)
    E["SQS"] = sb(nc, "SQS", [128, 1024], F32)
    E["AMR"] = sb(nc, "AMR", [128, 1], F32)
    if not skip_tr1:
        E["TR1"] = sb(nc, "TR1", [1, 128], F32)
    E["IDB"] = sb(nc, "IDB", [128, 128], BF16)
    E["IDF"] = sb(nc, "IDF", [128, 128], F32)
    E["ONESF"] = sb(nc, "ONESF", [1, 128], F32)
    E["psS"] = psum(nc, "psS", [128, 1024], F32)
    E["psO"] = psum(nc, "psO", [128, 2048], F32)
    E["psT"] = psum(nc, "psT", [128, 2048], BF16)
    if need_attn:
        E["WPR"] = sb(nc, "WPR", [128, 8192], BF16)
        E["V65"] = sb(nc, "V65", [65, 4096], BF16)
        E["VT"] = sb(nc, "VT", [128, 4160], BF16)
        E["PT"] = sb(nc, "PT", [128, 2048], BF16)
        E["ZF"] = sb(nc, "ZF", [1, 2048], F32)
        E["ZR"] = sb(nc, "ZR", [1, 2048], F32)
        E["OF"] = sb(nc, "OF", [64, 2048], F32)
        E["VB"] = sb(nc, "VB", [65, 1], F32)
        if attn_b:
            E["QH0"] = sb(nc, "QH0", [65, 2048], BF16)
            E["QH1"] = sb(nc, "QH1", [65, 2048], BF16)
            E["KH0"] = sb(nc, "KH0", [65, 2048], BF16)
            E["KH1"] = sb(nc, "KH1", [65, 2048], BF16)
            E["M16"] = sb(nc, "M16", [128, 16], F32)
            E["MC4"] = sb(nc, "MC4", [128, 4], F32)
            E["MT"] = sb(nc, "MT", [16, 128], F32)
        else:
            E["QP"] = sb(nc, "QP", [128, 2048], BF16)
            E["KP"] = sb(nc, "KP", [128, 2048], BF16)
            E["CM"] = sb(nc, "CM", [128, 128], BF16)


def _emit_const_loads(P, E, consts, attn_b=False):
    gd = P.sem("gd")
    ident = consts["ident"]

    def g(gp, attn_b=attn_b):
        gp.dma_start(out=E["IDB"][:], in_=ident).then_inc(gd.h, 16)
        gp.dma_start(out=E["IDF"][:], in_=ident).then_inc(gd.h, 16)
        if not attn_b and "CM" in E:
            gp.dma_start(out=E["CM"][:], in_=consts["cmask"]).then_inc(gd.h, 16)
    P.on("gpsimd", g)
    n = gd.inc(16 * (3 if (not attn_b and "CM" in E) else 2))

    def v(vec):
        vec.memset(E["ONESF"][:], 1.0)
        if "VB" in E:
            vec.memset(E["VB"][:], 0.0)
            vec.drain()
            vec.memset(E["VB"][64:65, :], 1.0)
        vec.drain().then_inc(P.sem("vv").h, 1)
    P.on("vector", v)
    vvn = P.sem("vv").inc()
    return (P.sem("gd"), n), (P.sem("vv"), vvn)


DBG = {"tap": None}


def build_L1():
    nc = bass.Bass("TRN2", target_bir_lowering=False)
    x = nc.dram_tensor("x", [T, D], F32, kind="ExternalInput").ap()
    wg = nc.dram_tensor("wg_a", [D, NPAIR * 512], F32, kind="ExternalInput").ap()
    wot = nc.dram_tensor("wot_a", [D, D], F32, kind="ExternalInput").ap()
    ident = nc.dram_tensor("ident", [128, 128], F32, kind="ExternalInput").ap()
    cmask = nc.dram_tensor("cmask", [128, 128], F32, kind="ExternalInput").ap()
    x1 = nc.dram_tensor("x1", [T, D], F32, kind="ExternalOutput").ap()
    p1 = nc.dram_tensor("p1", [1, 1], F32, kind="ExternalOutput").ap()

    P = Prog(nc)
    E = {}
    _common_tiles(nc, E, need_attn=True, attn_b=False)
    E["X"] = sb(nc, "X", [128, NT * 1024], F32)
    E["XLN"] = sb(nc, "XLN", [128, NT * 1024], BF16)
    E["XLT"] = sb(nc, "XLT", [128, ND * 2048], BF16)
    E["AFOUT"] = E["XLN"]
    E["WO"] = sb(nc, "WO", [128, 2048], BF16)
    E["PRF"] = E["SQS"]
    E["P11"] = sb(nc, "P11", [1, 1], F32)

    sd, gd, vv = P.sem("sd"), P.sem("gd"), P.sem("vv")
    st = P.sem("st")

    def ldx(sy, x=x):
        sy.dma_start(out=E["X"][:, :].rearrange("p (t d) -> p t d", t=NT),
                     in_=x.rearrange("(t p) d -> p t d", p=128)).then_inc(sd.h, 16)
    P.on("sync", ldx)
    x_mark = sd.inc(16)

    (cg, cgn), cvv = _emit_const_loads(P, E, {"ident": ident, "cmask": cmask})

    P.on("tensor", lambda t, cg=cg, cgn=cgn, cvv=cvv: (
        t.wait_ge(cg.h, cgn), t.wait_ge(cvv[0].h, cvv[1])))
    P.on("scalar", lambda s, cg=cg, cgn=cgn, cvv=cvv: (
        s.wait_ge(cg.h, cgn), s.wait_ge(cvv[0].h, cvv[1])))

    ln1 = emit_ln(P, E, E["X"], E["XLN"], [(sd, x_mark)])
    rot_psT = Rot(2)
    xlt_done = emit_tm2fm(P, E, E["XLN"], E["XLT"], ln1, rot_psT)
    af_done = emit_attention(P, E, "a", wg, E["XLT"], E["AFOUT"],
                             (P.sem("ev"), xlt_done[1]) if False else xlt_done)
    res_done = emit_proj_residual(P, E, wot, E["AFOUT"], E["X"], af_done)
    ln2 = emit_ln(P, E, E["X"], E["XLN"], [res_done])
    pm = emit_amax_partial(P, E, E["XLN"], E["P11"], ln2)

    def stx(sy, pm=pm, res_done=res_done):
        _wait(sy, res_done)
        sy.dma_start(out=x1.rearrange("(t p) d -> p t d", p=128),
                     in_=E["X"][:, :].rearrange("p (t d) -> p t d", t=NT)
                     ).then_inc(st.h, 16)
        _wait(sy, pm)
        sy.dma_start(out=p1, in_=E["P11"][:]).then_inc(st.h, 16)
        sy.wait_ge(st.h, 32)
    P.on("sync", stx)

    if DBG.get("tap"):
        tap = DBG["tap"]
        dbg = nc.dram_tensor("dbg", [128, 16384], F32, kind="ExternalOutput").ap()
        src = {"xln": (E["XLN"], ln1), "xlt": (E["XLT"], xlt_done),
               "af": (E["AFOUT"], af_done)}[tap]
        dsem = P.sem("dsem")

        def dtap(g, src=src):
            _wait(g, src[1])
            g.dma_start(out=dbg, in_=src[0][:, :]).then_inc(dsem.h, 16)
            g.wait_ge(dsem.h, 16)
        P.on("gpsimd", dtap)
    P.attach()
    return nc



def _prep_host(inputs):
    f = np.ascontiguousarray
    out = {}
    out["ident"] = np.eye(128, dtype=np.float32)
    cm = np.ones((128, 128), dtype=np.float32)
    cm[np.triu_indices(128, k=1)] = 0.0
    cm = np.tril(np.ones((128, 128), dtype=np.float32), k=0)
    cm = np.ascontiguousarray(cm.T)
    cm = np.triu(np.ones((128, 128), dtype=np.float32), k=0)
    out["cmask"] = f(cm)

    Wqkv = inputs["L0_attn_a_Wqkv"]
    WT = Wqkv.T
    blocks = []
    for p in range(NPAIR):
        q = WT[:, 128 * p:128 * (p + 1)]
        k = WT[:, 1024 + 128 * p:1024 + 128 * (p + 1)]
        v0 = WT[:, 2048 + 128 * p:2048 + 128 * p + 64]
        v1 = WT[:, 2048 + 128 * p + 64:2048 + 128 * (p + 1)]
        z = np.zeros((D, 64), dtype=np.float32)
        blocks.append(np.concatenate([q, k, v0, z, v1, z], axis=1))
    out["wg_a"] = f(np.concatenate(blocks, axis=1))
    out["wot_a"] = f(inputs["L0_attn_a_Wout"].T)

    qw = inputs["L0_attn_b_qkv_w"]
    qb = inputs["L0_attn_b_qkv_b"]
    hm = inputs["L0_attn_b_head_mask"]
    QT = qw.T
    blocks = []
    hmg = np.zeros((NPAIR * 512,), dtype=np.float32)
    bg = np.zeros((NPAIR * 512,), dtype=np.float32)
    og = np.zeros((NPAIR * 512,), dtype=np.float32)
    for p in range(NPAIR):
        h0, h1 = 2 * p, 2 * p + 1
        cols = []
        meta = []
        for (h, c) in ((h0, 0), (h1, 0), (h0, 1), (h1, 1)):
            cols.append(QT[:, 192 * h + 64 * c:192 * h + 64 * (c + 1)])
            meta.append((h, c, 64))
        cols.append(QT[:, 192 * h0 + 128:192 * h0 + 192])
        meta.append((h0, 2, 64))
        cols.append(np.zeros((D, 64), dtype=np.float32))
        meta.append((None, None, 64))
        cols.append(QT[:, 192 * h1 + 128:192 * h1 + 192])
        meta.append((h1, 2, 64))
        cols.append(np.zeros((D, 64), dtype=np.float32))
        meta.append((None, None, 64))
        blk = np.concatenate(cols, axis=1)
        blocks.append(blk)
        off = 512 * p
        i = 0
        for (h, c, w) in meta:
            if h is not None:
                hmg[off + i:off + i + w] = hm[h]
                e0 = 192 * h + 64 * c
                bg[off + i:off + i + w] = qb[e0:e0 + w]
            else:
                og[off + i] = 1.0
            i += w
    out["wg_b"] = f(np.concatenate(blocks, axis=1))
    out["hmg"] = f(hmg.reshape(32, 128).T)
    out["bg"] = f(bg.reshape(32, 128).T)
    out["og"] = f(og.reshape(32, 128).T)
    out["wot_b"] = f(inputs["L0_attn_b_fc_w"].T)
    out["fcb"] = f(inputs["L0_attn_b_fc_b"].reshape(8, 128).T)

    out["w_in"] = f(inputs["L0_ffn_W_in"])
    out["w_int"] = f(inputs["L0_ffn_W_in"].T)
    out["w_out"] = f(inputs["L0_ffn_W_out"])
    out["b_in"] = f(inputs["L0_ffn_b_in"].reshape(32, 128).T)
    out["b_out"] = f(inputs["L0_ffn_b_out"].reshape(8, 128).T)
    return out



def emit_amax_combine(P, E, PIN, SAB, AMB, wait_sem_mark, qscale, scr=("AS1", "AS2")):
    mm, ev, vv, gd = P.sem("mm"), P.sem("ev"), P.sem("vv"), P.sem("gd")
    AS1, AS2 = E[scr[0]], E[scr[1]]
    psS = E["psS"]

    AIN = E["AIN"] if scr[0] == "AS1" else E["AINB"]

    def g(gp, PIN=PIN, AIN=AIN, pre=gd.n):
        gp.wait_ge(gd.h, pre)
        gp.dma_start(out=AIN[:], in_=PIN.rearrange("a b -> b a")).then_inc(gd.h, 16)
    P.on("gpsimd", g)
    gm = gd.inc(16)

    def dv(v, gm=gm):
        v.wait_ge(gd.h, gm)
        v.tensor_reduce(out=AS1[:], in_=AIN[:], axis=AX.X, op=A.max,
                        apply_absolute_value=True)
        v.drain()
        v.tensor_scalar(out=AS1[:], in0=AS1[:], scalar1=1e-8, scalar2=None, op0=A.max)
        v.drain()
        v.reciprocal(out=AS2[:], in_=AS1[:])
        v.drain()
        v.tensor_scalar(out=AS2[:], in0=AS2[:], scalar1=qscale, scalar2=None, op0=A.mult)
        v.drain().then_inc(vv.h, 1)
    P.on("vector", dv)
    m0 = vv.inc()

    prev = wait_sem_mark
    for (bsrc, dst) in ((AS2, SAB), (AS1, AMB)):
        def pe(t, bsrc=bsrc, m0=m0, prev=prev):
            t.wait_ge(vv.h, m0)
            _wait(t, prev)
            t.matmul(psS[0:128, 0:1], E["ONESF"][0:1, :], bsrc[0:1, 0:1],
                     start=True, stop=True).then_inc(mm.h, 1)
        P.on("tensor", pe)
        m1 = mm.inc()

        def sc(s, dst=dst, m1=m1):
            s.wait_ge(mm.h, m1)
            s.copy(out=dst[:], in_=psS[0:128, 0:1])
            s.drain().then_inc(ev.h, 1)
        P.on("scalar", sc)
        prev = (ev, ev.inc())
    return prev


def emit_quant_x(P, E, XLN, RQ, SAB, wait):
    vv = P.sem("vv")
    T1 = E["SQS"]

    def dv(v, XLN=XLN, RQ=RQ, SAB=SAB, wait=tuple(wait)):
        for w in wait:
            _wait(v, w)
        for tt in range(NT):
            v.tensor_scalar(out=T1[:], in0=XLN[:, 1024 * tt:1024 * (tt + 1)],
                            scalar1=SAB[:, 0:1], scalar2=C32, op0=A.mult, op1=A.add)
            v.drain()
            v.tensor_scalar(out=T1[:], in0=T1[:], scalar1=E["KC64"][:, 0:1],
                            scalar2=None, op0=A.add)
            v.drain()
            v.tensor_scalar(out=RQ[:, 1024 * tt:1024 * (tt + 1)], in0=T1[:],
                            scalar1=E["DD"][:, 0:1], scalar2=None, op0=A.add)
            v.drain()
        v.drain().then_inc(vv.h, 1)
    P.on("vector", dv)
    return (vv, vv.inc())


def emit_qin_prep(P, E, w_int, QIN, wait, rot_psT=None):
    mm, ev, vv, gd = P.sem("mm"), P.sem("ev"), P.sem("vv"), P.sem("gd")
    WIT, QTC, QB = E["WIT"], E["QTC"], E["QB"]
    AC32, SCW = E["AC32"], E["SCW"]
    psT = E["psT"]
    rot_wit = Rot(2)
    if rot_psT is None:
        rot_psT = Rot(2)
    rot_qb = Rot(2)
    rot_qtc = Rot(2)
    last = None
    for j in range(NJ):
        uW, ws, ww = rot_wit.acquire()
        uQT, qs_j, wqt = rot_qtc.acquire()

        def g(gp, j=j, ws=ws, ww=ww, w_int=w_int, wait=wait, pre=gd.n):
            if j == 0 and wait is not None:
                _wait(gp, wait)
            gp.wait_ge(gd.h, pre)
            _wait(gp, ww)
            gp.dma_start(out=WIT[:, 1024 * ws:1024 * (ws + 1)],
                         in_=w_int[128 * j:128 * (j + 1), :]).then_inc(gd.h, 16)
        P.on("gpsimd", g)
        gm = gd.inc(16)

        def dv(v, j=j, ws=ws, gm=gm, qs_j=qs_j, wqt=wqt):
            v.wait_ge(gd.h, gm)
            _wait(v, wqt)
            wsl = WIT[:, 1024 * ws:1024 * (ws + 1)]
            v.tensor_reduce(out=AC32[:, j:j + 1], in_=wsl, axis=AX.X, op=A.max,
                            apply_absolute_value=True)
            v.drain()
            v.tensor_scalar(out=SCW[:, 0:1], in0=AC32[:, j:j + 1], scalar1=1e-8,
                            scalar2=None, op0=A.max)
            v.drain()
            v.reciprocal(out=SCW[:, 1:2], in_=SCW[:, 0:1])
            v.drain()
            v.tensor_scalar(out=SCW[:, 1:2], in0=SCW[:, 1:2], scalar1=QW,
                            scalar2=None, op0=A.mult)
            v.drain()
            v.tensor_scalar(out=E["T1W"][:], in0=wsl, scalar1=SCW[:, 1:2],
                            scalar2=C32, op0=A.mult, op1=A.add)
            v.drain()
            v.tensor_scalar(out=E["T1W"][:], in0=E["T1W"][:],
                            scalar1=E["KC64"][:, 0:1], scalar2=None, op0=A.add)
            v.drain()
            v.tensor_scalar(out=QTC[:, 1024 * qs_j:1024 * (qs_j + 1)],
                            in0=E["T1W"][:], scalar1=E["DD"][:, 0:1],
                            scalar2=None, op0=A.add)
            v.drain().then_inc(vv.h, 1)
        P.on("vector", dv)
        m1 = vv.inc()
        rot_wit.release(uW, (vv, m1))

        for g4 in range(2):
            uT, ts_, wt = rot_psT.acquire()

            def pe(t, j=j, g4=g4, ts_=ts_, wt=wt, m1=m1, qs_j=qs_j):
                t.wait_ge(vv.h, m1)
                _wait(t, wt)
                for jj in range(4):
                    k = 4 * g4 + jj
                    ins = t.transpose(
                        psT[:, 1024 * ts_ + 128 * jj:1024 * ts_ + 128 * (jj + 1)],
                        QTC[:, 1024 * qs_j + 128 * k:1024 * qs_j + 128 * (k + 1)],
                        E["IDB"][:, :])
                ins.then_inc(mm.h, 1)
            P.on("tensor", pe)
            m2 = mm.inc()

            uQ, qs, wq = rot_qb.acquire()

            def sc(s, ts_=ts_, qs=qs, wq=wq, m2=m2):
                s.wait_ge(mm.h, m2)
                _wait(s, wq)
                s.copy(out=QB[:, 512 * qs:512 * (qs + 1)],
                       in_=psT[:, 1024 * ts_:1024 * ts_ + 512])
                s.drain().then_inc(ev.h, 1)
            P.on("scalar", sc)
            m3 = ev.inc()
            rot_psT.release(uT, (ev, m3))

            def g2(gp, j=j, g4=g4, qs=qs, m3=m3, QIN=QIN, pre=gd.n):
                gp.wait_ge(gd.h, pre)
                gp.wait_ge(ev.h, m3)
                gp.dma_start(
                    out=QIN[j, 4 * g4:4 * (g4 + 1)].rearrange("k r c -> r k c"),
                    in_=QB[:, 512 * qs:512 * (qs + 1)].rearrange("p (k c) -> p k c", k=4)
                ).then_inc(gd.h, 16)
            P.on("gpsimd", g2)
            m4 = gd.inc(16)
            rot_qb.release(uQ, (gd, m4))
            last = (gd, m4)
        rot_qtc.release(uQT, (mm, mm.n))
    return last


def emit_qout_prep(P, E, w_out, QOUT, wait):
    vv, gd = P.sem("vv"), P.sem("gd")
    WIT, QTC = E["WIT"], E["QTC"]
    AR32, SCW = E["AR32"], E["SCW"]
    rot_wit = Rot(2)
    rot_qtc = Rot(2)
    last = None
    for h in range(NJ):
        uW, ws, ww = rot_wit.acquire()

        def g(gp, h=h, ws=ws, ww=ww, w_out=w_out, wait=wait, pre=gd.n):
            if h == 0 and wait is not None:
                _wait(gp, wait)
            gp.wait_ge(gd.h, pre)
            _wait(gp, ww)
            gp.dma_start(out=WIT[:, 1024 * ws:1024 * (ws + 1)],
                         in_=w_out[128 * h:128 * (h + 1), :]).then_inc(gd.h, 16)
        P.on("gpsimd", g)
        gm = gd.inc(16)

        uQ, qs, wq = rot_qtc.acquire()

        def dv(v, h=h, ws=ws, qs=qs, wq=wq, gm=gm):
            v.wait_ge(gd.h, gm)
            _wait(v, wq)
            wsl = WIT[:, 1024 * ws:1024 * (ws + 1)]
            v.tensor_reduce(out=AR32[:, h:h + 1], in_=wsl, axis=AX.X, op=A.max,
                            apply_absolute_value=True)
            v.drain()
            v.tensor_scalar(out=SCW[:, 0:1], in0=AR32[:, h:h + 1], scalar1=1e-8,
                            scalar2=None, op0=A.max)
            v.drain()
            v.reciprocal(out=SCW[:, 1:2], in_=SCW[:, 0:1])
            v.drain()
            v.tensor_scalar(out=SCW[:, 1:2], in0=SCW[:, 1:2], scalar1=QW,
                            scalar2=None, op0=A.mult)
            v.drain()
            v.tensor_scalar(out=E["T1W"][:], in0=wsl, scalar1=SCW[:, 1:2],
                            scalar2=C32, op0=A.mult, op1=A.add)
            v.drain()
            v.tensor_scalar(out=E["T1W"][:], in0=E["T1W"][:],
                            scalar1=E["KC64"][:, 0:1], scalar2=None, op0=A.add)
            v.drain()
            v.tensor_scalar(out=QTC[:, 1024 * qs:1024 * (qs + 1)], in0=E["T1W"][:],
                            scalar1=E["DD"][:, 0:1], scalar2=None, op0=A.add)
            v.drain().then_inc(vv.h, 1)
        P.on("vector", dv)
        m1 = vv.inc()
        rot_wit.release(uW, (vv, m1))

        def g2(gp, h=h, qs=qs, m1=m1, QOUT=QOUT, pre=gd.n):
            gp.wait_ge(gd.h, pre)
            gp.wait_ge(vv.h, m1)
            gp.dma_start(out=QOUT[128 * h:128 * (h + 1), :],
                         in_=QTC[:, 1024 * qs:1024 * (qs + 1)]).then_inc(gd.h, 16)
        P.on("gpsimd", g2)
        m2 = gd.inc(16)
        rot_qtc.release(uQ, (gd, m2))
        last = (gd, m2)
    return last


def emit_cscale(P, E, SRC32, AMXB, DST32, wait):
    vv = P.sem("vv")

    def dv(v, SRC32=SRC32, AMXB=AMXB, DST32=DST32, wait=tuple(wait)):
        for w in wait:
            _wait(v, w)
        v.tensor_scalar(out=DST32[:], in0=SRC32[:], scalar1=AMXB[:, 0:1],
                        scalar2=1.0 / (QW * QA), op0=A.mult, op1=A.mult)
        v.drain().then_inc(vv.h, 1)
    P.on("vector", dv)
    return (vv, vv.inc())


def emit_mm1(P, E, QIN, RT, mode, tok_off, tok_w, HQ, waits):
    mm, ev, vv, gd = P.sem("mm"), P.sem("ev"), P.sem("vv"), P.sem("gd")
    psS = E["psS"]
    QW1, HIDC, T1H = E["QW1"], E["HIDC"], E["T1H"]
    CJ, BI = E["CJ"], E["BI"]
    rot_qw = Rot(2)
    rot_ps1 = Rot(2)
    rot_hid = Rot(2)
    nn = tok_w // 512
    last = None
    for j in range(NJ):
        uW, ws, ww = rot_qw.acquire()

        def g(gp, j=j, ws=ws, ww=ww, QIN=QIN, waits=tuple(waits), pre=gd.n):
            if j == 0:
                for w in waits:
                    _wait(gp, w)
            gp.wait_ge(gd.h, pre)
            _wait(gp, ww)
            gp.dma_start(out=QW1[:, 1024 * ws:1024 * (ws + 1)].rearrange(
                "p (k c) -> p k c", k=ND),
                in_=QIN[j].rearrange("k r c -> r k c")).then_inc(gd.h, 16)
        P.on("gpsimd", g)
        gm = gd.inc(16)

        for n in range(nn):
            u1, s1_, w1 = rot_ps1.acquire()

            def pe(t, j=j, n=n, ws=ws, s1_=s1_, w1=w1, gm=gm, RT=RT,
                   tok_off=tok_off, waits=tuple(waits)):
                t.wait_ge(gd.h, gm)
                if j == 0 and n == 0:
                    for w in waits:
                        _wait(t, w)
                _wait(t, w1)
                for k in range(ND):
                    ins = t.matmul(psS[:, 512 * s1_:512 * (s1_ + 1)],
                                   QW1[:, 1024 * ws + 128 * k:1024 * ws + 128 * (k + 1)],
                                   RT[:, 2048 * k + tok_off + 512 * n:
                                      2048 * k + tok_off + 512 * (n + 1)],
                                   start=(k == 0), stop=(k == ND - 1))
                ins.then_inc(mm.h, 1)
            P.on("tensor", pe)
            m1 = mm.inc()

            uH, hs, wh = rot_hid.acquire()

            def sc(s, j=j, s1_=s1_, hs=hs, wh=wh, m1=m1):
                s.wait_ge(mm.h, m1)
                _wait(s, wh)
                s.activation(out=HIDC[:, 512 * hs:512 * (hs + 1)],
                             in_=psS[:, 512 * s1_:512 * (s1_ + 1)],
                             func=AF.Gelu, scale=CJ[:, j:j + 1], bias=BI[:, j:j + 1])
                s.drain().then_inc(ev.h, 1)
            P.on("scalar", sc)
            m2 = ev.inc()
            rot_ps1.release(u1, (ev, m2))

            if mode == "probe":
                def dv(v, j=j, n=n, hs=hs, m2=m2):
                    v.wait_ge(ev.h, m2)
                    v.tensor_reduce(out=E["HMM"][:, 4 * j + n:4 * j + n + 1],
                                    in_=HIDC[:, 512 * hs:512 * (hs + 1)],
                                    axis=AX.X, op=A.max, apply_absolute_value=True)
                    v.drain().then_inc(vv.h, 1)
                P.on("vector", dv)
            else:
                def dv(v, j=j, n=n, hs=hs, m2=m2, HQ=HQ):
                    v.wait_ge(ev.h, m2)
                    hsl = HIDC[:, 512 * hs:512 * (hs + 1)]
                    v.tensor_scalar(out=T1H[:], in0=hsl, scalar1=E["SHB"][:, 0:1],
                                    scalar2=C32, op0=A.mult, op1=A.add)
                    v.drain()
                    v.tensor_scalar(out=T1H[:], in0=T1H[:], scalar1=E["KC64"][:, 0:1],
                                    scalar2=None, op0=A.add)
                    v.drain()
                    v.tensor_scalar(out=HQ[:, 1024 * j + 512 * n:1024 * j + 512 * (n + 1)],
                                    in0=T1H[:], scalar1=E["DD"][:, 0:1],
                                    scalar2=E["C2J"][:, j:j + 1], op0=A.add, op1=A.mult)
                    v.drain().then_inc(vv.h, 1)
                P.on("vector", dv)
            m3 = vv.inc()
            rot_hid.release(uH, (vv, m3))
            last = (vv, m3)
        rot_qw.release(uW, (mm, mm.n))
    return last


def emit_colmax_final(P, E, SRC, ncols, OUT11, wait):
    mm, ev, vv = P.sem("mm"), P.sem("ev"), P.sem("vv")

    def d1(v, SRC=SRC, ncols=ncols, wait=wait):
        _wait(v, wait)
        v.tensor_reduce(out=E["AMR"][:], in_=SRC[:, 0:ncols], axis=AX.X, op=A.max,
                        apply_absolute_value=True)
        v.drain().then_inc(vv.h, 1)
    P.on("vector", d1)
    m0 = vv.inc()

    def pe(t, m0=m0):
        t.wait_ge(vv.h, m0)
        t.transpose(E["psS"][0:1, 0:128], E["AMR"][:], E["IDF"][:, :]).then_inc(mm.h, 1)
    P.on("tensor", pe)
    m1 = mm.inc()

    def sc(s, m1=m1):
        s.wait_ge(mm.h, m1)
        s.copy(out=E["TR1"][:], in_=E["psS"][0:1, 0:128])
        s.drain().then_inc(ev.h, 1)
    P.on("scalar", sc)
    m2 = ev.inc()

    def d2(v, m2=m2, OUT11=OUT11):
        v.wait_ge(ev.h, m2)
        v.tensor_reduce(out=OUT11[0:1, 0:1], in_=E["TR1"][:], axis=AX.X, op=A.max,
                        apply_absolute_value=True)
        v.drain().then_inc(vv.h, 1)
    P.on("vector", d2)
    return (vv, vv.inc())


def _ffn_tiles(nc, E):
    E["AIN"] = sb(nc, "AIN", [1, NCORES], F32)
    E["AINB"] = E["SU"][0:1, 0:NCORES]
    E["AS1"] = sb(nc, "AS1", [1, 1], F32)
    E["AS2"] = sb(nc, "AS2", [1, 1], F32)
    E["AS1B"] = E["SQ"][0:1, 0:1]
    E["AS2B"] = E["SQ"][0:1, 1:2]
    E["SAB"] = sb(nc, "SAB", [128, 1], F32)
    E["AMB"] = sb(nc, "AMB", [128, 1], F32)
    E["WIT"] = sb(nc, "WIT", [128, 2048], F32)
    E["T1W"] = sb(nc, "T1W", [128, 1024], F32)
    E["QTC2"] = None
    E["AC32"] = sb(nc, "AC32", [128, NJ], F32)
    E["AR32"] = sb(nc, "AR32", [128, NJ], F32)
    E["SCW"] = sb(nc, "SCW", [128, 2], F32)
    E["CJ"] = sb(nc, "CJ", [128, NJ], F32)
    E["BI"] = sb(nc, "BI", [128, NJ], F32)
    E["QW1"] = sb(nc, "QW1", [128, 2048], BF16)


def build_probe():
    nc = bass.Bass("TRN2", target_bir_lowering=False)
    xin = nc.dram_tensor("xin", [T, D], F32, kind="ExternalInput").ap()
    pin = nc.dram_tensor("pin", [NCORES, 1], F32, kind="ExternalInput").ap()
    w_int = nc.dram_tensor("w_int", [FF, D], F32, kind="ExternalInput").ap()
    b_in = nc.dram_tensor("b_in", [128, NJ], F32, kind="ExternalInput").ap()
    ident = nc.dram_tensor("ident", [128, 128], F32, kind="ExternalInput").ap()
    pout = nc.dram_tensor("pout", [1, 1], F32, kind="ExternalOutput").ap()
    QIN = nc.dram_tensor("qin_scr", [NJ, ND, 128, 128], BF16, kind="Internal").ap()

    P = Prog(nc)
    E = {}
    _common_tiles(nc, E, need_attn=False)
    _ffn_tiles(nc, E)
    E["X"] = sb(nc, "X", [128, NT * 1024], F32)
    E["XLN"] = sb(nc, "XLN", [128, NT * 1024], BF16)
    E["RQ"] = sb(nc, "RQ", [128, NT * 1024], BF16)
    E["XLT"] = sb(nc, "XLT", [128, ND * 2048], BF16)
    E["HMM"] = sb(nc, "HMM", [128, 128], F32)
    E["QTC"] = sb(nc, "QTC", [128, 2048], BF16)
    E["QB"] = sb(nc, "QB", [128, 1024], BF16)
    E["HIDC"] = sb(nc, "HIDC", [128, 1024], BF16)
    E["T1H"] = E["T1W"][:, 0:512]
    E["P11"] = sb(nc, "P11", [1, 1], F32)

    sd, gd, st = P.sem("sd"), P.sem("gd"), P.sem("st")

    def ldx(sy, xin=xin):
        sy.dma_start(out=E["X"][:, :].rearrange("p (t d) -> p t d", t=NT),
                     in_=xin.rearrange("(t p) d -> p t d", p=128)).then_inc(sd.h, 16)
        sy.dma_start(out=E["BI"][:], in_=b_in).then_inc(sd.h, 16)
    P.on("sync", ldx)
    x_mark = sd.inc(32)

    (cg, cgn), cvv = _emit_const_loads(P, E, {"ident": ident})
    P.on("tensor", lambda t, cg=cg, cgn=cgn, cvv=cvv: (
        t.wait_ge(cg.h, cgn), t.wait_ge(cvv[0].h, cvv[1])))
    P.on("scalar", lambda s, xm=x_mark: s.wait_ge(sd.h, xm))
    kap = emit_kappa(P, E)

    amx = emit_amax_combine(P, E, pin, E["SAB"], E["AMB"], None, QA)
    ln = emit_ln(P, E, E["X"], E["XLN"], [(sd, x_mark)])
    qx = emit_quant_x(P, E, E["XLN"], E["RQ"], E["SAB"],
                      [ln, amx, (P.sems["vv"], kap[1])])
    rot_psT = Rot(2)
    rt = emit_tm2fm(P, E, E["RQ"], E["XLT"], qx, rot_psT)
    qp = emit_qin_prep(P, E, w_int, QIN, None, rot_psT=rot_psT)
    cj = emit_cscale(P, E, E["AC32"], E["AMB"], E["CJ"], [amx, (P.sems["gd"], qp[1])])
    h = emit_mm1(P, E, QIN, E["XLT"], "probe", 0, T, None,
                 [rt, cj, (sd, x_mark)])
    pm = emit_colmax_final(P, E, E["HMM"], 128, E["P11"], h)

    def stx(sy, pm=pm):
        _wait(sy, pm)
        sy.dma_start(out=pout, in_=E["P11"][:]).then_inc(st.h, 16)
        sy.wait_ge(st.h, 16)
    P.on("sync", stx)
    P.attach()
    return nc


def build_main(final):
    nc = bass.Bass("TRN2", target_bir_lowering=False)
    xin = nc.dram_tensor("xin", [T, D], F32, kind="ExternalInput").ap()
    pin = nc.dram_tensor("pin", [NCORES, 1], F32, kind="ExternalInput").ap()
    phin = nc.dram_tensor("phin", [NCORES, 1], F32, kind="ExternalInput").ap()
    w_int = nc.dram_tensor("w_int", [FF, D], F32, kind="ExternalInput").ap()
    w_out = nc.dram_tensor("w_out", [FF, D], F32, kind="ExternalInput").ap()
    b_in = nc.dram_tensor("b_in", [128, NJ], F32, kind="ExternalInput").ap()
    b_out = nc.dram_tensor("b_out", [128, ND], F32, kind="ExternalInput").ap()
    ident = nc.dram_tensor("ident", [128, 128], F32, kind="ExternalInput").ap()
    if not final:
        wg_b = nc.dram_tensor("wg_b", [D, NPAIR * 512], F32, kind="ExternalInput").ap()
        hmg = nc.dram_tensor("hmg", [128, 32], F32, kind="ExternalInput").ap()
        bgt = nc.dram_tensor("bgt", [128, 32], F32, kind="ExternalInput").ap()
        ogt = nc.dram_tensor("ogt", [128, 32], F32, kind="ExternalInput").ap()
        wot_b = nc.dram_tensor("wot_b", [D, D], F32, kind="ExternalInput").ap()
        fcb = nc.dram_tensor("fcb", [128, ND], F32, kind="ExternalInput").ap()
        pout = nc.dram_tensor("pout", [1, 1], F32, kind="ExternalOutput").ap()
    xout = nc.dram_tensor("xout", [T, D], F32, kind="ExternalOutput").ap()
    QIN = nc.dram_tensor("qin_scr", [NJ, ND, 128, 128], BF16, kind="Internal").ap()
    QOUT = nc.dram_tensor("qout_scr", [FF, D], BF16, kind="Internal").ap()
    MROW = [nc.dram_tensor(f"mrow{i}", [16, 128], F32, kind="Internal").ap()
            for i in range(2)]

    P = Prog(nc)
    E = {}
    _common_tiles(nc, E, need_attn=False, skip_tr1=True)
    _ffn_tiles(nc, E)
    E["X"] = sb(nc, "X", [128, NT * 1024], F32)
    BIGB = sb(nc, "BIGB", [128, 2 * NT * 1024], BF16)
    E["XLN"] = BIGB[:, 0:16384]
    E["RQ"] = BIGB[:, 16384:32768]
    E["HQ"] = BIGB
    E["AFOUT"] = BIGB[:, 0:16384]
    if not final:
        E["QH0"] = BIGB[0:65, 16384 + 0 * 2048:16384 + 1 * 2048]
        E["QH1"] = BIGB[0:65, 16384 + 1 * 2048:16384 + 2 * 2048]
        E["KH0"] = BIGB[0:65, 16384 + 2 * 2048:16384 + 3 * 2048]
        E["KH1"] = BIGB[0:65, 16384 + 3 * 2048:16384 + 4 * 2048]
        E["V65"] = BIGB[0:65, 16384 + 4 * 2048:16384 + 6 * 2048]
        E["WPR"] = sb(nc, "WPR", [128, 8192], BF16)
        E["ZF"] = E["WIT"][32:33, 0:2048]
        E["ZR"] = E["WIT"][0:1, 0:2048]
        E["OF"] = E["WIT"][64:128, 0:2048]
        E["M16"] = sb(nc, "M16", [128, 16], F32)
        E["MC4"] = sb(nc, "MC4", [128, 4], F32)
        E["MT"] = sb(nc, "MT", [16, 128], F32)
        E["TR1"] = E["MT"][0:1, 0:128]
        E["HMG"] = sb(nc, "HMG", [128, 32], F32)
        E["BG"] = sb(nc, "BG", [128, 32], F32)
        E["OG"] = sb(nc, "OG", [128, 32], F32)
        E["HB"] = sb(nc, "HB", [128, 32], F32)
        E["FCB"] = sb(nc, "FCB", [128, ND], F32)
    E["XLT"] = sb(nc, "XLT", [128, ND * 2048], BF16)
    SHRB = sb(nc, "SHRB", [128, 4160], BF16)
    E["QTC"] = SHRB[:, 0:2048]
    E["QB"] = SHRB[:, 2048:3072]
    if not final:
        E["VT"] = SHRB
        E["PT"] = E["QW1"][:, 0:2048]
    E["HIDC"] = SHRB[:, 3072:4096]
    E["T1H"] = E["T1W"][:, 0:512]
    E["SHB"] = sb(nc, "SHB", [128, 1], F32)
    E["AMHB"] = sb(nc, "AMHB", [128, 1], F32)
    E["C2J"] = E["AC32"]
    E["BO"] = sb(nc, "BO", [128, ND], F32)
    E["QW2"] = E["WPR"][:, 0:8192] if not final else sb(nc, "QW2", [128, 8192], BF16)
    E["PRF"] = E["SQS"]
    E["P11"] = E["AS2"]

    sd, gd, st = P.sem("sd"), P.sem("gd"), P.sem("st")

    def ldx(sy, xin=xin):
        sy.dma_start(out=E["X"][:, :].rearrange("p (t d) -> p t d", t=NT),
                     in_=xin.rearrange("(t p) d -> p t d", p=128)).then_inc(sd.h, 16)
        sy.dma_start(out=E["BI"][:], in_=b_in).then_inc(sd.h, 16)
        sy.dma_start(out=E["BO"][:], in_=b_out).then_inc(sd.h, 16)
        if not final:
            sy.dma_start(out=E["HMG"][:], in_=hmg).then_inc(sd.h, 16)
            sy.dma_start(out=E["BG"][:], in_=bgt).then_inc(sd.h, 16)
            sy.dma_start(out=E["OG"][:], in_=ogt).then_inc(sd.h, 16)
            sy.dma_start(out=E["FCB"][:], in_=fcb).then_inc(sd.h, 16)
    P.on("sync", ldx)
    x_mark = sd.inc(16 * (7 if not final else 3))

    (cg, cgn), cvv = _emit_const_loads(P, E, {"ident": ident}, attn_b=True)
    P.on("tensor", lambda t, cg=cg, cgn=cgn, cvv=cvv: (
        t.wait_ge(cg.h, cgn), t.wait_ge(cvv[0].h, cvv[1])))
    P.on("scalar", lambda s, xm=x_mark: s.wait_ge(sd.h, xm))
    kap = emit_kappa(P, E)
    vvs = P.sems["vv"]

    if not final:
        def hbv(v, x_mark=x_mark):
            v.wait_ge(sd.h, x_mark)
            v.tensor_tensor(out=E["HB"][:], in0=E["HMG"][:], in1=E["BG"][:], op=A.mult)
            v.drain()
            v.tensor_tensor(out=E["HB"][:], in0=E["HB"][:], in1=E["OG"][:], op=A.add)
            v.drain().then_inc(vvs.h, 1)
        P.on("vector", hbv)
        hb_mark = vvs.inc()
        P.on("tensor", lambda t, n=hb_mark: t.wait_ge(vvs.h, n))
        P.on("scalar", lambda s, n=hb_mark: s.wait_ge(vvs.h, n))

    amx = emit_amax_combine(P, E, pin, E["SAB"], E["AMB"], None, QA)
    amh = emit_amax_combine(P, E, phin, E["SHB"], E["AMHB"], amx, QA, scr=("AS1B", "AS2B"))
    ln1 = emit_ln(P, E, E["X"], E["XLN"], [(sd, x_mark)])
    qx = emit_quant_x(P, E, E["XLN"], E["RQ"], E["SAB"],
                      [ln1, amx, (vvs, kap[1])])
    rot_psT = Rot(2)
    rt = emit_tm2fm(P, E, E["RQ"], E["XLT"], qx, rot_psT)
    qp = emit_qin_prep(P, E, w_int, QIN, None, rot_psT=rot_psT)
    qo = emit_qout_prep(P, E, w_out, QOUT, qp)
    cj = emit_cscale(P, E, E["AC32"], E["AMB"], E["CJ"], [amx, (P.sems["gd"], qp[1])])
    c2 = emit_cscale(P, E, E["AR32"], E["AMHB"], E["C2J"], [amh, (P.sems["gd"], qo[1])])

    E["_tw"] = 1024
    res_marks = []
    for half in range(2):
        w1 = [rt, cj, c2] + ([res_marks[0]] if half == 1 else [])
        hq = emit_mm1(P, E, QIN, E["XLT"], "full", 1024 * half, 1024, E["HQ"], w1)
        rm = emit_mm2_residual(P, E, QOUT, E["HQ"], E["X"], E["BO"],
                               half, [hq, qo])
        res_marks.append(rm)
    ffn_done = res_marks[-1]

    if DBG.get("tap3"):
        dbg = nc.dram_tensor("dbg", [128, 128], F32, kind="ExternalOutput").ap()
        dsem = P.sem("dsem")

        def dtap(g, ffn_done=ffn_done):
            _wait(g, ffn_done)
            ctx = P.nc.allow_non_contiguous_dma(reason="debug tap")
            ctx.__enter__()
            g.dma_start(out=dbg[:, 0:1], in_=E["SAB"][:]).then_inc(dsem.h, 16)
            g.dma_start(out=dbg[:, 1:2], in_=E["SHB"][:]).then_inc(dsem.h, 16)
            g.dma_start(out=dbg[:, 2:3], in_=E["AMB"][:]).then_inc(dsem.h, 16)
            g.dma_start(out=dbg[:, 3:4], in_=E["AMHB"][:]).then_inc(dsem.h, 16)
            g.dma_start(out=dbg[:, 4:36], in_=E["CJ"][:]).then_inc(dsem.h, 16)
            g.dma_start(out=dbg[:, 36:68], in_=E["C2J"][:]).then_inc(dsem.h, 16)
            g.dma_start(out=dbg[:, 68:100], in_=E["AR32"][:]).then_inc(dsem.h, 16)
            g.wait_ge(dsem.h, 112)
        P.on("gpsimd", dtap)

    if final:
        def stx(sy, ffn_done=ffn_done):
            _wait(sy, ffn_done)
            sy.dma_start(out=xout.rearrange("(t p) d -> p t d", p=128),
                         in_=E["X"][:, :].rearrange("p (t d) -> p t d", t=NT)
                         ).then_inc(st.h, 16)
            sy.wait_ge(st.h, 16)
        P.on("sync", stx)
        P.attach()
        return nc

    ln2 = emit_ln(P, E, E["X"], E["XLN"], [ffn_done])
    rot_psT2 = Rot(2)
    xlt2 = emit_tm2fm(P, E, E["XLN"], E["XLT"], ln2, rot_psT2)
    af = emit_attention(P, E, "b", wg_b, E["XLT"], E["AFOUT"], xlt2,
                        extras={"HMG": E["HMG"], "HB": E["HB"], "MROW": MROW})
    res = emit_proj_residual(P, E, wot_b, E["AFOUT"], E["X"], af,
                             WOt=E["WPR"], BIAS=E["FCB"])
    ln3 = emit_ln(P, E, E["X"], E["XLN"], [res])
    pm = emit_amax_partial(P, E, E["XLN"], E["P11"], ln3)

    def stx(sy, pm=pm, res=res):
        _wait(sy, res)
        sy.dma_start(out=xout.rearrange("(t p) d -> p t d", p=128),
                     in_=E["X"][:, :].rearrange("p (t d) -> p t d", t=NT)
                     ).then_inc(st.h, 16)
        _wait(sy, pm)
        sy.dma_start(out=pout, in_=E["P11"][:]).then_inc(st.h, 16)
        sy.wait_ge(st.h, 32)
    P.on("sync", stx)
    P.attach()
    return nc


def emit_mm2_residual(P, E, QOUT, HQ, X, BO, half, waits):
    mm, ev, vv, gd = P.sem("mm"), P.sem("ev"), P.sem("vv"), P.sem("gd")
    psS, psO = E["psS"], E["psO"]
    QW2, PRF = E["QW2"], E["PRF"]
    rot_qw = Rot(2)
    rot_ps1 = Rot(2)
    rot_prf = Rot(2)
    rot_ps2 = Rot(4)
    last = None
    for m in range(ND):
        uW, ws, ww = rot_qw.acquire()

        def g(gp, m=m, ws=ws, ww=ww, QOUT=QOUT, waits=tuple(waits), pre=gd.n):
            if m == 0:
                for w in waits:
                    _wait(gp, w)
            gp.wait_ge(gd.h, pre)
            _wait(gp, ww)
            for k in range(NJ):
                gp.dma_start(out=QW2[:, 4096 * ws + 128 * k:4096 * ws + 128 * (k + 1)],
                             in_=QOUT[128 * k:128 * (k + 1), 128 * m:128 * (m + 1)]
                             ).then_inc(gd.h, 16)
        P.on("gpsimd", g)
        gm = gd.inc(16 * NJ)

        for n in range(2):
            u1, s1_, w1 = rot_ps1.acquire()

            def pe(t, m=m, n=n, ws=ws, s1_=s1_, w1=w1, gm=gm, HQ=HQ,
                   waits=tuple(waits)):
                t.wait_ge(gd.h, gm)
                if m == 0 and n == 0:
                    for w in waits:
                        _wait(t, w)
                _wait(t, w1)
                for k in range(NJ):
                    ins = t.matmul(psS[:, 512 * s1_:512 * (s1_ + 1)],
                                   QW2[:, 4096 * ws + 128 * k:4096 * ws + 128 * (k + 1)],
                                   HQ[:, 1024 * k + 512 * n:1024 * k + 512 * (n + 1)],
                                   start=(k == 0), stop=(k == NJ - 1))
                ins.then_inc(mm.h, 1)
            P.on("tensor", pe)
            m1 = mm.inc()

            uP, sp, wp = rot_prf.acquire()

            def sc(s, m=m, s1_=s1_, sp=sp, wp=wp, m1=m1):
                s.wait_ge(mm.h, m1)
                _wait(s, wp)
                s.activation(out=PRF[:, 512 * sp:512 * (sp + 1)],
                             in_=psS[:, 512 * s1_:512 * (s1_ + 1)],
                             func=AF.Identity, bias=BO[:, m:m + 1], scale=1.0)
                s.drain().then_inc(ev.h, 1)
            P.on("scalar", sc)
            m2 = ev.inc()
            rot_ps1.release(u1, (ev, m2))

            u2, s2_, w2 = rot_ps2.acquire()

            def pe2(t, sp=sp, s2_=s2_, w2=w2, m2=m2):
                t.wait_ge(ev.h, m2)
                _wait(t, w2)
                for j in range(4):
                    ins = t.transpose(psO[:, 512 * s2_ + 128 * j:512 * s2_ + 128 * (j + 1)],
                                      PRF[:, 512 * sp + 128 * j:512 * sp + 128 * (j + 1)],
                                      E["IDF"][:, :])
                ins.then_inc(mm.h, 1)
            P.on("tensor", pe2)
            m3 = mm.inc()

            def dv(v, m=m, n=n, half=half, s2_=s2_, m3=m3, X=X):
                v.wait_ge(mm.h, m3)
                for j in range(4):
                    tt = 8 * half + 4 * n + j
                    xsl = X[:, 1024 * tt + 128 * m:1024 * tt + 128 * (m + 1)]
                    v.tensor_tensor(out=xsl,
                                    in0=psO[:, 512 * s2_ + 128 * j:512 * s2_ + 128 * (j + 1)],
                                    in1=xsl, op=A.add)
                v.drain().then_inc(vv.h, 1)
            P.on("vector", dv)
            m4 = vv.inc()
            rot_prf.release(uP, (mm, m3))
            rot_ps2.release(u2, (vv, m4))
            last = (vv, m4)
        rot_qw.release(uW, (mm, mm.n))
    return last


def kernel(**inputs):
    host = _prep_host(inputs)
    x_full = inputs["L0_attn_a_x"].astype(np.float32, copy=False)
    cores = list(range(NCORES))

    nc1 = build_L1()
    maps = [{"x": np.ascontiguousarray(x_full[c]), "wg_a": host["wg_a"],
             "wot_a": host["wot_a"], "ident": host["ident"],
             "cmask": host["cmask"]} for c in cores]
    r1 = run_bass_kernel_spmd(nc1, maps, core_ids=cores)
    x1 = [np.ascontiguousarray(np.asarray(r["x1"], dtype=np.float32)) for r in r1.results]
    P1 = np.ascontiguousarray(np.stack(
        [np.asarray(r["p1"], dtype=np.float32).reshape(()) for r in r1.results]
    ).reshape(NCORES, 1))

    ncp = build_probe()
    maps = [{"xin": x1[c], "pin": P1, "w_int": host["w_int"],
             "b_in": host["b_in"], "ident": host["ident"]} for c in cores]
    r2 = run_bass_kernel_spmd(ncp, maps, core_ids=cores)
    P2 = np.ascontiguousarray(np.stack(
        [np.asarray(r["pout"], dtype=np.float32).reshape(()) for r in r2.results]
    ).reshape(NCORES, 1))

    nc3 = build_main(final=False)
    maps = [{"xin": x1[c], "pin": P1, "phin": P2, "w_int": host["w_int"],
             "w_out": host["w_out"], "b_in": host["b_in"], "b_out": host["b_out"],
             "ident": host["ident"], "wg_b": host["wg_b"], "hmg": host["hmg"],
             "bgt": host["bg"], "ogt": host["og"], "wot_b": host["wot_b"],
             "fcb": host["fcb"]} for c in cores]
    r3 = run_bass_kernel_spmd(nc3, maps, core_ids=cores)
    x3 = [np.ascontiguousarray(np.asarray(r["xout"], dtype=np.float32)) for r in r3.results]
    P3 = np.ascontiguousarray(np.stack(
        [np.asarray(r["pout"], dtype=np.float32).reshape(()) for r in r3.results]
    ).reshape(NCORES, 1))

    maps = [{"xin": x3[c], "pin": P3, "w_int": host["w_int"],
             "b_in": host["b_in"], "ident": host["ident"]} for c in cores]
    r4 = run_bass_kernel_spmd(ncp, maps, core_ids=cores)
    P4 = np.ascontiguousarray(np.stack(
        [np.asarray(r["pout"], dtype=np.float32).reshape(()) for r in r4.results]
    ).reshape(NCORES, 1))

    nc5 = build_main(final=True)
    maps = [{"xin": x3[c], "pin": P3, "phin": P4, "w_int": host["w_int"],
             "w_out": host["w_out"], "b_in": host["b_in"], "b_out": host["b_out"],
             "ident": host["ident"]} for c in cores]
    r5 = run_bass_kernel_spmd(nc5, maps, core_ids=cores)
    out = np.stack([np.asarray(r["xout"], dtype=np.float32) for r in r5.results])
    return out


# version 07
# speedup vs baseline: 1.1866x; 1.1866x over previous
import numpy as np
import concourse.bass as bass
import concourse.mybir as mybir
from concourse.alu_op_type import AluOpType as A
from concourse.bass_utils import run_bass_kernel_spmd

F32 = mybir.dt.float32
BF16 = mybir.dt.bfloat16
AF = mybir.ActivationFunctionType
AX = mybir.AxisListType

NCORES = 8
T = 2048
D = 1024
NT = T // 128
ND = D // 128
H = 16
DK = 64
NPAIR = 8
FF = 4096
NJ = FF // 128

C32 = 1.5 * 2.0 ** 23
C64 = 1.5 * 2.0 ** 52
E_NEG10 = float(np.float32(np.exp(np.float64(-10.0))))
E_POS10 = float(np.float32(np.exp(np.float64(10.0))))
LN_EPS = 1e-5
INV_D = 1.0 / D
QA = 32767.0
QW = 127.0


class Sem:
    def __init__(self, handle):
        self.h = handle
        self.n = 0

    def inc(self, k=1):
        self.n += k
        return self.n


class Rot:
    def __init__(self, nslots):
        self.k = nslots
        self.marks = []

    def acquire(self):
        u = len(self.marks)
        wait = self.marks[u - self.k] if u >= self.k else None
        self.marks.append(None)
        return u, u % self.k, wait

    def release(self, u, mark):
        self.marks[u] = mark


def _wait(eng, w):
    if w is not None:
        eng.wait_ge(w[0].h, w[1])


class Prog:
    ENGINES = ("sync", "tensor", "vector", "scalar", "gpsimd")

    def __init__(self, nc):
        self.nc = nc
        self.ops = {e: [] for e in self.ENGINES}
        self.sems = {}

    def sem(self, name):
        if name not in self.sems:
            h = self.nc.ctx.enter_context(self.nc.semaphore(name))
            self.sems[name] = Sem(h)
        return self.sems[name]

    def on(self, eng, fn):
        self.ops[eng].append(fn)

    def attach(self):
        cm = self.nc.Block()
        block = cm.__enter__()

        @block.sync
        def _(eng):
            for fn in self.ops["sync"]:
                fn(eng)

        @block.tensor
        def _(eng):
            for fn in self.ops["tensor"]:
                fn(eng)

        @block.vector
        def _(eng):
            for fn in self.ops["vector"]:
                fn(eng)

        @block.scalar
        def _(eng):
            for fn in self.ops["scalar"]:
                fn(eng)

        @block.gpsimd
        def _(eng):
            for fn in self.ops["gpsimd"]:
                fn(eng)

        cm.__exit__(None, None, None)


def sb(nc, name, shape, dt):
    return nc.ctx.enter_context(nc.sbuf_tensor(name, list(shape), dt))


def psum(nc, name, shape, dt):
    return nc.ctx.enter_context(nc.psum_tensor(name, list(shape), dt))



def emit_kappa(P, E):
    vv = P.sem("vv")

    def go(v, KAP=E["KAP"], KC64=E["KC64"], DD=E["DD"]):
        v.memset(KAP[:], 1.0)
        v.drain()
        v.tensor_scalar(out=KAP[:], in0=KAP[:], scalar1=C64, scalar2=None, op0=A.add)
        v.drain()
        v.tensor_scalar(out=KAP[:], in0=KAP[:], scalar1=-C64, scalar2=None, op0=A.add)
        v.drain()
        v.tensor_scalar(out=KC64[:], in0=KAP[:], scalar1=C64, scalar2=None, op0=A.mult)
        v.drain()
        v.tensor_scalar(out=DD[:], in0=KC64[:], scalar1=C32, scalar2=-1.0,
                        op0=A.add, op1=A.mult)
        v.drain().then_inc(vv.h, 1)
    P.on("vector", go)
    return ("vv", vv.inc())


def emit_ln(P, E, X, XLN, waits):
    vv, ev = P.sem("vv"), P.sem("ev")
    SU, SQ, M, V, R, MSQ, SQS = (E[k] for k in ("SU", "SQ", "M", "V", "R", "MSQ", "SQS"))

    ev0 = P.sem("ev")

    def sq1(s, X=X, waits=tuple(waits)):
        for w in waits:
            _wait(s, w)
        for tt in range(NT):
            s.activation(out=SQS[:], in_=X[:, 1024 * tt:1024 * (tt + 1)],
                         func=AF.Square, accum_out=SQ[:, tt:tt + 1])
            s.drain()
        s.drain().then_inc(ev0.h, 1)
    P.on("scalar", sq1)
    m_sq = ev0.inc()

    def d1(v, X=X, waits=tuple(waits), m_sq=m_sq):
        for w in waits:
            _wait(v, w)
        for tt in range(NT):
            xs = X[:, 1024 * tt:1024 * (tt + 1)]
            v.reduce_sum(out=SU[:, tt:tt + 1], in_=xs, axis=AX.X)
        v.wait_ge(ev0.h, m_sq)
        v.drain()
        v.tensor_scalar(out=M[:], in0=SU[:], scalar1=INV_D, scalar2=None, op0=A.mult)
        v.tensor_scalar(out=V[:], in0=SQ[:], scalar1=INV_D, scalar2=None, op0=A.mult)
        v.drain()
        v.tensor_tensor(out=MSQ[:], in0=M[:], in1=M[:], op=A.mult)
        v.drain()
        v.scalar_tensor_tensor(out=V[:], in0=V[:], scalar=LN_EPS, in1=MSQ[:],
                               op0=A.add, op1=A.subtract)
        v.drain().then_inc(vv.h, 1)
    P.on("vector", d1)
    m_var = vv.inc()

    def s1(s, m_var=m_var):
        s.wait_ge(vv.h, m_var)
        s.activation(out=R[:], in_=V[:], func=AF.Sqrt)
        s.drain().then_inc(ev.h, 1)
    P.on("scalar", s1)
    m_sd = ev.inc()

    def d2(v, X=X, XLN=XLN, m_sd=m_sd):
        v.wait_ge(ev.h, m_sd)
        v.reciprocal(out=R[:], in_=R[:])
        v.drain()
        for tt in range(NT):
            v.tensor_scalar(out=XLN[:, 1024 * tt:1024 * (tt + 1)],
                            in0=X[:, 1024 * tt:1024 * (tt + 1)],
                            scalar1=M[:, tt:tt + 1], scalar2=R[:, tt:tt + 1],
                            op0=A.subtract, op1=A.mult)
        v.drain().then_inc(vv.h, 1)
    P.on("vector", d2)
    return (vv, vv.inc())


def emit_tm2fm(P, E, SRC, DST, wait, rot_psT, n_src_cols=1024, bf=True):
    mm, ev = P.sem("mm"), P.sem("ev")
    psT = E["psT"]
    nch = n_src_cols // 128
    last = None
    first = True
    for dc in range(nch):
        for g4 in range(NT // 4):
            u, slot, w = rot_psT.acquire()

            def pe(t, dc=dc, g4=g4, slot=slot, w=w, SRC=SRC, first=first,
                   wait=wait, ncols=n_src_cols):
                if first and wait is not None:
                    _wait(t, wait)
                _wait(t, w)
                for j in range(4):
                    tt = 4 * g4 + j
                    ins = t.transpose(psT[:, 1024 * slot + 128 * j:1024 * slot + 128 * (j + 1)],
                                      SRC[:, ncols * tt + 128 * dc:ncols * tt + 128 * (dc + 1)],
                                      E["IDB"][:, :])
                ins.then_inc(mm.h, 1)
            P.on("tensor", pe)
            first = False
            m1 = mm.inc()

            def sc(s, dc=dc, g4=g4, slot=slot, m1=m1, DST=DST):
                s.wait_ge(mm.h, m1)
                s.copy(out=DST[:, 2048 * dc + 512 * g4:2048 * dc + 512 * (g4 + 1)],
                       in_=psT[:, 1024 * slot:1024 * slot + 512])
                s.drain().then_inc(ev.h, 1)
            P.on("scalar", sc)
            m2 = ev.inc()
            rot_psT.release(u, (ev, m2))
            last = (ev, m2)
    return last


def emit_amax_partial(P, E, XLN, OUT11, wait):
    mm, ev, vv = P.sem("mm"), P.sem("ev"), P.sem("vv")
    AM, AMR, TR1 = E["AM16"], E["AMR"], E["TR1"]

    def d1(v, XLN=XLN, wait=wait):
        _wait(v, wait)
        for tt in range(NT):
            v.tensor_reduce(out=AM[:, tt:tt + 1], in_=XLN[:, 1024 * tt:1024 * (tt + 1)],
                            axis=AX.X, op=A.max, apply_absolute_value=True)
        v.drain()
        v.tensor_reduce(out=AMR[:], in_=AM[:], axis=AX.X, op=A.max,
                        apply_absolute_value=True)
        v.drain().then_inc(vv.h, 1)
    P.on("vector", d1)
    m0 = vv.inc()

    def pe(t, m0=m0):
        t.wait_ge(vv.h, m0)
        t.transpose(E["psS"][0:1, 0:128], AMR[:], E["IDF"][:, :]).then_inc(mm.h, 1)
    P.on("tensor", pe)
    m1 = mm.inc()

    def sc(s, m1=m1):
        s.wait_ge(mm.h, m1)
        s.copy(out=TR1[:], in_=E["psS"][0:1, 0:128])
        s.drain().then_inc(ev.h, 1)
    P.on("scalar", sc)
    m2 = ev.inc()

    def d2(v, m2=m2, OUT11=OUT11):
        v.wait_ge(ev.h, m2)
        v.tensor_reduce(out=OUT11[0:1, 0:1], in_=TR1[:], axis=AX.X, op=A.max,
                        apply_absolute_value=True)
        v.drain().then_inc(vv.h, 1)
    P.on("vector", d2)
    return (vv, vv.inc())


def _chunks_for_tkb(tkb):
    out = []
    c = 128 * tkb
    while c < T:
        e = min(T, (c // 512 + 1) * 512)
        out.append((c, e))
        c = e
    return out


def emit_attention(P, E, kind, Wg, XLT, AFOUT, xlt_wait, extras=None):
    nc = P.nc
    mm, ev, vv, gd, gw, sd = (P.sem("mm"), P.sem("ev"), P.sem("vv"),
                              P.sem("gd"), P.sem("gw"), P.sem("sd"))
    psS, psO, psT = E["psS"], E["psO"], E["psT"]
    WPR, V65, VT, PT = E["WPR"], E["V65"], E["VT"], E["PT"]
    ZF, ZR, OF, ONESF = E["ZF"], E["ZR"], E["OF"], E["ONESF"]
    rot_ps1 = Rot(2)
    rot_wpr = Rot(2)
    rot_psT = Rot(2)
    rot_mrow = Rot(2)
    is_b = (kind == "b")
    if is_b:
        M16, MC4, MT = E["M16"], E["MC4"], E["MT"]
        HMG, HB = extras["HMG"], extras["HB"]
        MROW = extras["MROW"]
        QP = KP = None
    else:
        QP, KP = E["QP"], E["KP"]
        CM = E["CM"]

    pair_done = {}
    last_of_mark = None
    kh_ready = None
    if is_b:
        def khm(v, xlt_wait=xlt_wait):
            if xlt_wait is not None:
                _wait(v, xlt_wait)
            v.memset(E["KH0"][64:65, :], 1.0)
            v.memset(E["KH1"][64:65, :], 1.0)
            v.drain().then_inc(vv.h, 1)
        P.on("vector", khm)
        kh_ready = (vv, vv.inc())

    for p in range(NPAIR):
        uW, wslot, wwait = rot_wpr.acquire()
        wbase = 4096 * wslot

        def ld(g, p=p, wbase=wbase, wwait=wwait, Wg=Wg, pre=gd.n,
               xlt_wait=xlt_wait):
            g.wait_ge(gd.h, pre)
            if p < 2 and xlt_wait is not None:
                _wait(g, xlt_wait)
            _wait(g, wwait)
            for k in range(ND):
                g.dma_start(out=WPR[:, wbase + 512 * k:wbase + 512 * (k + 1)],
                            in_=Wg[128 * k:128 * (k + 1), 512 * p:512 * (p + 1)]
                            ).then_inc(gd.h, 16)
        P.on("gpsimd", ld)
        gmark = gd.inc(16 * ND)

        secs = [("q", 0, 128), ("k", 128, 128), ("v0", 256, 65), ("v1", 384, 65)]
        for sec, off, mw in secs:
            for n in range(4):
                u1, s1_, w1 = rot_ps1.acquire()
                pdst = psS[:, 512 * s1_:512 * (s1_ + 1)] if mw == 128 else \
                    psS[0:65, 512 * s1_:512 * s1_ + 512]

                def pe(t, p=p, off=off, mw=mw, n=n, s1_=s1_, w1=w1, gmark=gmark,
                       wbase=wbase, XLT=XLT, xlt_wait=xlt_wait, pdst=pdst):
                    t.wait_ge(gd.h, gmark)
                    if xlt_wait is not None:
                        _wait(t, xlt_wait)
                    _wait(t, w1)
                    for k in range(ND):
                        ins = t.matmul(pdst,
                                       WPR[:, wbase + 512 * k + off:wbase + 512 * k + off + mw],
                                       XLT[:, 2048 * k + 512 * n:2048 * k + 512 * (n + 1)],
                                       start=(k == 0), stop=(k == ND - 1))
                    ins.then_inc(mm.h, 1)
                P.on("tensor", pe)
                m1 = mm.inc()

                def sc(s, p=p, sec=sec, n=n, s1_=s1_, m1=m1):
                    s.wait_ge(mm.h, m1)
                    src128 = psS[:, 512 * s1_:512 * (s1_ + 1)]
                    src65 = psS[0:65, 512 * s1_:512 * s1_ + 512]
                    cols = slice(512 * n, 512 * (n + 1))
                    if not is_b:
                        if sec == "q":
                            s.copy(out=QP[:, cols], in_=src128)
                        elif sec == "k":
                            s.copy(out=KP[:, cols], in_=src128)
                        else:
                            hh = 0 if sec == "v0" else 1
                            s.activation(out=V65[:, 2048 * hh + 512 * n:2048 * hh + 512 * (n + 1)],
                                         in_=src65, func=AF.Identity,
                                         bias=E["VB"][:, :], scale=1.0)
                    else:
                        ci = 4 * p
                        if sec in ("q", "k"):
                            dsts = (E["QH0"], E["QH1"]) if sec == "q" else (E["KH0"], E["KH1"])
                            co = ci + (0 if sec == "q" else 1)
                            for hh in (0, 1):
                                s.activation(
                                    out=dsts[hh][0:64, cols],
                                    in_=psS[64 * hh:64 * hh + 64, 512 * s1_:512 * s1_ + 512],
                                    func=AF.Identity,
                                    scale=HMG[64 * hh:64 * hh + 64, co:co + 1],
                                    bias=HB[64 * hh:64 * hh + 64, co:co + 1])
                        else:
                            hh = 0 if sec == "v0" else 1
                            co = ci + 2 + hh
                            s.activation(out=V65[:, 2048 * hh + 512 * n:2048 * hh + 512 * (n + 1)],
                                         in_=src65, func=AF.Identity,
                                         scale=HMG[0:65, co:co + 1],
                                         bias=HB[0:65, co:co + 1])
                    s.drain().then_inc(ev.h, 1)
                P.on("scalar", sc)
                m2 = ev.inc()
                rot_ps1.release(u1, (ev, m2))
        qkv_done = (ev, ev.n)

        for hh in (0, 1):
            for c4 in range(4):
                u2, s2_, w2 = rot_psT.acquire()

                def pe(t, hh=hh, c4=c4, s2_=s2_, w2=w2, qkv_done=qkv_done):
                    _wait(t, qkv_done)
                    _wait(t, w2)
                    for j in range(4):
                        c = 4 * c4 + j
                        ins = t.transpose(psT[:, 1024 * s2_ + 128 * j:1024 * s2_ + 128 * j + 65],
                                          V65[0:65, 2048 * hh + 128 * c:2048 * hh + 128 * (c + 1)],
                                          E["IDB"][0:65, 0:65])
                    ins.then_inc(mm.h, 1)
                P.on("tensor", pe)
                m1 = mm.inc()

                def sc(s, hh=hh, c4=c4, s2_=s2_, m1=m1):
                    s.wait_ge(mm.h, m1)
                    src = psT[:, 1024 * s2_:1024 * s2_ + 512].rearrange(
                        "p (j c) -> p j c", j=4)[:, :, 0:65]
                    dst = VT[:, 2080 * hh + 260 * c4:2080 * hh + 260 * (c4 + 1)].rearrange(
                        "p (j c) -> p j c", j=4)
                    s.copy(out=dst, in_=src)
                    s.drain().then_inc(ev.h, 1)
                P.on("scalar", sc)
                m2 = ev.inc()
                rot_psT.release(u2, (ev, m2))
        vt_done = (ev, ev.n)

        for hh in (0, 1):
            h = 2 * p + hh
            if is_b:
                QHh = E["QH0"] if hh == 0 else E["QH1"]
                KHh = E["KH0"] if hh == 0 else E["KH1"]
                for tqb in range(NT):
                    for cc in range(4):
                        u1, s1_, w1 = rot_ps1.acquire()

                        def pe(t, tqb=tqb, cc=cc, s1_=s1_, w1=w1, QHh=QHh, KHh=KHh,
                               vt_done=vt_done):
                            _wait(t, vt_done)
                            _wait(t, w1)
                            t.matmul(psS[:, 512 * s1_:512 * (s1_ + 1)],
                                     QHh[0:64, 128 * tqb:128 * (tqb + 1)],
                                     KHh[0:64, 512 * cc:512 * (cc + 1)],
                                     start=True, stop=True).then_inc(mm.h, 1)
                        P.on("tensor", pe)
                        m1 = mm.inc()

                        def dv(v, cc=cc, s1_=s1_, m1=m1):
                            v.wait_ge(mm.h, m1)
                            v.tensor_reduce(out=MC4[:, cc:cc + 1],
                                            in_=psS[:, 512 * s1_:512 * (s1_ + 1)],
                                            axis=AX.X, op=A.max)
                            v.drain().then_inc(vv.h, 1)
                        P.on("vector", dv)
                        m2 = vv.inc()
                        rot_ps1.release(u1, (vv, m2))

                    def dv2(v, tqb=tqb):
                        v.tensor_reduce(out=M16[:, tqb:tqb + 1], in_=MC4[:],
                                        axis=AX.X, op=A.max)
                        v.drain()
                    P.on("vector", dv2)

                def dv3(v):
                    v.tensor_scalar(out=M16[:], in0=M16[:], scalar1=-1.0,
                                    scalar2=None, op0=A.mult)
                    v.drain().then_inc(vv.h, 1)
                P.on("vector", dv3)
                m3 = vv.inc()

                def pe2(t, m3=m3):
                    t.wait_ge(vv.h, m3)
                    t.transpose(psO[0:16, 0:128], M16[:], E["IDF"][:, :]).then_inc(mm.h, 1)
                P.on("tensor", pe2)
                m4 = mm.inc()

                def sc2(s, m4=m4):
                    s.wait_ge(mm.h, m4)
                    s.copy(out=MT[:], in_=psO[0:16, 0:128])
                    s.drain().then_inc(ev.h, 1)
                P.on("scalar", sc2)
                m5 = ev.inc()
                uM, mslot, wM = rot_mrow.acquire()

                def g1(sy, m5=m5, mslot=mslot, wM=wM, MROW=MROW, pre=gw.n,
                       sdn=sd.n):
                    sy.wait_ge(sd.h, sdn)
                    sy.wait_ge(gw.h, pre)
                    sy.wait_ge(ev.h, m5)
                    _wait(sy, wM)
                    sy.dma_start(out=MROW[mslot], in_=MT[:]).then_inc(gw.h, 16)
                P.on("sync", g1)
                m6 = gw.inc(16)

                def g2(g, m6=m6, mslot=mslot, MROW=MROW, QHh=QHh, pre=gd.n):
                    g.wait_ge(gd.h, pre)
                    g.wait_ge(gw.h, m6)
                    g.dma_start(out=QHh[64:65, :],
                                in_=MROW[mslot].rearrange("(o a) b -> o (a b)", o=1)
                                ).then_inc(gd.h, 16)
                P.on("gpsimd", g2)
                m7 = gd.inc(16)
                rot_mrow.release(uM, (gd, m7))
                row_ready = (gd, m7)
            else:
                row_ready = vt_done

            tkbs = range(NT)
            av_started = False
            for tkb in tkbs:
                chunks = _chunks_for_tkb(tkb) if not is_b else [
                    (512 * i, 512 * (i + 1)) for i in range(4)]
                for (cs, ce) in chunks:
                    u1, s1_, w1 = rot_ps1.acquire()
                    plen = ce - cs
                    poff = 512 * s1_ + (cs % 512)

                    def pe(t, tkb=tkb, cs=cs, ce=ce, s1_=s1_, w1=w1, poff=poff,
                           hh=hh, row_ready=row_ready, kh_ready=kh_ready):
                        _wait(t, row_ready)
                        _wait(t, kh_ready)
                        _wait(t, w1)
                        if is_b:
                            lhs = (E["KH0"] if hh == 0 else E["KH1"])[0:65, 128 * tkb:128 * (tkb + 1)]
                            rhs = (E["QH0"] if hh == 0 else E["QH1"])[0:65, cs:ce]
                        else:
                            hb = 64 * hh
                            lhs = KP[hb:hb + 64, 128 * tkb:128 * (tkb + 1)]
                            rhs = QP[hb:hb + 64, cs:ce]
                        t.matmul(psS[:, poff:poff + (ce - cs)], lhs, rhs,
                                 start=True, stop=True).then_inc(mm.h, 1)
                    P.on("tensor", pe)
                    m1 = mm.inc()

                    def sc(s, cs=cs, ce=ce, poff=poff, m1=m1):
                        s.wait_ge(mm.h, m1)
                        s.activation(out=PT[:, cs:ce], in_=psS[:, poff:poff + (ce - cs)],
                                     func=AF.Exp, scale=0.125)
                        s.drain().then_inc(ev.h, 1)
                    P.on("scalar", sc)
                    m2 = ev.inc()
                    rot_ps1.release(u1, (ev, m2))

                    def dv(v, tkb=tkb, cs=cs, ce=ce, m2=m2, diag=(cs == 128 * tkb)):
                        v.wait_ge(ev.h, m2)
                        if not is_b:
                            v.tensor_scalar(out=PT[:, cs:ce], in0=PT[:, cs:ce],
                                            scalar1=E_NEG10, scalar2=E_POS10,
                                            op0=A.max, op1=A.min)
                            if diag:
                                v.drain()
                                v.tensor_tensor(out=PT[:, cs:cs + 128],
                                                in0=PT[:, cs:cs + 128],
                                                in1=E["CM"][:, :], op=A.mult)
                        v.drain().then_inc(vv.h, 1)
                    P.on("vector", dv)
                    m3 = vv.inc()

                    def peav(t, tkb=tkb, cs=cs, ce=ce, m3=m3, hh=hh,
                             startflag=(tkb == 0)):
                        t.wait_ge(vv.h, m3)
                        t.matmul(psO[0:65, cs:ce],
                                 VT[:, 2080 * hh + 65 * tkb:2080 * hh + 65 * (tkb + 1)],
                                 PT[:, cs:ce], start=startflag, stop=(tkb == NT - 1),
                                 skip_group_check=True).then_inc(mm.h, 1)
                    P.on("tensor", peav)
                    mm.inc()
                    av_started = av_started or True
                av_started = True
            av_done = (mm, mm.n)

            def sc3(s, av_done=av_done):
                _wait(s, av_done)
                s.copy(out=ZF[:], in_=psO[64:65, :])
                s.drain().then_inc(ev.h, 1)
            P.on("scalar", sc3)
            m8 = ev.inc()

            def dv4(v, m8=m8):
                v.wait_ge(ev.h, m8)
                v.reciprocal(out=ZR[:], in_=ZF[:])
                v.drain().then_inc(vv.h, 1)
            P.on("vector", dv4)
            m9 = vv.inc()

            for n in range(4):
                u1, s1_, w1 = rot_ps1.acquire()

                def pe3(t, n=n, s1_=s1_, w1=w1, m9=m9):
                    t.wait_ge(vv.h, m9)
                    _wait(t, w1)
                    t.matmul(psS[0:64, 512 * s1_:512 * s1_ + 512],
                             ONESF[0:1, 0:64], ZR[0:1, 512 * n:512 * (n + 1)],
                             start=True, stop=True).then_inc(mm.h, 1)
                P.on("tensor", pe3)
                m10 = mm.inc()

                def sc4(s, n=n, m10=m10):
                    s.wait_ge(mm.h, m10)
                    s.copy(out=OF[:, 512 * n:512 * (n + 1)], in_=psO[0:64, 512 * n:512 * (n + 1)])
                    s.drain().then_inc(ev.h, 1)
                P.on("scalar", sc4)
                m11 = ev.inc()

                def dv5(v, n=n, h=h, s1_=s1_, m10=m10, m11=m11, AFOUT=AFOUT):
                    v.wait_ge(mm.h, m10)
                    v.wait_ge(ev.h, m11)
                    rb = 64 * (h % 2)
                    v.tensor_tensor(out=AFOUT[rb:rb + 64, 2048 * (h // 2) + 512 * n:
                                              2048 * (h // 2) + 512 * (n + 1)],
                                    in0=OF[:, 512 * n:512 * (n + 1)],
                                    in1=psS[0:64, 512 * s1_:512 * s1_ + 512],
                                    op=A.mult)
                    v.drain().then_inc(vv.h, 1)
                P.on("vector", dv5)
                m12 = vv.inc()
                rot_ps1.release(u1, (vv, m12))
            head_done = (vv, vv.n)
            pair_done[p] = (mm, mm.n)
            def fence(t, head_done=head_done):
                t.wait_ge(head_done[0].h, head_done[1])
            P.on("tensor", fence)
        rot_wpr.release(uW, (mm, mm.n))
        last_of_mark = (vv, vv.n)
    return last_of_mark


def emit_proj_residual(P, E, Wt, AFIN, X, af_wait, nk=ND, WOt=None, BIAS=None):
    mm, ev, vv, gd = P.sem("mm"), P.sem("ev"), P.sem("vv"), P.sem("gd")
    psS, psO, PRF = E["psS"], E["psO"], E["PRF"]
    WO = WOt if WOt is not None else E["WO"]
    rot_wo = Rot(2)
    rot_ps2 = Rot(4)
    rot_prf = Rot(2)
    rot_ps1 = Rot(2)
    last = None
    for ec in range(ND):
        uW, wslot, wwait = rot_wo.acquire()

        def ld(g, ec=ec, wslot=wslot, wwait=wwait, Wt=Wt, pre=gd.n):
            g.wait_ge(gd.h, pre)
            _wait(g, wwait)
            for k in range(nk):
                g.dma_start(out=WO[:, 1024 * wslot + 128 * k:1024 * wslot + 128 * (k + 1)],
                            in_=Wt[128 * k:128 * (k + 1), 128 * ec:128 * (ec + 1)]
                            ).then_inc(gd.h, 16)
        P.on("gpsimd", ld)
        gmark = gd.inc(16 * nk)

        for n in range(4):
            u2, s2_, w2 = rot_ps2.acquire()

            def pe(t, ec=ec, n=n, s2_=s2_, w2=w2, gmark=gmark, wslot=wslot,
                   AFIN=AFIN, af_wait=af_wait):
                t.wait_ge(gd.h, gmark)
                if af_wait is not None:
                    _wait(t, af_wait)
                _wait(t, w2)
                for k in range(nk):
                    ins = t.matmul(psO[:, 512 * s2_:512 * (s2_ + 1)],
                                   WO[:, 1024 * wslot + 128 * k:1024 * wslot + 128 * (k + 1)],
                                   AFIN[:, 2048 * k + 512 * n:2048 * k + 512 * (n + 1)],
                                   start=(k == 0), stop=(k == nk - 1))
                ins.then_inc(mm.h, 1)
            P.on("tensor", pe)
            m1 = mm.inc()

            uP, sp, wp = rot_prf.acquire()

            def sc(s, ec=ec, s2_=s2_, sp=sp, wp=wp, m1=m1, BIAS=BIAS):
                s.wait_ge(mm.h, m1)
                _wait(s, wp)
                if BIAS is None:
                    s.copy(out=PRF[:, 512 * sp:512 * (sp + 1)],
                           in_=psO[:, 512 * s2_:512 * (s2_ + 1)])
                else:
                    s.activation(out=PRF[:, 512 * sp:512 * (sp + 1)],
                                 in_=psO[:, 512 * s2_:512 * (s2_ + 1)],
                                 func=AF.Identity, bias=BIAS[:, ec:ec + 1], scale=1.0)
                s.drain().then_inc(ev.h, 1)
            P.on("scalar", sc)
            m2 = ev.inc()
            rot_ps2.release(u2, (ev, m2))

            u1, s1_, w1 = rot_ps1.acquire()

            def pe2(t, sp=sp, s1_=s1_, w1=w1, m2=m2):
                t.wait_ge(ev.h, m2)
                _wait(t, w1)
                for j in range(4):
                    ins = t.transpose(psS[:, 512 * s1_ + 128 * j:512 * s1_ + 128 * (j + 1)],
                                      PRF[:, 512 * sp + 128 * j:512 * sp + 128 * (j + 1)],
                                      E["IDF"][:, :])
                ins.then_inc(mm.h, 1)
            P.on("tensor", pe2)
            m3 = mm.inc()

            def dv(v, ec=ec, n=n, s1_=s1_, m3=m3, X=X):
                v.wait_ge(mm.h, m3)
                for j in range(4):
                    tt = 4 * n + j
                    xsl = X[:, 1024 * tt + 128 * ec:1024 * tt + 128 * (ec + 1)]
                    v.tensor_tensor(out=xsl, in0=psS[:, 512 * s1_ + 128 * j:512 * s1_ + 128 * (j + 1)],
                                    in1=xsl, op=A.add)
                v.drain().then_inc(vv.h, 1)
            P.on("vector", dv)
            m4 = vv.inc()
            rot_prf.release(uP, (mm, m3))
            rot_ps1.release(u1, (vv, m4))
            last = (vv, m4)
        rot_wo.release(uW, (mm, mm.n))
    return last



def _common_tiles(nc, E, need_attn=True, attn_b=False, skip_tr1=False):
    E["KAP"] = sb(nc, "KAP", [128, 1], F32)
    E["KC64"] = sb(nc, "KC64", [128, 1], F32)
    E["DD"] = sb(nc, "DD", [128, 1], F32)
    for k in ("SU", "SQ", "M", "V", "R", "MSQ", "AM16"):
        E[k] = sb(nc, k, [128, 16], F32)
    E["SQS"] = sb(nc, "SQS", [128, 1024], F32)
    E["AMR"] = sb(nc, "AMR", [128, 1], F32)
    if not skip_tr1:
        E["TR1"] = sb(nc, "TR1", [1, 128], F32)
    E["IDB"] = sb(nc, "IDB", [128, 128], BF16)
    E["IDF"] = sb(nc, "IDF", [128, 128], F32)
    E["ONESF"] = sb(nc, "ONESF", [1, 128], F32)
    E["psS"] = psum(nc, "psS", [128, 1024], F32)
    E["psO"] = psum(nc, "psO", [128, 2048], F32)
    E["psT"] = psum(nc, "psT", [128, 2048], BF16)
    if need_attn:
        E["WPR"] = sb(nc, "WPR", [128, 8192], BF16)
        E["V65"] = sb(nc, "V65", [65, 4096], BF16)
        E["VT"] = sb(nc, "VT", [128, 4160], BF16)
        E["PT"] = sb(nc, "PT", [128, 2048], BF16)
        E["ZF"] = sb(nc, "ZF", [1, 2048], F32)
        E["ZR"] = sb(nc, "ZR", [1, 2048], F32)
        E["OF"] = sb(nc, "OF", [64, 2048], F32)
        E["VB"] = sb(nc, "VB", [65, 1], F32)
        if attn_b:
            E["QH0"] = sb(nc, "QH0", [65, 2048], BF16)
            E["QH1"] = sb(nc, "QH1", [65, 2048], BF16)
            E["KH0"] = sb(nc, "KH0", [65, 2048], BF16)
            E["KH1"] = sb(nc, "KH1", [65, 2048], BF16)
            E["M16"] = sb(nc, "M16", [128, 16], F32)
            E["MC4"] = sb(nc, "MC4", [128, 4], F32)
            E["MT"] = sb(nc, "MT", [16, 128], F32)
        else:
            E["QP"] = sb(nc, "QP", [128, 2048], BF16)
            E["KP"] = sb(nc, "KP", [128, 2048], BF16)
            E["CM"] = sb(nc, "CM", [128, 128], BF16)


def _emit_const_loads(P, E, consts, attn_b=False):
    gd = P.sem("gd")
    ident = consts["ident"]

    def g(gp, attn_b=attn_b):
        gp.dma_start(out=E["IDB"][:], in_=ident).then_inc(gd.h, 16)
        gp.dma_start(out=E["IDF"][:], in_=ident).then_inc(gd.h, 16)
        if not attn_b and "CM" in E:
            gp.dma_start(out=E["CM"][:], in_=consts["cmask"]).then_inc(gd.h, 16)
    P.on("gpsimd", g)
    n = gd.inc(16 * (3 if (not attn_b and "CM" in E) else 2))

    def v(vec):
        vec.memset(E["ONESF"][:], 1.0)
        if "VB" in E:
            vec.memset(E["VB"][:], 0.0)
            vec.drain()
            vec.memset(E["VB"][64:65, :], 1.0)
        vec.drain().then_inc(P.sem("vv").h, 1)
    P.on("vector", v)
    vvn = P.sem("vv").inc()
    return (P.sem("gd"), n), (P.sem("vv"), vvn)


DBG = {"tap": None}


def build_L1():
    nc = bass.Bass("TRN2", target_bir_lowering=False)
    x = nc.dram_tensor("x", [T, D], F32, kind="ExternalInput").ap()
    wg = nc.dram_tensor("wg_a", [D, NPAIR * 512], F32, kind="ExternalInput").ap()
    wot = nc.dram_tensor("wot_a", [D, D], F32, kind="ExternalInput").ap()
    ident = nc.dram_tensor("ident", [128, 128], F32, kind="ExternalInput").ap()
    cmask = nc.dram_tensor("cmask", [128, 128], F32, kind="ExternalInput").ap()
    x1 = nc.dram_tensor("x1", [T, D], F32, kind="ExternalOutput").ap()
    p1 = nc.dram_tensor("p1", [1, 1], F32, kind="ExternalOutput").ap()

    P = Prog(nc)
    E = {}
    _common_tiles(nc, E, need_attn=True, attn_b=False)
    E["X"] = sb(nc, "X", [128, NT * 1024], F32)
    E["XLN"] = sb(nc, "XLN", [128, NT * 1024], BF16)
    E["XLT"] = sb(nc, "XLT", [128, ND * 2048], BF16)
    E["AFOUT"] = E["XLN"]
    E["WO"] = sb(nc, "WO", [128, 2048], BF16)
    E["PRF"] = E["SQS"]
    E["P11"] = sb(nc, "P11", [1, 1], F32)

    sd, gd, vv = P.sem("sd"), P.sem("gd"), P.sem("vv")
    st = P.sem("st")

    def ldx(sy, x=x):
        sy.dma_start(out=E["X"][:, :].rearrange("p (t d) -> p t d", t=NT),
                     in_=x.rearrange("(t p) d -> p t d", p=128)).then_inc(sd.h, 16)
    P.on("sync", ldx)
    x_mark = sd.inc(16)

    (cg, cgn), cvv = _emit_const_loads(P, E, {"ident": ident, "cmask": cmask})

    P.on("tensor", lambda t, cg=cg, cgn=cgn, cvv=cvv: (
        t.wait_ge(cg.h, cgn), t.wait_ge(cvv[0].h, cvv[1])))
    P.on("scalar", lambda s, cg=cg, cgn=cgn, cvv=cvv: (
        s.wait_ge(cg.h, cgn), s.wait_ge(cvv[0].h, cvv[1])))

    ln1 = emit_ln(P, E, E["X"], E["XLN"], [(sd, x_mark)])
    rot_psT = Rot(2)
    xlt_done = emit_tm2fm(P, E, E["XLN"], E["XLT"], ln1, rot_psT)
    af_done = emit_attention(P, E, "a", wg, E["XLT"], E["AFOUT"],
                             (P.sem("ev"), xlt_done[1]) if False else xlt_done)
    res_done = emit_proj_residual(P, E, wot, E["AFOUT"], E["X"], af_done)
    ln2 = emit_ln(P, E, E["X"], E["XLN"], [res_done])
    pm = emit_amax_partial(P, E, E["XLN"], E["P11"], ln2)

    def stx(sy, pm=pm, res_done=res_done):
        _wait(sy, res_done)
        sy.dma_start(out=x1.rearrange("(t p) d -> p t d", p=128),
                     in_=E["X"][:, :].rearrange("p (t d) -> p t d", t=NT)
                     ).then_inc(st.h, 16)
        _wait(sy, pm)
        sy.dma_start(out=p1, in_=E["P11"][:]).then_inc(st.h, 16)
        sy.wait_ge(st.h, 32)
    P.on("sync", stx)

    if DBG.get("tap"):
        tap = DBG["tap"]
        dbg = nc.dram_tensor("dbg", [128, 16384], F32, kind="ExternalOutput").ap()
        src = {"xln": (E["XLN"], ln1), "xlt": (E["XLT"], xlt_done),
               "af": (E["AFOUT"], af_done)}[tap]
        dsem = P.sem("dsem")

        def dtap(g, src=src):
            _wait(g, src[1])
            g.dma_start(out=dbg, in_=src[0][:, :]).then_inc(dsem.h, 16)
            g.wait_ge(dsem.h, 16)
        P.on("gpsimd", dtap)
    P.attach()
    return nc



def _prep_host(inputs):
    f = np.ascontiguousarray
    out = {}
    out["ident"] = np.eye(128, dtype=np.float32)
    cm = np.ones((128, 128), dtype=np.float32)
    cm[np.triu_indices(128, k=1)] = 0.0
    cm = np.tril(np.ones((128, 128), dtype=np.float32), k=0)
    cm = np.ascontiguousarray(cm.T)
    cm = np.triu(np.ones((128, 128), dtype=np.float32), k=0)
    out["cmask"] = f(cm)

    Wqkv = inputs["L0_attn_a_Wqkv"]
    WT = Wqkv.T
    blocks = []
    for p in range(NPAIR):
        q = WT[:, 128 * p:128 * (p + 1)]
        k = WT[:, 1024 + 128 * p:1024 + 128 * (p + 1)]
        v0 = WT[:, 2048 + 128 * p:2048 + 128 * p + 64]
        v1 = WT[:, 2048 + 128 * p + 64:2048 + 128 * (p + 1)]
        z = np.zeros((D, 64), dtype=np.float32)
        blocks.append(np.concatenate([q, k, v0, z, v1, z], axis=1))
    out["wg_a"] = f(np.concatenate(blocks, axis=1))
    out["wot_a"] = f(inputs["L0_attn_a_Wout"].T)

    qw = inputs["L0_attn_b_qkv_w"]
    qb = inputs["L0_attn_b_qkv_b"]
    hm = inputs["L0_attn_b_head_mask"]
    QT = qw.T
    blocks = []
    hmg = np.zeros((NPAIR * 512,), dtype=np.float32)
    bg = np.zeros((NPAIR * 512,), dtype=np.float32)
    og = np.zeros((NPAIR * 512,), dtype=np.float32)
    for p in range(NPAIR):
        h0, h1 = 2 * p, 2 * p + 1
        cols = []
        meta = []
        for (h, c) in ((h0, 0), (h1, 0), (h0, 1), (h1, 1)):
            cols.append(QT[:, 192 * h + 64 * c:192 * h + 64 * (c + 1)])
            meta.append((h, c, 64))
        cols.append(QT[:, 192 * h0 + 128:192 * h0 + 192])
        meta.append((h0, 2, 64))
        cols.append(np.zeros((D, 64), dtype=np.float32))
        meta.append((None, None, 64))
        cols.append(QT[:, 192 * h1 + 128:192 * h1 + 192])
        meta.append((h1, 2, 64))
        cols.append(np.zeros((D, 64), dtype=np.float32))
        meta.append((None, None, 64))
        blk = np.concatenate(cols, axis=1)
        blocks.append(blk)
        off = 512 * p
        i = 0
        for (h, c, w) in meta:
            if h is not None:
                hmg[off + i:off + i + w] = hm[h]
                e0 = 192 * h + 64 * c
                bg[off + i:off + i + w] = qb[e0:e0 + w]
            else:
                og[off + i] = 1.0
            i += w
    out["wg_b"] = f(np.concatenate(blocks, axis=1))
    out["hmg"] = f(hmg.reshape(32, 128).T)
    out["bg"] = f(bg.reshape(32, 128).T)
    out["og"] = f(og.reshape(32, 128).T)
    out["wot_b"] = f(inputs["L0_attn_b_fc_w"].T)
    out["fcb"] = f(inputs["L0_attn_b_fc_b"].reshape(8, 128).T)

    out["w_in"] = f(inputs["L0_ffn_W_in"])
    out["w_int"] = f(inputs["L0_ffn_W_in"].T)
    out["w_out"] = f(inputs["L0_ffn_W_out"])
    out["b_in"] = f(inputs["L0_ffn_b_in"].reshape(32, 128).T)
    out["b_out"] = f(inputs["L0_ffn_b_out"].reshape(8, 128).T)
    return out



def emit_amax_combine(P, E, PIN, SAB, AMB, wait_sem_mark, qscale, scr=("AS1", "AS2")):
    mm, ev, vv, gd = P.sem("mm"), P.sem("ev"), P.sem("vv"), P.sem("gd")
    AS1, AS2 = E[scr[0]], E[scr[1]]
    psS = E["psS"]

    AIN = E["AIN"] if scr[0] == "AS1" else E["AINB"]

    def g(gp, PIN=PIN, AIN=AIN, pre=gd.n):
        gp.wait_ge(gd.h, pre)
        gp.dma_start(out=AIN[:], in_=PIN.rearrange("a b -> b a")).then_inc(gd.h, 16)
    P.on("gpsimd", g)
    gm = gd.inc(16)

    def dv(v, gm=gm):
        v.wait_ge(gd.h, gm)
        v.tensor_reduce(out=AS1[:], in_=AIN[:], axis=AX.X, op=A.max,
                        apply_absolute_value=True)
        v.drain()
        v.tensor_scalar(out=AS1[:], in0=AS1[:], scalar1=1e-8, scalar2=None, op0=A.max)
        v.drain()
        v.reciprocal(out=AS2[:], in_=AS1[:])
        v.drain()
        v.tensor_scalar(out=AS2[:], in0=AS2[:], scalar1=qscale, scalar2=None, op0=A.mult)
        v.drain().then_inc(vv.h, 1)
    P.on("vector", dv)
    m0 = vv.inc()

    prev = wait_sem_mark
    for (bsrc, dst) in ((AS2, SAB), (AS1, AMB)):
        def pe(t, bsrc=bsrc, m0=m0, prev=prev):
            t.wait_ge(vv.h, m0)
            _wait(t, prev)
            t.matmul(psS[0:128, 0:1], E["ONESF"][0:1, :], bsrc[0:1, 0:1],
                     start=True, stop=True).then_inc(mm.h, 1)
        P.on("tensor", pe)
        m1 = mm.inc()

        def sc(s, dst=dst, m1=m1):
            s.wait_ge(mm.h, m1)
            s.copy(out=dst[:], in_=psS[0:128, 0:1])
            s.drain().then_inc(ev.h, 1)
        P.on("scalar", sc)
        prev = (ev, ev.inc())
    return prev


def emit_quant_x(P, E, XLN, RQ, SAB, wait):
    vv = P.sem("vv")
    T1 = E["SQS"]

    def dv(v, XLN=XLN, RQ=RQ, SAB=SAB, wait=tuple(wait)):
        for w in wait:
            _wait(v, w)
        for tt in range(NT):
            v.tensor_scalar(out=T1[:], in0=XLN[:, 1024 * tt:1024 * (tt + 1)],
                            scalar1=SAB[:, 0:1], scalar2=C32, op0=A.mult, op1=A.add)
            v.drain()
            v.tensor_scalar(out=RQ[:, 1024 * tt:1024 * (tt + 1)], in0=T1[:],
                            scalar1=E["KC64"][:, 0:1], scalar2=E["DD"][:, 0:1],
                            op0=A.add, op1=A.add)
            v.drain()
        v.drain().then_inc(vv.h, 1)
    P.on("vector", dv)
    return (vv, vv.inc())


def emit_qin_prep(P, E, w_int, QIN, wait, rot_psT=None):
    mm, ev, vv, gd, gw, sd = (P.sem("mm"), P.sem("ev"), P.sem("vv"),
                              P.sem("gd"), P.sem("gw"), P.sem("sd"))
    WIT, QTC, QB = E["WIT"], E["QTC"], E["QB"]
    AC32, SCW = E["AC32"], E["SCW"]
    psT = E["psT"]
    rot_wit = Rot(2)
    if rot_psT is None:
        rot_psT = Rot(2)
    rot_qb = Rot(2)
    rot_qtc = Rot(2)
    last = None
    for j in range(NJ):
        uW, ws, ww = rot_wit.acquire()
        uQT, qs_j, wqt = rot_qtc.acquire()

        def g(gp, j=j, ws=ws, ww=ww, w_int=w_int, wait=wait, pre=gd.n):
            if j == 0 and wait is not None:
                _wait(gp, wait)
            gp.wait_ge(gd.h, pre)
            _wait(gp, ww)
            gp.dma_start(out=WIT[:, 1024 * ws:1024 * (ws + 1)],
                         in_=w_int[128 * j:128 * (j + 1), :]).then_inc(gd.h, 16)
        P.on("gpsimd", g)
        gm = gd.inc(16)

        def dv(v, j=j, ws=ws, gm=gm, qs_j=qs_j, wqt=wqt):
            v.wait_ge(gd.h, gm)
            _wait(v, wqt)
            wsl = WIT[:, 1024 * ws:1024 * (ws + 1)]
            v.tensor_reduce(out=AC32[:, j:j + 1], in_=wsl, axis=AX.X, op=A.max,
                            apply_absolute_value=True)
            v.drain()
            v.tensor_scalar(out=SCW[:, 0:1], in0=AC32[:, j:j + 1], scalar1=1e-8,
                            scalar2=None, op0=A.max)
            v.drain()
            v.reciprocal(out=SCW[:, 1:2], in_=SCW[:, 0:1])
            v.drain()
            v.tensor_scalar(out=SCW[:, 1:2], in0=SCW[:, 1:2], scalar1=QW,
                            scalar2=None, op0=A.mult)
            v.drain()
            v.tensor_scalar(out=E["T1W"][:], in0=wsl, scalar1=SCW[:, 1:2],
                            scalar2=C32, op0=A.mult, op1=A.add)
            v.drain()
            v.tensor_scalar(out=QTC[:, 1024 * qs_j:1024 * (qs_j + 1)],
                            in0=E["T1W"][:], scalar1=E["KC64"][:, 0:1],
                            scalar2=E["DD"][:, 0:1], op0=A.add, op1=A.add)
            v.drain().then_inc(vv.h, 1)
        P.on("vector", dv)
        m1 = vv.inc()
        rot_wit.release(uW, (vv, m1))

        for g4 in range(2):
            uT, ts_, wt = rot_psT.acquire()

            def pe(t, j=j, g4=g4, ts_=ts_, wt=wt, m1=m1, qs_j=qs_j):
                t.wait_ge(vv.h, m1)
                _wait(t, wt)
                for jj in range(4):
                    k = 4 * g4 + jj
                    ins = t.transpose(
                        psT[:, 1024 * ts_ + 128 * jj:1024 * ts_ + 128 * (jj + 1)],
                        QTC[:, 1024 * qs_j + 128 * k:1024 * qs_j + 128 * (k + 1)],
                        E["IDB"][:, :])
                ins.then_inc(mm.h, 1)
            P.on("tensor", pe)
            m2 = mm.inc()

            uQ, qs, wq = rot_qb.acquire()

            def sc(s, ts_=ts_, qs=qs, wq=wq, m2=m2):
                s.wait_ge(mm.h, m2)
                _wait(s, wq)
                s.copy(out=QB[:, 512 * qs:512 * (qs + 1)],
                       in_=psT[:, 1024 * ts_:1024 * ts_ + 512])
                s.drain().then_inc(ev.h, 1)
            P.on("scalar", sc)
            m3 = ev.inc()
            rot_psT.release(uT, (ev, m3))

            def g2(sy, j=j, g4=g4, qs=qs, m3=m3, QIN=QIN, pre=gw.n, sdn=sd.n):
                sy.wait_ge(sd.h, sdn)
                sy.wait_ge(gw.h, pre)
                sy.wait_ge(ev.h, m3)
                sy.dma_start(
                    out=QIN[j, 4 * g4:4 * (g4 + 1)].rearrange("k r c -> r k c"),
                    in_=QB[:, 512 * qs:512 * (qs + 1)].rearrange("p (k c) -> p k c", k=4)
                ).then_inc(gw.h, 16)
            P.on("sync", g2)
            m4 = gw.inc(16)
            rot_qb.release(uQ, (gw, m4))
            last = (gw, m4)
        rot_qtc.release(uQT, (mm, mm.n))
    return last


def emit_qout_prep(P, E, w_out, QOUT, wait):
    vv, gd, gw, sd = P.sem("vv"), P.sem("gd"), P.sem("gw"), P.sem("sd")
    WIT, QTC = E["WIT"], E["QTC"]
    AR32, SCW = E["AR32"], E["SCW"]
    rot_wit = Rot(2)
    rot_qtc = Rot(2)
    last = None
    for h in range(NJ):
        uW, ws, ww = rot_wit.acquire()

        def g(gp, h=h, ws=ws, ww=ww, w_out=w_out, wait=wait, pre=gd.n):
            if h == 0 and wait is not None:
                _wait(gp, wait)
            gp.wait_ge(gd.h, pre)
            _wait(gp, ww)
            gp.dma_start(out=WIT[:, 1024 * ws:1024 * (ws + 1)],
                         in_=w_out[128 * h:128 * (h + 1), :]).then_inc(gd.h, 16)
        P.on("gpsimd", g)
        gm = gd.inc(16)

        uQ, qs, wq = rot_qtc.acquire()

        def dv(v, h=h, ws=ws, qs=qs, wq=wq, gm=gm):
            v.wait_ge(gd.h, gm)
            _wait(v, wq)
            wsl = WIT[:, 1024 * ws:1024 * (ws + 1)]
            v.tensor_reduce(out=AR32[:, h:h + 1], in_=wsl, axis=AX.X, op=A.max,
                            apply_absolute_value=True)
            v.drain()
            v.tensor_scalar(out=SCW[:, 0:1], in0=AR32[:, h:h + 1], scalar1=1e-8,
                            scalar2=None, op0=A.max)
            v.drain()
            v.reciprocal(out=SCW[:, 1:2], in_=SCW[:, 0:1])
            v.drain()
            v.tensor_scalar(out=SCW[:, 1:2], in0=SCW[:, 1:2], scalar1=QW,
                            scalar2=None, op0=A.mult)
            v.drain()
            v.tensor_scalar(out=E["T1W"][:], in0=wsl, scalar1=SCW[:, 1:2],
                            scalar2=C32, op0=A.mult, op1=A.add)
            v.drain()
            v.tensor_scalar(out=QTC[:, 1024 * qs:1024 * (qs + 1)], in0=E["T1W"][:],
                            scalar1=E["KC64"][:, 0:1], scalar2=E["DD"][:, 0:1],
                            op0=A.add, op1=A.add)
            v.drain().then_inc(vv.h, 1)
        P.on("vector", dv)
        m1 = vv.inc()
        rot_wit.release(uW, (vv, m1))

        def g2(sy, h=h, qs=qs, m1=m1, QOUT=QOUT, pre=gw.n, sdn=sd.n):
            sy.wait_ge(sd.h, sdn)
            sy.wait_ge(gw.h, pre)
            sy.wait_ge(vv.h, m1)
            sy.dma_start(out=QOUT[128 * h:128 * (h + 1), :],
                         in_=QTC[:, 1024 * qs:1024 * (qs + 1)]).then_inc(gw.h, 16)
        P.on("sync", g2)
        m2 = gw.inc(16)
        rot_qtc.release(uQ, (gw, m2))
        last = (gw, m2)
    return last


def emit_cscale(P, E, SRC32, AMXB, DST32, wait):
    vv = P.sem("vv")

    def dv(v, SRC32=SRC32, AMXB=AMXB, DST32=DST32, wait=tuple(wait)):
        for w in wait:
            _wait(v, w)
        v.tensor_scalar(out=DST32[:], in0=SRC32[:], scalar1=AMXB[:, 0:1],
                        scalar2=1.0 / (QW * QA), op0=A.mult, op1=A.mult)
        v.drain().then_inc(vv.h, 1)
    P.on("vector", dv)
    return (vv, vv.inc())


def emit_mm1(P, E, QIN, RT, mode, tok_off, tok_w, HQ, waits):
    mm, ev, vv, gd = P.sem("mm"), P.sem("ev"), P.sem("vv"), P.sem("gd")
    psS = E["psS"]
    QW1, HIDC, T1H = E["QW1"], E["HIDC"], E["T1H"]
    CJ, BI = E["CJ"], E["BI"]
    rot_qw = Rot(2)
    rot_ps1 = Rot(2)
    rot_hid = Rot(2)
    nn = tok_w // 512
    last = None
    for j in range(NJ):
        uW, ws, ww = rot_qw.acquire()

        def g(gp, j=j, ws=ws, ww=ww, QIN=QIN, waits=tuple(waits), pre=gd.n):
            if j == 0:
                for w in waits:
                    _wait(gp, w)
            gp.wait_ge(gd.h, pre)
            _wait(gp, ww)
            gp.dma_start(out=QW1[:, 1024 * ws:1024 * (ws + 1)].rearrange(
                "p (k c) -> p k c", k=ND),
                in_=QIN[j].rearrange("k r c -> r k c")).then_inc(gd.h, 16)
        P.on("gpsimd", g)
        gm = gd.inc(16)

        for n in range(nn):
            u1, s1_, w1 = rot_ps1.acquire()

            def pe(t, j=j, n=n, ws=ws, s1_=s1_, w1=w1, gm=gm, RT=RT,
                   tok_off=tok_off, waits=tuple(waits)):
                t.wait_ge(gd.h, gm)
                if j == 0 and n == 0:
                    for w in waits:
                        _wait(t, w)
                _wait(t, w1)
                for k in range(ND):
                    ins = t.matmul(psS[:, 512 * s1_:512 * (s1_ + 1)],
                                   QW1[:, 1024 * ws + 128 * k:1024 * ws + 128 * (k + 1)],
                                   RT[:, 2048 * k + tok_off + 512 * n:
                                      2048 * k + tok_off + 512 * (n + 1)],
                                   start=(k == 0), stop=(k == ND - 1))
                ins.then_inc(mm.h, 1)
            P.on("tensor", pe)
            m1 = mm.inc()

            uH, hs, wh = rot_hid.acquire()

            def sc(s, j=j, s1_=s1_, hs=hs, wh=wh, m1=m1):
                s.wait_ge(mm.h, m1)
                _wait(s, wh)
                s.activation(out=HIDC[:, 512 * hs:512 * (hs + 1)],
                             in_=psS[:, 512 * s1_:512 * (s1_ + 1)],
                             func=AF.Gelu, scale=CJ[:, j:j + 1], bias=BI[:, j:j + 1])
                s.drain().then_inc(ev.h, 1)
            P.on("scalar", sc)
            m2 = ev.inc()
            rot_ps1.release(u1, (ev, m2))

            if mode == "probe":
                def dv(v, j=j, n=n, hs=hs, m2=m2):
                    v.wait_ge(ev.h, m2)
                    v.tensor_reduce(out=E["HMM"][:, 4 * j + n:4 * j + n + 1],
                                    in_=HIDC[:, 512 * hs:512 * (hs + 1)],
                                    axis=AX.X, op=A.max, apply_absolute_value=True)
                    v.drain().then_inc(vv.h, 1)
                P.on("vector", dv)
            else:
                def dv(v, j=j, n=n, hs=hs, m2=m2, HQ=HQ):
                    v.wait_ge(ev.h, m2)
                    hsl = HIDC[:, 512 * hs:512 * (hs + 1)]
                    v.tensor_scalar(out=T1H[:], in0=hsl, scalar1=E["SHB"][:, 0:1],
                                    scalar2=C32, op0=A.mult, op1=A.add)
                    v.drain()
                    v.tensor_scalar(out=T1H[:], in0=T1H[:], scalar1=E["KC64"][:, 0:1],
                                    scalar2=E["DD"][:, 0:1], op0=A.add, op1=A.add)
                    v.drain()
                    v.tensor_scalar(out=HQ[:, 1024 * j + 512 * n:1024 * j + 512 * (n + 1)],
                                    in0=T1H[:], scalar1=E["C2J"][:, j:j + 1],
                                    scalar2=None, op0=A.mult)
                    v.drain().then_inc(vv.h, 1)
                P.on("vector", dv)
            m3 = vv.inc()
            rot_hid.release(uH, (vv, m3))
            last = (vv, m3)
        rot_qw.release(uW, (mm, mm.n))
    return last


def emit_colmax_final(P, E, SRC, ncols, OUT11, wait):
    mm, ev, vv = P.sem("mm"), P.sem("ev"), P.sem("vv")

    def d1(v, SRC=SRC, ncols=ncols, wait=wait):
        _wait(v, wait)
        v.tensor_reduce(out=E["AMR"][:], in_=SRC[:, 0:ncols], axis=AX.X, op=A.max,
                        apply_absolute_value=True)
        v.drain().then_inc(vv.h, 1)
    P.on("vector", d1)
    m0 = vv.inc()

    def pe(t, m0=m0):
        t.wait_ge(vv.h, m0)
        t.transpose(E["psS"][0:1, 0:128], E["AMR"][:], E["IDF"][:, :]).then_inc(mm.h, 1)
    P.on("tensor", pe)
    m1 = mm.inc()

    def sc(s, m1=m1):
        s.wait_ge(mm.h, m1)
        s.copy(out=E["TR1"][:], in_=E["psS"][0:1, 0:128])
        s.drain().then_inc(ev.h, 1)
    P.on("scalar", sc)
    m2 = ev.inc()

    def d2(v, m2=m2, OUT11=OUT11):
        v.wait_ge(ev.h, m2)
        v.tensor_reduce(out=OUT11[0:1, 0:1], in_=E["TR1"][:], axis=AX.X, op=A.max,
                        apply_absolute_value=True)
        v.drain().then_inc(vv.h, 1)
    P.on("vector", d2)
    return (vv, vv.inc())


def _ffn_tiles(nc, E):
    E["AIN"] = sb(nc, "AIN", [1, NCORES], F32)
    E["AINB"] = E["SU"][0:1, 0:NCORES]
    E["AS1"] = sb(nc, "AS1", [1, 1], F32)
    E["AS2"] = sb(nc, "AS2", [1, 1], F32)
    E["AS1B"] = E["SQ"][0:1, 0:1]
    E["AS2B"] = E["SQ"][0:1, 1:2]
    E["SAB"] = sb(nc, "SAB", [128, 1], F32)
    E["AMB"] = sb(nc, "AMB", [128, 1], F32)
    E["WIT"] = sb(nc, "WIT", [128, 2048], F32)
    E["T1W"] = sb(nc, "T1W", [128, 1024], F32)
    E["QTC2"] = None
    E["AC32"] = sb(nc, "AC32", [128, NJ], F32)
    E["AR32"] = sb(nc, "AR32", [128, NJ], F32)
    E["SCW"] = sb(nc, "SCW", [128, 2], F32)
    E["CJ"] = sb(nc, "CJ", [128, NJ], F32)
    E["BI"] = sb(nc, "BI", [128, NJ], F32)
    E["QW1"] = sb(nc, "QW1", [128, 2048], BF16)


def build_probe():
    nc = bass.Bass("TRN2", target_bir_lowering=False)
    xin = nc.dram_tensor("xin", [T, D], F32, kind="ExternalInput").ap()
    pin = nc.dram_tensor("pin", [NCORES, 1], F32, kind="ExternalInput").ap()
    w_int = nc.dram_tensor("w_int", [FF, D], F32, kind="ExternalInput").ap()
    b_in = nc.dram_tensor("b_in", [128, NJ], F32, kind="ExternalInput").ap()
    ident = nc.dram_tensor("ident", [128, 128], F32, kind="ExternalInput").ap()
    pout = nc.dram_tensor("pout", [1, 1], F32, kind="ExternalOutput").ap()
    QIN = nc.dram_tensor("qin_scr", [NJ, ND, 128, 128], BF16, kind="Internal").ap()

    P = Prog(nc)
    E = {}
    _common_tiles(nc, E, need_attn=False)
    _ffn_tiles(nc, E)
    E["X"] = sb(nc, "X", [128, NT * 1024], F32)
    E["XLN"] = sb(nc, "XLN", [128, NT * 1024], BF16)
    E["RQ"] = sb(nc, "RQ", [128, NT * 1024], BF16)
    E["XLT"] = sb(nc, "XLT", [128, ND * 2048], BF16)
    E["HMM"] = sb(nc, "HMM", [128, 128], F32)
    E["QTC"] = sb(nc, "QTC", [128, 2048], BF16)
    E["QB"] = sb(nc, "QB", [128, 1024], BF16)
    E["HIDC"] = sb(nc, "HIDC", [128, 1024], BF16)
    E["T1H"] = E["T1W"][:, 0:512]
    E["P11"] = sb(nc, "P11", [1, 1], F32)

    sd, gd, st = P.sem("sd"), P.sem("gd"), P.sem("st")

    def ldx(sy, xin=xin):
        sy.dma_start(out=E["X"][:, :].rearrange("p (t d) -> p t d", t=NT),
                     in_=xin.rearrange("(t p) d -> p t d", p=128)).then_inc(sd.h, 16)
        sy.dma_start(out=E["BI"][:], in_=b_in).then_inc(sd.h, 16)
    P.on("sync", ldx)
    x_mark = sd.inc(32)

    (cg, cgn), cvv = _emit_const_loads(P, E, {"ident": ident})
    P.on("tensor", lambda t, cg=cg, cgn=cgn, cvv=cvv: (
        t.wait_ge(cg.h, cgn), t.wait_ge(cvv[0].h, cvv[1])))
    P.on("scalar", lambda s, xm=x_mark: s.wait_ge(sd.h, xm))
    kap = emit_kappa(P, E)

    amx = emit_amax_combine(P, E, pin, E["SAB"], E["AMB"], None, QA)
    ln = emit_ln(P, E, E["X"], E["XLN"], [(sd, x_mark)])
    qx = emit_quant_x(P, E, E["XLN"], E["RQ"], E["SAB"],
                      [ln, amx, (P.sems["vv"], kap[1])])
    rot_psT = Rot(2)
    rt = emit_tm2fm(P, E, E["RQ"], E["XLT"], qx, rot_psT)
    qp = emit_qin_prep(P, E, w_int, QIN, None, rot_psT=rot_psT)
    cj = emit_cscale(P, E, E["AC32"], E["AMB"], E["CJ"], [amx, qp])
    h = emit_mm1(P, E, QIN, E["XLT"], "probe", 0, T, None,
                 [rt, cj, (sd, x_mark)])
    pm = emit_colmax_final(P, E, E["HMM"], 128, E["P11"], h)

    gw = P.sem("gw")

    def stx(sy, pm=pm, gwn=gw.n):
        sy.wait_ge(gw.h, gwn)
        _wait(sy, pm)
        sy.dma_start(out=pout, in_=E["P11"][:]).then_inc(st.h, 16)
        sy.wait_ge(st.h, 16)
    P.on("sync", stx)
    P.attach()
    return nc


def build_main(final):
    nc = bass.Bass("TRN2", target_bir_lowering=False)
    xin = nc.dram_tensor("xin", [T, D], F32, kind="ExternalInput").ap()
    pin = nc.dram_tensor("pin", [NCORES, 1], F32, kind="ExternalInput").ap()
    phin = nc.dram_tensor("phin", [NCORES, 1], F32, kind="ExternalInput").ap()
    w_int = nc.dram_tensor("w_int", [FF, D], F32, kind="ExternalInput").ap()
    w_out = nc.dram_tensor("w_out", [FF, D], F32, kind="ExternalInput").ap()
    b_in = nc.dram_tensor("b_in", [128, NJ], F32, kind="ExternalInput").ap()
    b_out = nc.dram_tensor("b_out", [128, ND], F32, kind="ExternalInput").ap()
    ident = nc.dram_tensor("ident", [128, 128], F32, kind="ExternalInput").ap()
    if not final:
        wg_b = nc.dram_tensor("wg_b", [D, NPAIR * 512], F32, kind="ExternalInput").ap()
        hmg = nc.dram_tensor("hmg", [128, 32], F32, kind="ExternalInput").ap()
        bgt = nc.dram_tensor("bgt", [128, 32], F32, kind="ExternalInput").ap()
        ogt = nc.dram_tensor("ogt", [128, 32], F32, kind="ExternalInput").ap()
        wot_b = nc.dram_tensor("wot_b", [D, D], F32, kind="ExternalInput").ap()
        fcb = nc.dram_tensor("fcb", [128, ND], F32, kind="ExternalInput").ap()
        pout = nc.dram_tensor("pout", [1, 1], F32, kind="ExternalOutput").ap()
    xout = nc.dram_tensor("xout", [T, D], F32, kind="ExternalOutput").ap()
    QIN = nc.dram_tensor("qin_scr", [NJ, ND, 128, 128], BF16, kind="Internal").ap()
    QOUT = nc.dram_tensor("qout_scr", [FF, D], BF16, kind="Internal").ap()
    MROW = [nc.dram_tensor(f"mrow{i}", [16, 128], F32, kind="Internal").ap()
            for i in range(2)]

    P = Prog(nc)
    E = {}
    _common_tiles(nc, E, need_attn=False, skip_tr1=True)
    _ffn_tiles(nc, E)
    E["X"] = sb(nc, "X", [128, NT * 1024], F32)
    BIGB = sb(nc, "BIGB", [128, 2 * NT * 1024], BF16)
    E["XLN"] = BIGB[:, 0:16384]
    E["RQ"] = BIGB[:, 16384:32768]
    E["HQ"] = BIGB
    E["AFOUT"] = BIGB[:, 0:16384]
    if not final:
        E["QH0"] = BIGB[0:65, 16384 + 0 * 2048:16384 + 1 * 2048]
        E["QH1"] = BIGB[0:65, 16384 + 1 * 2048:16384 + 2 * 2048]
        E["KH0"] = BIGB[0:65, 16384 + 2 * 2048:16384 + 3 * 2048]
        E["KH1"] = BIGB[0:65, 16384 + 3 * 2048:16384 + 4 * 2048]
        E["V65"] = BIGB[0:65, 16384 + 4 * 2048:16384 + 6 * 2048]
        E["WPR"] = sb(nc, "WPR", [128, 8192], BF16)
        E["ZF"] = E["WIT"][32:33, 0:2048]
        E["ZR"] = E["WIT"][0:1, 0:2048]
        E["OF"] = E["WIT"][64:128, 0:2048]
        E["M16"] = sb(nc, "M16", [128, 16], F32)
        E["MC4"] = sb(nc, "MC4", [128, 4], F32)
        E["MT"] = sb(nc, "MT", [16, 128], F32)
        E["TR1"] = E["MT"][0:1, 0:128]
        E["HMG"] = sb(nc, "HMG", [128, 32], F32)
        E["BG"] = sb(nc, "BG", [128, 32], F32)
        E["OG"] = sb(nc, "OG", [128, 32], F32)
        E["HB"] = sb(nc, "HB", [128, 32], F32)
        E["FCB"] = sb(nc, "FCB", [128, ND], F32)
    E["XLT"] = sb(nc, "XLT", [128, ND * 2048], BF16)
    SHRB = sb(nc, "SHRB", [128, 4160], BF16)
    E["QTC"] = SHRB[:, 0:2048]
    E["QB"] = SHRB[:, 2048:3072]
    if not final:
        E["VT"] = SHRB
        E["PT"] = E["QW1"][:, 0:2048]
    E["HIDC"] = SHRB[:, 3072:4096]
    E["T1H"] = E["T1W"][:, 0:512]
    E["SHB"] = sb(nc, "SHB", [128, 1], F32)
    E["AMHB"] = sb(nc, "AMHB", [128, 1], F32)
    E["C2J"] = E["AC32"]
    E["BO"] = sb(nc, "BO", [128, ND], F32)
    E["QW2"] = E["WPR"][:, 0:8192] if not final else sb(nc, "QW2", [128, 8192], BF16)
    E["PRF"] = E["SQS"]
    E["P11"] = E["AS2"]

    sd, gd, st = P.sem("sd"), P.sem("gd"), P.sem("st")

    def ldx(sy, xin=xin):
        sy.dma_start(out=E["X"][:, :].rearrange("p (t d) -> p t d", t=NT),
                     in_=xin.rearrange("(t p) d -> p t d", p=128)).then_inc(sd.h, 16)
        sy.dma_start(out=E["BI"][:], in_=b_in).then_inc(sd.h, 16)
        sy.dma_start(out=E["BO"][:], in_=b_out).then_inc(sd.h, 16)
        if not final:
            sy.dma_start(out=E["HMG"][:], in_=hmg).then_inc(sd.h, 16)
            sy.dma_start(out=E["BG"][:], in_=bgt).then_inc(sd.h, 16)
            sy.dma_start(out=E["OG"][:], in_=ogt).then_inc(sd.h, 16)
            sy.dma_start(out=E["FCB"][:], in_=fcb).then_inc(sd.h, 16)
    P.on("sync", ldx)
    x_mark = sd.inc(16 * (7 if not final else 3))

    (cg, cgn), cvv = _emit_const_loads(P, E, {"ident": ident}, attn_b=True)
    P.on("tensor", lambda t, cg=cg, cgn=cgn, cvv=cvv: (
        t.wait_ge(cg.h, cgn), t.wait_ge(cvv[0].h, cvv[1])))
    P.on("scalar", lambda s, xm=x_mark: s.wait_ge(sd.h, xm))
    kap = emit_kappa(P, E)
    vvs = P.sems["vv"]

    if not final:
        def hbv(v, x_mark=x_mark):
            v.wait_ge(sd.h, x_mark)
            v.tensor_tensor(out=E["HB"][:], in0=E["HMG"][:], in1=E["BG"][:], op=A.mult)
            v.drain()
            v.tensor_tensor(out=E["HB"][:], in0=E["HB"][:], in1=E["OG"][:], op=A.add)
            v.drain().then_inc(vvs.h, 1)
        P.on("vector", hbv)
        hb_mark = vvs.inc()
        P.on("tensor", lambda t, n=hb_mark: t.wait_ge(vvs.h, n))
        P.on("scalar", lambda s, n=hb_mark: s.wait_ge(vvs.h, n))

    amx = emit_amax_combine(P, E, pin, E["SAB"], E["AMB"], None, QA)
    amh = emit_amax_combine(P, E, phin, E["SHB"], E["AMHB"], amx, QA, scr=("AS1B", "AS2B"))
    ln1 = emit_ln(P, E, E["X"], E["XLN"], [(sd, x_mark)])
    qx = emit_quant_x(P, E, E["XLN"], E["RQ"], E["SAB"],
                      [ln1, amx, (vvs, kap[1])])
    rot_psT = Rot(2)
    rt = emit_tm2fm(P, E, E["RQ"], E["XLT"], qx, rot_psT)
    qp = emit_qin_prep(P, E, w_int, QIN, None, rot_psT=rot_psT)
    qo = emit_qout_prep(P, E, w_out, QOUT, qp)
    cj = emit_cscale(P, E, E["AC32"], E["AMB"], E["CJ"], [amx, qp])
    c2 = emit_cscale(P, E, E["AR32"], E["AMHB"], E["C2J"], [amh, qo])

    E["_tw"] = 1024
    res_marks = []
    for half in range(2):
        w1 = [rt, cj, c2] + ([res_marks[0]] if half == 1 else [])
        hq = emit_mm1(P, E, QIN, E["XLT"], "full", 1024 * half, 1024, E["HQ"], w1)
        rm = emit_mm2_residual(P, E, QOUT, E["HQ"], E["X"], E["BO"],
                               half, [hq, qo])
        res_marks.append(rm)
    ffn_done = res_marks[-1]

    if DBG.get("tap3"):
        dbg = nc.dram_tensor("dbg", [128, 128], F32, kind="ExternalOutput").ap()
        dsem = P.sem("dsem")

        def dtap(g, ffn_done=ffn_done):
            _wait(g, ffn_done)
            ctx = P.nc.allow_non_contiguous_dma(reason="debug tap")
            ctx.__enter__()
            g.dma_start(out=dbg[:, 0:1], in_=E["SAB"][:]).then_inc(dsem.h, 16)
            g.dma_start(out=dbg[:, 1:2], in_=E["SHB"][:]).then_inc(dsem.h, 16)
            g.dma_start(out=dbg[:, 2:3], in_=E["AMB"][:]).then_inc(dsem.h, 16)
            g.dma_start(out=dbg[:, 3:4], in_=E["AMHB"][:]).then_inc(dsem.h, 16)
            g.dma_start(out=dbg[:, 4:36], in_=E["CJ"][:]).then_inc(dsem.h, 16)
            g.dma_start(out=dbg[:, 36:68], in_=E["C2J"][:]).then_inc(dsem.h, 16)
            g.dma_start(out=dbg[:, 68:100], in_=E["AR32"][:]).then_inc(dsem.h, 16)
            g.wait_ge(dsem.h, 112)
        P.on("gpsimd", dtap)

    if final:
        gw = P.sem("gw")

        def stx(sy, ffn_done=ffn_done, gwn=gw.n):
            sy.wait_ge(gw.h, gwn)
            _wait(sy, ffn_done)
            sy.dma_start(out=xout.rearrange("(t p) d -> p t d", p=128),
                         in_=E["X"][:, :].rearrange("p (t d) -> p t d", t=NT)
                         ).then_inc(st.h, 16)
            sy.wait_ge(st.h, 16)
        P.on("sync", stx)
        P.attach()
        return nc

    ln2 = emit_ln(P, E, E["X"], E["XLN"], [ffn_done])
    rot_psT2 = Rot(2)
    xlt2 = emit_tm2fm(P, E, E["XLN"], E["XLT"], ln2, rot_psT2)
    af = emit_attention(P, E, "b", wg_b, E["XLT"], E["AFOUT"], xlt2,
                        extras={"HMG": E["HMG"], "HB": E["HB"], "MROW": MROW})
    res = emit_proj_residual(P, E, wot_b, E["AFOUT"], E["X"], af,
                             WOt=E["WPR"], BIAS=E["FCB"])
    ln3 = emit_ln(P, E, E["X"], E["XLN"], [res])
    pm = emit_amax_partial(P, E, E["XLN"], E["P11"], ln3)

    gw = P.sem("gw")

    def stx(sy, pm=pm, res=res, gwn=gw.n):
        sy.wait_ge(gw.h, gwn)
        _wait(sy, res)
        sy.dma_start(out=xout.rearrange("(t p) d -> p t d", p=128),
                     in_=E["X"][:, :].rearrange("p (t d) -> p t d", t=NT)
                     ).then_inc(st.h, 16)
        _wait(sy, pm)
        sy.dma_start(out=pout, in_=E["P11"][:]).then_inc(st.h, 16)
        sy.wait_ge(st.h, 32)
    P.on("sync", stx)
    P.attach()
    return nc


def emit_mm2_residual(P, E, QOUT, HQ, X, BO, half, waits):
    mm, ev, vv, gd = P.sem("mm"), P.sem("ev"), P.sem("vv"), P.sem("gd")
    psS, psO = E["psS"], E["psO"]
    QW2, PRF = E["QW2"], E["PRF"]
    rot_qw = Rot(2)
    rot_ps1 = Rot(2)
    rot_prf = Rot(2)
    rot_ps2 = Rot(4)
    last = None
    for m in range(ND):
        uW, ws, ww = rot_qw.acquire()

        def g(gp, m=m, ws=ws, ww=ww, QOUT=QOUT, waits=tuple(waits), pre=gd.n):
            if m == 0:
                for w in waits:
                    _wait(gp, w)
            gp.wait_ge(gd.h, pre)
            _wait(gp, ww)
            for k in range(NJ):
                gp.dma_start(out=QW2[:, 4096 * ws + 128 * k:4096 * ws + 128 * (k + 1)],
                             in_=QOUT[128 * k:128 * (k + 1), 128 * m:128 * (m + 1)]
                             ).then_inc(gd.h, 16)
        P.on("gpsimd", g)
        gm = gd.inc(16 * NJ)

        for n in range(2):
            u1, s1_, w1 = rot_ps1.acquire()

            def pe(t, m=m, n=n, ws=ws, s1_=s1_, w1=w1, gm=gm, HQ=HQ,
                   waits=tuple(waits)):
                t.wait_ge(gd.h, gm)
                if m == 0 and n == 0:
                    for w in waits:
                        _wait(t, w)
                _wait(t, w1)
                for k in range(NJ):
                    ins = t.matmul(psS[:, 512 * s1_:512 * (s1_ + 1)],
                                   QW2[:, 4096 * ws + 128 * k:4096 * ws + 128 * (k + 1)],
                                   HQ[:, 1024 * k + 512 * n:1024 * k + 512 * (n + 1)],
                                   start=(k == 0), stop=(k == NJ - 1))
                ins.then_inc(mm.h, 1)
            P.on("tensor", pe)
            m1 = mm.inc()

            uP, sp, wp = rot_prf.acquire()

            def sc(s, m=m, s1_=s1_, sp=sp, wp=wp, m1=m1):
                s.wait_ge(mm.h, m1)
                _wait(s, wp)
                s.activation(out=PRF[:, 512 * sp:512 * (sp + 1)],
                             in_=psS[:, 512 * s1_:512 * (s1_ + 1)],
                             func=AF.Identity, bias=BO[:, m:m + 1], scale=1.0)
                s.drain().then_inc(ev.h, 1)
            P.on("scalar", sc)
            m2 = ev.inc()
            rot_ps1.release(u1, (ev, m2))

            u2, s2_, w2 = rot_ps2.acquire()

            def pe2(t, sp=sp, s2_=s2_, w2=w2, m2=m2):
                t.wait_ge(ev.h, m2)
                _wait(t, w2)
                for j in range(4):
                    ins = t.transpose(psO[:, 512 * s2_ + 128 * j:512 * s2_ + 128 * (j + 1)],
                                      PRF[:, 512 * sp + 128 * j:512 * sp + 128 * (j + 1)],
                                      E["IDF"][:, :])
                ins.then_inc(mm.h, 1)
            P.on("tensor", pe2)
            m3 = mm.inc()

            def dv(v, m=m, n=n, half=half, s2_=s2_, m3=m3, X=X):
                v.wait_ge(mm.h, m3)
                for j in range(4):
                    tt = 8 * half + 4 * n + j
                    xsl = X[:, 1024 * tt + 128 * m:1024 * tt + 128 * (m + 1)]
                    v.tensor_tensor(out=xsl,
                                    in0=psO[:, 512 * s2_ + 128 * j:512 * s2_ + 128 * (j + 1)],
                                    in1=xsl, op=A.add)
                v.drain().then_inc(vv.h, 1)
            P.on("vector", dv)
            m4 = vv.inc()
            rot_prf.release(uP, (mm, m3))
            rot_ps2.release(u2, (vv, m4))
            last = (vv, m4)
        rot_qw.release(uW, (mm, mm.n))
    return last


def kernel(**inputs):
    host = _prep_host(inputs)
    x_full = inputs["L0_attn_a_x"].astype(np.float32, copy=False)
    cores = list(range(NCORES))

    nc1 = build_L1()
    maps = [{"x": np.ascontiguousarray(x_full[c]), "wg_a": host["wg_a"],
             "wot_a": host["wot_a"], "ident": host["ident"],
             "cmask": host["cmask"]} for c in cores]
    r1 = run_bass_kernel_spmd(nc1, maps, core_ids=cores)
    x1 = [np.ascontiguousarray(np.asarray(r["x1"], dtype=np.float32)) for r in r1.results]
    P1 = np.ascontiguousarray(np.stack(
        [np.asarray(r["p1"], dtype=np.float32).reshape(()) for r in r1.results]
    ).reshape(NCORES, 1))

    ncp = build_probe()
    maps = [{"xin": x1[c], "pin": P1, "w_int": host["w_int"],
             "b_in": host["b_in"], "ident": host["ident"]} for c in cores]
    r2 = run_bass_kernel_spmd(ncp, maps, core_ids=cores)
    P2 = np.ascontiguousarray(np.stack(
        [np.asarray(r["pout"], dtype=np.float32).reshape(()) for r in r2.results]
    ).reshape(NCORES, 1))

    nc3 = build_main(final=False)
    maps = [{"xin": x1[c], "pin": P1, "phin": P2, "w_int": host["w_int"],
             "w_out": host["w_out"], "b_in": host["b_in"], "b_out": host["b_out"],
             "ident": host["ident"], "wg_b": host["wg_b"], "hmg": host["hmg"],
             "bgt": host["bg"], "ogt": host["og"], "wot_b": host["wot_b"],
             "fcb": host["fcb"]} for c in cores]
    r3 = run_bass_kernel_spmd(nc3, maps, core_ids=cores)
    x3 = [np.ascontiguousarray(np.asarray(r["xout"], dtype=np.float32)) for r in r3.results]
    P3 = np.ascontiguousarray(np.stack(
        [np.asarray(r["pout"], dtype=np.float32).reshape(()) for r in r3.results]
    ).reshape(NCORES, 1))

    maps = [{"xin": x3[c], "pin": P3, "w_int": host["w_int"],
             "b_in": host["b_in"], "ident": host["ident"]} for c in cores]
    r4 = run_bass_kernel_spmd(ncp, maps, core_ids=cores)
    P4 = np.ascontiguousarray(np.stack(
        [np.asarray(r["pout"], dtype=np.float32).reshape(()) for r in r4.results]
    ).reshape(NCORES, 1))

    nc5 = build_main(final=True)
    maps = [{"xin": x3[c], "pin": P3, "phin": P4, "w_int": host["w_int"],
             "w_out": host["w_out"], "b_in": host["b_in"], "b_out": host["b_out"],
             "ident": host["ident"]} for c in cores]
    r5 = run_bass_kernel_spmd(nc5, maps, core_ids=cores)
    out = np.stack([np.asarray(r["xout"], dtype=np.float32) for r in r5.results])
    return out


# version 08 (final)
# speedup vs baseline: 1.2151x; 1.0240x over previous
import numpy as np
import concourse.bass as bass
import concourse.mybir as mybir
from concourse.alu_op_type import AluOpType as A
from concourse.bass_utils import run_bass_kernel_spmd

F32 = mybir.dt.float32
BF16 = mybir.dt.bfloat16
AF = mybir.ActivationFunctionType
AX = mybir.AxisListType

NCORES = 8
T = 2048
D = 1024
NT = T // 128
ND = D // 128
H = 16
DK = 64
NPAIR = 8
FF = 4096
NJ = FF // 128

C32 = 1.5 * 2.0 ** 23
C64 = 1.5 * 2.0 ** 52
E_NEG10 = float(np.float32(np.exp(np.float64(-10.0))))
E_POS10 = float(np.float32(np.exp(np.float64(10.0))))
LN_EPS = 1e-5
INV_D = 1.0 / D
QA = 32767.0
QW = 127.0


class Sem:
    def __init__(self, handle):
        self.h = handle
        self.n = 0

    def inc(self, k=1):
        self.n += k
        return self.n


class Rot:
    def __init__(self, nslots):
        self.k = nslots
        self.marks = []

    def acquire(self):
        u = len(self.marks)
        wait = self.marks[u - self.k] if u >= self.k else None
        self.marks.append(None)
        return u, u % self.k, wait

    def release(self, u, mark):
        self.marks[u] = mark


def _wait(eng, w):
    if w is not None:
        eng.wait_ge(w[0].h, w[1])


class Prog:
    ENGINES = ("sync", "tensor", "vector", "scalar", "gpsimd")

    def __init__(self, nc):
        self.nc = nc
        self.ops = {e: [] for e in self.ENGINES}
        self.sems = {}

    def sem(self, name):
        if name not in self.sems:
            h = self.nc.ctx.enter_context(self.nc.semaphore(name))
            self.sems[name] = Sem(h)
        return self.sems[name]

    def on(self, eng, fn):
        self.ops[eng].append(fn)

    def attach(self):
        cm = self.nc.Block()
        block = cm.__enter__()

        @block.sync
        def _(eng):
            for fn in self.ops["sync"]:
                fn(eng)

        @block.tensor
        def _(eng):
            for fn in self.ops["tensor"]:
                fn(eng)

        @block.vector
        def _(eng):
            for fn in self.ops["vector"]:
                fn(eng)

        @block.scalar
        def _(eng):
            for fn in self.ops["scalar"]:
                fn(eng)

        @block.gpsimd
        def _(eng):
            for fn in self.ops["gpsimd"]:
                fn(eng)

        cm.__exit__(None, None, None)


def sb(nc, name, shape, dt):
    return nc.ctx.enter_context(nc.sbuf_tensor(name, list(shape), dt))


def psum(nc, name, shape, dt):
    return nc.ctx.enter_context(nc.psum_tensor(name, list(shape), dt))



def emit_kappa(P, E):
    vv = P.sem("vv")

    def go(v, KAP=E["KAP"], KC64=E["KC64"], DD=E["DD"]):
        v.memset(KAP[:], 1.0)
        v.drain()
        v.tensor_scalar(out=KAP[:], in0=KAP[:], scalar1=C64, scalar2=None, op0=A.add)
        v.drain()
        v.tensor_scalar(out=KAP[:], in0=KAP[:], scalar1=-C64, scalar2=None, op0=A.add)
        v.drain()
        v.tensor_scalar(out=KC64[:], in0=KAP[:], scalar1=C64, scalar2=None, op0=A.mult)
        v.drain()
        v.tensor_scalar(out=DD[:], in0=KC64[:], scalar1=C32, scalar2=-1.0,
                        op0=A.add, op1=A.mult)
        v.drain().then_inc(vv.h, 1)
    P.on("vector", go)
    return ("vv", vv.inc())


def emit_ln(P, E, X, XLN, waits):
    vv, ev = P.sem("vv"), P.sem("ev")
    SU, SQ, M, V, R, MSQ, SQS = (E[k] for k in ("SU", "SQ", "M", "V", "R", "MSQ", "SQS"))

    ev0 = P.sem("ev")

    def sq1(s, X=X, waits=tuple(waits)):
        for w in waits:
            _wait(s, w)
        for tt in range(NT):
            s.activation(out=SQS[:], in_=X[:, 1024 * tt:1024 * (tt + 1)],
                         func=AF.Square, accum_out=SQ[:, tt:tt + 1])
            s.drain()
        s.drain().then_inc(ev0.h, 1)
    P.on("scalar", sq1)
    m_sq = ev0.inc()

    def d1(v, X=X, waits=tuple(waits), m_sq=m_sq):
        for w in waits:
            _wait(v, w)
        for tt in range(NT):
            xs = X[:, 1024 * tt:1024 * (tt + 1)]
            v.reduce_sum(out=SU[:, tt:tt + 1], in_=xs, axis=AX.X)
        v.wait_ge(ev0.h, m_sq)
        v.drain()
        v.tensor_scalar(out=M[:], in0=SU[:], scalar1=INV_D, scalar2=None, op0=A.mult)
        v.tensor_scalar(out=V[:], in0=SQ[:], scalar1=INV_D, scalar2=None, op0=A.mult)
        v.drain()
        v.tensor_tensor(out=MSQ[:], in0=M[:], in1=M[:], op=A.mult)
        v.drain()
        v.scalar_tensor_tensor(out=V[:], in0=V[:], scalar=LN_EPS, in1=MSQ[:],
                               op0=A.add, op1=A.subtract)
        v.drain().then_inc(vv.h, 1)
    P.on("vector", d1)
    m_var = vv.inc()

    def s1(s, m_var=m_var):
        s.wait_ge(vv.h, m_var)
        s.activation(out=R[:], in_=V[:], func=AF.Sqrt)
        s.drain().then_inc(ev.h, 1)
    P.on("scalar", s1)
    m_sd = ev.inc()

    def d2(v, X=X, XLN=XLN, m_sd=m_sd):
        v.wait_ge(ev.h, m_sd)
        v.reciprocal(out=R[:], in_=R[:])
        v.drain()
        for tt in range(NT):
            v.tensor_scalar(out=XLN[:, 1024 * tt:1024 * (tt + 1)],
                            in0=X[:, 1024 * tt:1024 * (tt + 1)],
                            scalar1=M[:, tt:tt + 1], scalar2=R[:, tt:tt + 1],
                            op0=A.subtract, op1=A.mult)
        v.drain().then_inc(vv.h, 1)
    P.on("vector", d2)
    return (vv, vv.inc())


def emit_tm2fm(P, E, SRC, DST, wait, rot_psT, n_src_cols=1024, bf=True):
    mm, ev = P.sem("mm"), P.sem("ev")
    psT = E["psT"]
    nch = n_src_cols // 128
    last = None
    first = True
    for dc in range(nch):
        for g4 in range(NT // 4):
            u, slot, w = rot_psT.acquire()

            def pe(t, dc=dc, g4=g4, slot=slot, w=w, SRC=SRC, first=first,
                   wait=wait, ncols=n_src_cols):
                if first and wait is not None:
                    _wait(t, wait)
                _wait(t, w)
                for j in range(4):
                    tt = 4 * g4 + j
                    ins = t.transpose(psT[:, 1024 * slot + 128 * j:1024 * slot + 128 * (j + 1)],
                                      SRC[:, ncols * tt + 128 * dc:ncols * tt + 128 * (dc + 1)],
                                      E["IDB"][:, :])
                ins.then_inc(mm.h, 1)
            P.on("tensor", pe)
            first = False
            m1 = mm.inc()

            def sc(s, dc=dc, g4=g4, slot=slot, m1=m1, DST=DST):
                s.wait_ge(mm.h, m1)
                s.copy(out=DST[:, 2048 * dc + 512 * g4:2048 * dc + 512 * (g4 + 1)],
                       in_=psT[:, 1024 * slot:1024 * slot + 512])
                s.drain().then_inc(ev.h, 1)
            P.on("scalar", sc)
            m2 = ev.inc()
            rot_psT.release(u, (ev, m2))
            last = (ev, m2)
    return last


def emit_amax_partial(P, E, XLN, OUT11, wait):
    mm, ev, vv = P.sem("mm"), P.sem("ev"), P.sem("vv")
    AM, AMR, TR1 = E["AM16"], E["AMR"], E["TR1"]

    def d1(v, XLN=XLN, wait=wait):
        _wait(v, wait)
        for tt in range(NT):
            v.tensor_reduce(out=AM[:, tt:tt + 1], in_=XLN[:, 1024 * tt:1024 * (tt + 1)],
                            axis=AX.X, op=A.max, apply_absolute_value=True)
        v.drain()
        v.tensor_reduce(out=AMR[:], in_=AM[:], axis=AX.X, op=A.max,
                        apply_absolute_value=True)
        v.drain().then_inc(vv.h, 1)
    P.on("vector", d1)
    m0 = vv.inc()

    def pe(t, m0=m0):
        t.wait_ge(vv.h, m0)
        t.transpose(E["psS"][0:1, 0:128], AMR[:], E["IDF"][:, :]).then_inc(mm.h, 1)
    P.on("tensor", pe)
    m1 = mm.inc()

    def sc(s, m1=m1):
        s.wait_ge(mm.h, m1)
        s.copy(out=TR1[:], in_=E["psS"][0:1, 0:128])
        s.drain().then_inc(ev.h, 1)
    P.on("scalar", sc)
    m2 = ev.inc()

    def d2(v, m2=m2, OUT11=OUT11):
        v.wait_ge(ev.h, m2)
        v.tensor_reduce(out=OUT11[0:1, 0:1], in_=TR1[:], axis=AX.X, op=A.max,
                        apply_absolute_value=True)
        v.drain().then_inc(vv.h, 1)
    P.on("vector", d2)
    return (vv, vv.inc())


def _chunks_for_tkb(tkb):
    out = []
    c = 128 * tkb
    while c < T:
        e = min(T, (c // 512 + 1) * 512)
        out.append((c, e))
        c = e
    return out


def emit_attention(P, E, kind, Wg, XLT, AFOUT, xlt_wait, extras=None):
    nc = P.nc
    mm, ev, vv, gd, gw, sd = (P.sem("mm"), P.sem("ev"), P.sem("vv"),
                              P.sem("gd"), P.sem("gw"), P.sem("sd"))
    psS, psO, psT = E["psS"], E["psO"], E["psT"]
    WPR, V65, VT, PT = E["WPR"], E["V65"], E["VT"], E["PT"]
    ZF, ZR, OF, ONESF = E["ZF"], E["ZR"], E["OF"], E["ONESF"]
    rot_ps1 = Rot(2)
    rot_wpr = Rot(2)
    rot_psT = Rot(2)
    rot_mrow = Rot(2)
    is_b = (kind == "b")
    if is_b:
        M16, MC4, MT = E["M16"], E["MC4"], E["MT"]
        HMG, HB = extras["HMG"], extras["HB"]
        MROW = extras["MROW"]
        QP = KP = None
    else:
        QP, KP = E["QP"], E["KP"]
        CM = E["CM"]

    pair_done = {}
    last_of_mark = None
    kh_ready = None
    if is_b:
        def khm(v, xlt_wait=xlt_wait):
            if xlt_wait is not None:
                _wait(v, xlt_wait)
            v.memset(E["KH0"][64:65, :], 1.0)
            v.memset(E["KH1"][64:65, :], 1.0)
            v.drain().then_inc(vv.h, 1)
        P.on("vector", khm)
        kh_ready = (vv, vv.inc())

    for p in range(NPAIR):
        uW, wslot, wwait = rot_wpr.acquire()
        wbase = 4096 * wslot

        def ld(g, p=p, wbase=wbase, wwait=wwait, Wg=Wg, pre=gd.n,
               xlt_wait=xlt_wait):
            g.wait_ge(gd.h, pre)
            if p < 2 and xlt_wait is not None:
                _wait(g, xlt_wait)
            _wait(g, wwait)
            for k in range(ND):
                g.dma_start(out=WPR[:, wbase + 512 * k:wbase + 512 * (k + 1)],
                            in_=Wg[128 * k:128 * (k + 1), 512 * p:512 * (p + 1)]
                            ).then_inc(gd.h, 16)
        P.on("gpsimd", ld)
        gmark = gd.inc(16 * ND)

        secs = [("q", 0, 128), ("k", 128, 128), ("v0", 256, 65), ("v1", 384, 65)]
        for sec, off, mw in secs:
            for n in range(4):
                u1, s1_, w1 = rot_ps1.acquire()
                pdst = psS[:, 512 * s1_:512 * (s1_ + 1)] if mw == 128 else \
                    psS[0:65, 512 * s1_:512 * s1_ + 512]

                def pe(t, p=p, off=off, mw=mw, n=n, s1_=s1_, w1=w1, gmark=gmark,
                       wbase=wbase, XLT=XLT, xlt_wait=xlt_wait, pdst=pdst):
                    t.wait_ge(gd.h, gmark)
                    if xlt_wait is not None:
                        _wait(t, xlt_wait)
                    _wait(t, w1)
                    for k in range(ND):
                        ins = t.matmul(pdst,
                                       WPR[:, wbase + 512 * k + off:wbase + 512 * k + off + mw],
                                       XLT[:, 2048 * k + 512 * n:2048 * k + 512 * (n + 1)],
                                       start=(k == 0), stop=(k == ND - 1))
                    ins.then_inc(mm.h, 1)
                P.on("tensor", pe)
                m1 = mm.inc()

                def sc(s, p=p, sec=sec, n=n, s1_=s1_, m1=m1):
                    s.wait_ge(mm.h, m1)
                    src128 = psS[:, 512 * s1_:512 * (s1_ + 1)]
                    src65 = psS[0:65, 512 * s1_:512 * s1_ + 512]
                    cols = slice(512 * n, 512 * (n + 1))
                    if not is_b:
                        if sec == "q":
                            s.copy(out=QP[:, cols], in_=src128)
                        elif sec == "k":
                            s.copy(out=KP[:, cols], in_=src128)
                        else:
                            hh = 0 if sec == "v0" else 1
                            s.activation(out=V65[:, 2048 * hh + 512 * n:2048 * hh + 512 * (n + 1)],
                                         in_=src65, func=AF.Identity,
                                         bias=E["VB"][:, :], scale=1.0)
                    else:
                        ci = 4 * p
                        if sec in ("q", "k"):
                            dsts = (E["QH0"], E["QH1"]) if sec == "q" else (E["KH0"], E["KH1"])
                            co = ci + (0 if sec == "q" else 1)
                            for hh in (0, 1):
                                s.activation(
                                    out=dsts[hh][0:64, cols],
                                    in_=psS[64 * hh:64 * hh + 64, 512 * s1_:512 * s1_ + 512],
                                    func=AF.Identity,
                                    scale=HMG[64 * hh:64 * hh + 64, co:co + 1],
                                    bias=HB[64 * hh:64 * hh + 64, co:co + 1])
                        else:
                            hh = 0 if sec == "v0" else 1
                            co = ci + 2 + hh
                            s.activation(out=V65[:, 2048 * hh + 512 * n:2048 * hh + 512 * (n + 1)],
                                         in_=src65, func=AF.Identity,
                                         scale=HMG[0:65, co:co + 1],
                                         bias=HB[0:65, co:co + 1])
                    s.drain().then_inc(ev.h, 1)
                P.on("scalar", sc)
                m2 = ev.inc()
                rot_ps1.release(u1, (ev, m2))
        qkv_done = (ev, ev.n)

        for hh in (0, 1):
            for c4 in range(4):
                u2, s2_, w2 = rot_psT.acquire()

                def pe(t, hh=hh, c4=c4, s2_=s2_, w2=w2, qkv_done=qkv_done):
                    _wait(t, qkv_done)
                    _wait(t, w2)
                    for j in range(4):
                        c = 4 * c4 + j
                        ins = t.transpose(psT[:, 1024 * s2_ + 128 * j:1024 * s2_ + 128 * j + 65],
                                          V65[0:65, 2048 * hh + 128 * c:2048 * hh + 128 * (c + 1)],
                                          E["IDB"][0:65, 0:65])
                    ins.then_inc(mm.h, 1)
                P.on("tensor", pe)
                m1 = mm.inc()

                def sc(s, hh=hh, c4=c4, s2_=s2_, m1=m1):
                    s.wait_ge(mm.h, m1)
                    src = psT[:, 1024 * s2_:1024 * s2_ + 512].rearrange(
                        "p (j c) -> p j c", j=4)[:, :, 0:65]
                    dst = VT[:, 2080 * hh + 260 * c4:2080 * hh + 260 * (c4 + 1)].rearrange(
                        "p (j c) -> p j c", j=4)
                    s.copy(out=dst, in_=src)
                    s.drain().then_inc(ev.h, 1)
                P.on("scalar", sc)
                m2 = ev.inc()
                rot_psT.release(u2, (ev, m2))
        vt_done = (ev, ev.n)

        def normalize_emit(h, av_done):
            def sc3(s, av_done=av_done):
                _wait(s, av_done)
                s.copy(out=ZF[:], in_=psO[64:65, :])
                s.drain().then_inc(ev.h, 1)
            P.on("scalar", sc3)
            m8 = ev.inc()

            def dv4(v, m8=m8):
                v.wait_ge(ev.h, m8)
                v.reciprocal(out=ZR[:], in_=ZF[:])
                v.drain().then_inc(vv.h, 1)
            P.on("vector", dv4)
            m9 = vv.inc()

            for n in range(4):
                u1, s1_, w1 = rot_ps1.acquire()

                def pe3(t, n=n, s1_=s1_, w1=w1, m9=m9):
                    t.wait_ge(vv.h, m9)
                    _wait(t, w1)
                    t.matmul(psS[0:64, 512 * s1_:512 * s1_ + 512],
                             ONESF[0:1, 0:64], ZR[0:1, 512 * n:512 * (n + 1)],
                             start=True, stop=True).then_inc(mm.h, 1)
                P.on("tensor", pe3)
                m10 = mm.inc()

                def sc4(s, n=n, m10=m10):
                    s.wait_ge(mm.h, m10)
                    s.copy(out=OF[:, 512 * n:512 * (n + 1)], in_=psO[0:64, 512 * n:512 * (n + 1)])
                    s.drain().then_inc(ev.h, 1)
                P.on("scalar", sc4)
                m11 = ev.inc()

                def dv5(v, n=n, h=h, s1_=s1_, m10=m10, m11=m11, AFOUT=AFOUT):
                    v.wait_ge(mm.h, m10)
                    v.wait_ge(ev.h, m11)
                    rb = 64 * (h % 2)
                    v.tensor_tensor(out=AFOUT[rb:rb + 64, 2048 * (h // 2) + 512 * n:
                                              2048 * (h // 2) + 512 * (n + 1)],
                                    in0=OF[:, 512 * n:512 * (n + 1)],
                                    in1=psS[0:64, 512 * s1_:512 * s1_ + 512],
                                    op=A.mult)
                    v.drain().then_inc(vv.h, 1)
                P.on("vector", dv5)
                m12 = vv.inc()
                rot_ps1.release(u1, (vv, m12))
            head_done = (vv, vv.n)

            def fence(t, head_done=head_done):
                t.wait_ge(head_done[0].h, head_done[1])
            P.on("tensor", fence)

        if is_b:
            def p1_steps(hh, vt_done=vt_done):
                QHh = E["QH0"] if hh == 0 else E["QH1"]
                KHh = E["KH0"] if hh == 0 else E["KH1"]
                steps = []
                for tqb in range(NT):
                    def step(tqb=tqb, QHh=QHh, KHh=KHh, vt_done=vt_done):
                        for cc in range(4):
                            u1, s1_, w1 = rot_ps1.acquire()

                            def pe(t, tqb=tqb, cc=cc, s1_=s1_, w1=w1, QHh=QHh,
                                   KHh=KHh, vt_done=vt_done):
                                _wait(t, vt_done)
                                _wait(t, w1)
                                t.matmul(psS[:, 512 * s1_:512 * (s1_ + 1)],
                                         QHh[0:64, 128 * tqb:128 * (tqb + 1)],
                                         KHh[0:64, 512 * cc:512 * (cc + 1)],
                                         start=True, stop=True).then_inc(mm.h, 1)
                            P.on("tensor", pe)
                            m1 = mm.inc()

                            def dv(v, cc=cc, s1_=s1_, m1=m1):
                                v.wait_ge(mm.h, m1)
                                v.tensor_reduce(out=MC4[:, cc:cc + 1],
                                                in_=psS[:, 512 * s1_:512 * (s1_ + 1)],
                                                axis=AX.X, op=A.max)
                                v.drain().then_inc(vv.h, 1)
                            P.on("vector", dv)
                            m2 = vv.inc()
                            rot_ps1.release(u1, (vv, m2))

                        def dv2(v, tqb=tqb):
                            v.tensor_reduce(out=M16[:, tqb:tqb + 1], in_=MC4[:],
                                            axis=AX.X, op=A.max)
                            v.drain()
                        P.on("vector", dv2)
                    steps.append(step)
                return steps

            def mrow_emit(hh):
                QHh = E["QH0"] if hh == 0 else E["QH1"]

                def dv3(v):
                    v.tensor_scalar(out=M16[:], in0=M16[:], scalar1=-1.0,
                                    scalar2=None, op0=A.mult)
                    v.drain().then_inc(vv.h, 1)
                P.on("vector", dv3)
                m3 = vv.inc()

                def pe2(t, m3=m3):
                    t.wait_ge(vv.h, m3)
                    t.transpose(psO[0:16, 0:128], M16[:], E["IDF"][:, :]).then_inc(mm.h, 1)
                P.on("tensor", pe2)
                m4 = mm.inc()

                def sc2(s, m4=m4):
                    s.wait_ge(mm.h, m4)
                    s.copy(out=MT[:], in_=psO[0:16, 0:128])
                    s.drain().then_inc(ev.h, 1)
                P.on("scalar", sc2)
                m5 = ev.inc()
                uM, mslot, wM = rot_mrow.acquire()

                def g1(sy, m5=m5, mslot=mslot, wM=wM, MROW=MROW, pre=gw.n,
                       sdn=sd.n):
                    sy.wait_ge(sd.h, sdn)
                    sy.wait_ge(gw.h, pre)
                    sy.wait_ge(ev.h, m5)
                    _wait(sy, wM)
                    sy.dma_start(out=MROW[mslot], in_=MT[:]).then_inc(gw.h, 16)
                P.on("sync", g1)
                m6 = gw.inc(16)

                def g2(g, m6=m6, mslot=mslot, MROW=MROW, QHh=QHh, pre=gd.n):
                    g.wait_ge(gd.h, pre)
                    g.wait_ge(gw.h, m6)
                    g.dma_start(out=QHh[64:65, :],
                                in_=MROW[mslot].rearrange("(o a) b -> o (a b)", o=1)
                                ).then_inc(gd.h, 16)
                P.on("gpsimd", g2)
                m7 = gd.inc(16)
                rot_mrow.release(uM, (gd, m7))
                return (gd, m7)

            def p2_steps(hh, row_ready):
                steps = []
                for tkb in range(NT):
                    def step(tkb=tkb, hh=hh, row_ready=row_ready):
                        for ci in range(4):
                            cs, ce = 512 * ci, 512 * (ci + 1)
                            u1, s1_, w1 = rot_ps1.acquire()
                            poff = 512 * s1_

                            def pe(t, tkb=tkb, cs=cs, ce=ce, s1_=s1_, w1=w1,
                                   poff=poff, hh=hh, row_ready=row_ready,
                                   kh_ready=kh_ready):
                                _wait(t, row_ready)
                                _wait(t, kh_ready)
                                _wait(t, w1)
                                lhs = (E["KH0"] if hh == 0 else E["KH1"])[0:65, 128 * tkb:128 * (tkb + 1)]
                                rhs = (E["QH0"] if hh == 0 else E["QH1"])[0:65, cs:ce]
                                t.matmul(psS[:, poff:poff + (ce - cs)], lhs, rhs,
                                         start=True, stop=True).then_inc(mm.h, 1)
                            P.on("tensor", pe)
                            m1 = mm.inc()

                            def sc(s, cs=cs, ce=ce, poff=poff, m1=m1):
                                s.wait_ge(mm.h, m1)
                                s.activation(out=PT[:, cs:ce], in_=psS[:, poff:poff + (ce - cs)],
                                             func=AF.Exp, scale=0.125)
                                s.drain().then_inc(ev.h, 1)
                            P.on("scalar", sc)
                            m2 = ev.inc()
                            rot_ps1.release(u1, (ev, m2))

                            def peav(t, tkb=tkb, cs=cs, ce=ce, m2=m2, hh=hh,
                                     startflag=(tkb == 0)):
                                t.wait_ge(ev.h, m2)
                                t.matmul(psO[0:65, cs:ce],
                                         VT[:, 2080 * hh + 65 * tkb:2080 * hh + 65 * (tkb + 1)],
                                         PT[:, cs:ce], start=startflag, stop=(tkb == NT - 1),
                                         skip_group_check=True).then_inc(mm.h, 1)
                            P.on("tensor", peav)
                            mm.inc()
                    steps.append(step)
                return steps

            s10 = p1_steps(0)
            for s in s10:
                s()
            r0 = mrow_emit(0)
            s20 = p2_steps(0, r0)
            s11 = p1_steps(1)
            for a, b in zip(s20, s11):
                a()
                b()
            normalize_emit(2 * p + 0, (mm, mm.n))
            r1 = mrow_emit(1)
            for s in p2_steps(1, r1):
                s()
            normalize_emit(2 * p + 1, (mm, mm.n))
            pair_done[p] = (mm, mm.n)
        else:
          for hh in (0, 1):
            h = 2 * p + hh
            row_ready = vt_done

            tkbs = range(NT)
            for tkb in tkbs:
                chunks = _chunks_for_tkb(tkb)
                for (cs, ce) in chunks:
                    u1, s1_, w1 = rot_ps1.acquire()
                    plen = ce - cs
                    poff = 512 * s1_ + (cs % 512)

                    def pe(t, tkb=tkb, cs=cs, ce=ce, s1_=s1_, w1=w1, poff=poff,
                           hh=hh, row_ready=row_ready, kh_ready=kh_ready):
                        _wait(t, row_ready)
                        _wait(t, kh_ready)
                        _wait(t, w1)
                        hb = 64 * hh
                        lhs = KP[hb:hb + 64, 128 * tkb:128 * (tkb + 1)]
                        rhs = QP[hb:hb + 64, cs:ce]
                        t.matmul(psS[:, poff:poff + (ce - cs)], lhs, rhs,
                                 start=True, stop=True).then_inc(mm.h, 1)
                    P.on("tensor", pe)
                    m1 = mm.inc()

                    def sc(s, cs=cs, ce=ce, poff=poff, m1=m1):
                        s.wait_ge(mm.h, m1)
                        s.activation(out=PT[:, cs:ce], in_=psS[:, poff:poff + (ce - cs)],
                                     func=AF.Exp, scale=0.125)
                        s.drain().then_inc(ev.h, 1)
                    P.on("scalar", sc)
                    m2 = ev.inc()
                    rot_ps1.release(u1, (ev, m2))

                    def dv(v, tkb=tkb, cs=cs, ce=ce, m2=m2, diag=(cs == 128 * tkb)):
                        v.wait_ge(ev.h, m2)
                        v.tensor_scalar(out=PT[:, cs:ce], in0=PT[:, cs:ce],
                                        scalar1=E_NEG10, scalar2=E_POS10,
                                        op0=A.max, op1=A.min)
                        if diag:
                            v.drain()
                            v.tensor_tensor(out=PT[:, cs:cs + 128],
                                            in0=PT[:, cs:cs + 128],
                                            in1=E["CM"][:, :], op=A.mult)
                        v.drain().then_inc(vv.h, 1)
                    P.on("vector", dv)
                    m3 = vv.inc()

                    def peav(t, tkb=tkb, cs=cs, ce=ce, m3=m3, hh=hh,
                             startflag=(tkb == 0)):
                        t.wait_ge(vv.h, m3)
                        t.matmul(psO[0:65, cs:ce],
                                 VT[:, 2080 * hh + 65 * tkb:2080 * hh + 65 * (tkb + 1)],
                                 PT[:, cs:ce], start=startflag, stop=(tkb == NT - 1),
                                 skip_group_check=True).then_inc(mm.h, 1)
                    P.on("tensor", peav)
                    mm.inc()
            av_done = (mm, mm.n)
            normalize_emit(h, av_done)
            pair_done[p] = (mm, mm.n)
        rot_wpr.release(uW, (mm, mm.n))
        last_of_mark = (vv, vv.n)
    return last_of_mark


def emit_proj_residual(P, E, Wt, AFIN, X, af_wait, nk=ND, WOt=None, BIAS=None):
    mm, ev, vv, gd = P.sem("mm"), P.sem("ev"), P.sem("vv"), P.sem("gd")
    psS, psO, PRF = E["psS"], E["psO"], E["PRF"]
    WO = WOt if WOt is not None else E["WO"]
    rot_wo = Rot(2)
    rot_ps2 = Rot(4)
    rot_prf = Rot(2)
    rot_ps1 = Rot(2)
    last = None
    for ec in range(ND):
        uW, wslot, wwait = rot_wo.acquire()

        def ld(g, ec=ec, wslot=wslot, wwait=wwait, Wt=Wt, pre=gd.n):
            g.wait_ge(gd.h, pre)
            _wait(g, wwait)
            for k in range(nk):
                g.dma_start(out=WO[:, 1024 * wslot + 128 * k:1024 * wslot + 128 * (k + 1)],
                            in_=Wt[128 * k:128 * (k + 1), 128 * ec:128 * (ec + 1)]
                            ).then_inc(gd.h, 16)
        P.on("gpsimd", ld)
        gmark = gd.inc(16 * nk)

        for n in range(4):
            u2, s2_, w2 = rot_ps2.acquire()

            def pe(t, ec=ec, n=n, s2_=s2_, w2=w2, gmark=gmark, wslot=wslot,
                   AFIN=AFIN, af_wait=af_wait):
                t.wait_ge(gd.h, gmark)
                if af_wait is not None:
                    _wait(t, af_wait)
                _wait(t, w2)
                for k in range(nk):
                    ins = t.matmul(psO[:, 512 * s2_:512 * (s2_ + 1)],
                                   WO[:, 1024 * wslot + 128 * k:1024 * wslot + 128 * (k + 1)],
                                   AFIN[:, 2048 * k + 512 * n:2048 * k + 512 * (n + 1)],
                                   start=(k == 0), stop=(k == nk - 1))
                ins.then_inc(mm.h, 1)
            P.on("tensor", pe)
            m1 = mm.inc()

            uP, sp, wp = rot_prf.acquire()

            def sc(s, ec=ec, s2_=s2_, sp=sp, wp=wp, m1=m1, BIAS=BIAS):
                s.wait_ge(mm.h, m1)
                _wait(s, wp)
                if BIAS is None:
                    s.copy(out=PRF[:, 512 * sp:512 * (sp + 1)],
                           in_=psO[:, 512 * s2_:512 * (s2_ + 1)])
                else:
                    s.activation(out=PRF[:, 512 * sp:512 * (sp + 1)],
                                 in_=psO[:, 512 * s2_:512 * (s2_ + 1)],
                                 func=AF.Identity, bias=BIAS[:, ec:ec + 1], scale=1.0)
                s.drain().then_inc(ev.h, 1)
            P.on("scalar", sc)
            m2 = ev.inc()
            rot_ps2.release(u2, (ev, m2))

            u1, s1_, w1 = rot_ps1.acquire()

            def pe2(t, sp=sp, s1_=s1_, w1=w1, m2=m2):
                t.wait_ge(ev.h, m2)
                _wait(t, w1)
                for j in range(4):
                    ins = t.transpose(psS[:, 512 * s1_ + 128 * j:512 * s1_ + 128 * (j + 1)],
                                      PRF[:, 512 * sp + 128 * j:512 * sp + 128 * (j + 1)],
                                      E["IDF"][:, :])
                ins.then_inc(mm.h, 1)
            P.on("tensor", pe2)
            m3 = mm.inc()

            def dv(v, ec=ec, n=n, s1_=s1_, m3=m3, X=X):
                v.wait_ge(mm.h, m3)
                for j in range(4):
                    tt = 4 * n + j
                    xsl = X[:, 1024 * tt + 128 * ec:1024 * tt + 128 * (ec + 1)]
                    v.tensor_tensor(out=xsl, in0=psS[:, 512 * s1_ + 128 * j:512 * s1_ + 128 * (j + 1)],
                                    in1=xsl, op=A.add)
                v.drain().then_inc(vv.h, 1)
            P.on("vector", dv)
            m4 = vv.inc()
            rot_prf.release(uP, (mm, m3))
            rot_ps1.release(u1, (vv, m4))
            last = (vv, m4)
        rot_wo.release(uW, (mm, mm.n))
    return last



def _common_tiles(nc, E, need_attn=True, attn_b=False, skip_tr1=False):
    E["KAP"] = sb(nc, "KAP", [128, 1], F32)
    E["KC64"] = sb(nc, "KC64", [128, 1], F32)
    E["DD"] = sb(nc, "DD", [128, 1], F32)
    for k in ("SU", "SQ", "M", "V", "R", "MSQ", "AM16"):
        E[k] = sb(nc, k, [128, 16], F32)
    E["SQS"] = sb(nc, "SQS", [128, 1024], F32)
    E["AMR"] = sb(nc, "AMR", [128, 1], F32)
    if not skip_tr1:
        E["TR1"] = sb(nc, "TR1", [1, 128], F32)
    E["IDB"] = sb(nc, "IDB", [128, 128], BF16)
    E["IDF"] = sb(nc, "IDF", [128, 128], F32)
    E["ONESF"] = sb(nc, "ONESF", [1, 128], F32)
    E["psS"] = psum(nc, "psS", [128, 1024], F32)
    E["psO"] = psum(nc, "psO", [128, 2048], F32)
    E["psT"] = psum(nc, "psT", [128, 2048], BF16)
    if need_attn:
        E["WPR"] = sb(nc, "WPR", [128, 8192], BF16)
        E["V65"] = sb(nc, "V65", [65, 4096], BF16)
        E["VT"] = sb(nc, "VT", [128, 4160], BF16)
        E["PT"] = sb(nc, "PT", [128, 2048], BF16)
        E["ZF"] = sb(nc, "ZF", [1, 2048], F32)
        E["ZR"] = sb(nc, "ZR", [1, 2048], F32)
        E["OF"] = sb(nc, "OF", [64, 2048], F32)
        E["VB"] = sb(nc, "VB", [65, 1], F32)
        if attn_b:
            E["QH0"] = sb(nc, "QH0", [65, 2048], BF16)
            E["QH1"] = sb(nc, "QH1", [65, 2048], BF16)
            E["KH0"] = sb(nc, "KH0", [65, 2048], BF16)
            E["KH1"] = sb(nc, "KH1", [65, 2048], BF16)
            E["M16"] = sb(nc, "M16", [128, 16], F32)
            E["MC4"] = sb(nc, "MC4", [128, 4], F32)
            E["MT"] = sb(nc, "MT", [16, 128], F32)
        else:
            E["QP"] = sb(nc, "QP", [128, 2048], BF16)
            E["KP"] = sb(nc, "KP", [128, 2048], BF16)
            E["CM"] = sb(nc, "CM", [128, 128], BF16)


def _emit_const_loads(P, E, consts, attn_b=False):
    gd = P.sem("gd")
    ident = consts["ident"]

    def g(gp, attn_b=attn_b):
        gp.dma_start(out=E["IDB"][:], in_=ident).then_inc(gd.h, 16)
        gp.dma_start(out=E["IDF"][:], in_=ident).then_inc(gd.h, 16)
        if not attn_b and "CM" in E:
            gp.dma_start(out=E["CM"][:], in_=consts["cmask"]).then_inc(gd.h, 16)
    P.on("gpsimd", g)
    n = gd.inc(16 * (3 if (not attn_b and "CM" in E) else 2))

    def v(vec):
        vec.memset(E["ONESF"][:], 1.0)
        if "VB" in E:
            vec.memset(E["VB"][:], 0.0)
            vec.drain()
            vec.memset(E["VB"][64:65, :], 1.0)
        vec.drain().then_inc(P.sem("vv").h, 1)
    P.on("vector", v)
    vvn = P.sem("vv").inc()
    return (P.sem("gd"), n), (P.sem("vv"), vvn)


DBG = {"tap": None}


def build_L1():
    nc = bass.Bass("TRN2", target_bir_lowering=False)
    x = nc.dram_tensor("x", [T, D], F32, kind="ExternalInput").ap()
    wg = nc.dram_tensor("wg_a", [D, NPAIR * 512], F32, kind="ExternalInput").ap()
    wot = nc.dram_tensor("wot_a", [D, D], F32, kind="ExternalInput").ap()
    ident = nc.dram_tensor("ident", [128, 128], F32, kind="ExternalInput").ap()
    cmask = nc.dram_tensor("cmask", [128, 128], F32, kind="ExternalInput").ap()
    x1 = nc.dram_tensor("x1", [T, D], F32, kind="ExternalOutput").ap()
    p1 = nc.dram_tensor("p1", [1, 1], F32, kind="ExternalOutput").ap()

    P = Prog(nc)
    E = {}
    _common_tiles(nc, E, need_attn=True, attn_b=False)
    E["X"] = sb(nc, "X", [128, NT * 1024], F32)
    E["XLN"] = sb(nc, "XLN", [128, NT * 1024], BF16)
    E["XLT"] = sb(nc, "XLT", [128, ND * 2048], BF16)
    E["AFOUT"] = E["XLN"]
    E["WO"] = sb(nc, "WO", [128, 2048], BF16)
    E["PRF"] = E["SQS"]
    E["P11"] = sb(nc, "P11", [1, 1], F32)

    sd, gd, vv = P.sem("sd"), P.sem("gd"), P.sem("vv")
    st = P.sem("st")

    def ldx(sy, x=x):
        sy.dma_start(out=E["X"][:, :].rearrange("p (t d) -> p t d", t=NT),
                     in_=x.rearrange("(t p) d -> p t d", p=128)).then_inc(sd.h, 16)
    P.on("sync", ldx)
    x_mark = sd.inc(16)

    (cg, cgn), cvv = _emit_const_loads(P, E, {"ident": ident, "cmask": cmask})

    P.on("tensor", lambda t, cg=cg, cgn=cgn, cvv=cvv: (
        t.wait_ge(cg.h, cgn), t.wait_ge(cvv[0].h, cvv[1])))
    P.on("scalar", lambda s, cg=cg, cgn=cgn, cvv=cvv: (
        s.wait_ge(cg.h, cgn), s.wait_ge(cvv[0].h, cvv[1])))

    ln1 = emit_ln(P, E, E["X"], E["XLN"], [(sd, x_mark)])
    rot_psT = Rot(2)
    xlt_done = emit_tm2fm(P, E, E["XLN"], E["XLT"], ln1, rot_psT)
    af_done = emit_attention(P, E, "a", wg, E["XLT"], E["AFOUT"],
                             (P.sem("ev"), xlt_done[1]) if False else xlt_done)
    res_done = emit_proj_residual(P, E, wot, E["AFOUT"], E["X"], af_done)
    ln2 = emit_ln(P, E, E["X"], E["XLN"], [res_done])
    pm = emit_amax_partial(P, E, E["XLN"], E["P11"], ln2)

    def stx(sy, pm=pm, res_done=res_done):
        _wait(sy, res_done)
        sy.dma_start(out=x1.rearrange("(t p) d -> p t d", p=128),
                     in_=E["X"][:, :].rearrange("p (t d) -> p t d", t=NT)
                     ).then_inc(st.h, 16)
        _wait(sy, pm)
        sy.dma_start(out=p1, in_=E["P11"][:]).then_inc(st.h, 16)
        sy.wait_ge(st.h, 32)
    P.on("sync", stx)

    if DBG.get("tap"):
        tap = DBG["tap"]
        dbg = nc.dram_tensor("dbg", [128, 16384], F32, kind="ExternalOutput").ap()
        src = {"xln": (E["XLN"], ln1), "xlt": (E["XLT"], xlt_done),
               "af": (E["AFOUT"], af_done)}[tap]
        dsem = P.sem("dsem")

        def dtap(g, src=src):
            _wait(g, src[1])
            g.dma_start(out=dbg, in_=src[0][:, :]).then_inc(dsem.h, 16)
            g.wait_ge(dsem.h, 16)
        P.on("gpsimd", dtap)
    P.attach()
    return nc



def _prep_host(inputs):
    f = np.ascontiguousarray
    out = {}
    out["ident"] = np.eye(128, dtype=np.float32)
    cm = np.ones((128, 128), dtype=np.float32)
    cm[np.triu_indices(128, k=1)] = 0.0
    cm = np.tril(np.ones((128, 128), dtype=np.float32), k=0)
    cm = np.ascontiguousarray(cm.T)
    cm = np.triu(np.ones((128, 128), dtype=np.float32), k=0)
    out["cmask"] = f(cm)

    Wqkv = inputs["L0_attn_a_Wqkv"]
    WT = Wqkv.T
    blocks = []
    for p in range(NPAIR):
        q = WT[:, 128 * p:128 * (p + 1)]
        k = WT[:, 1024 + 128 * p:1024 + 128 * (p + 1)]
        v0 = WT[:, 2048 + 128 * p:2048 + 128 * p + 64]
        v1 = WT[:, 2048 + 128 * p + 64:2048 + 128 * (p + 1)]
        z = np.zeros((D, 64), dtype=np.float32)
        blocks.append(np.concatenate([q, k, v0, z, v1, z], axis=1))
    out["wg_a"] = f(np.concatenate(blocks, axis=1))
    out["wot_a"] = f(inputs["L0_attn_a_Wout"].T)

    qw = inputs["L0_attn_b_qkv_w"]
    qb = inputs["L0_attn_b_qkv_b"]
    hm = inputs["L0_attn_b_head_mask"]
    QT = qw.T
    blocks = []
    hmg = np.zeros((NPAIR * 512,), dtype=np.float32)
    bg = np.zeros((NPAIR * 512,), dtype=np.float32)
    og = np.zeros((NPAIR * 512,), dtype=np.float32)
    for p in range(NPAIR):
        h0, h1 = 2 * p, 2 * p + 1
        cols = []
        meta = []
        for (h, c) in ((h0, 0), (h1, 0), (h0, 1), (h1, 1)):
            cols.append(QT[:, 192 * h + 64 * c:192 * h + 64 * (c + 1)])
            meta.append((h, c, 64))
        cols.append(QT[:, 192 * h0 + 128:192 * h0 + 192])
        meta.append((h0, 2, 64))
        cols.append(np.zeros((D, 64), dtype=np.float32))
        meta.append((None, None, 64))
        cols.append(QT[:, 192 * h1 + 128:192 * h1 + 192])
        meta.append((h1, 2, 64))
        cols.append(np.zeros((D, 64), dtype=np.float32))
        meta.append((None, None, 64))
        blk = np.concatenate(cols, axis=1)
        blocks.append(blk)
        off = 512 * p
        i = 0
        for (h, c, w) in meta:
            if h is not None:
                hmg[off + i:off + i + w] = hm[h]
                e0 = 192 * h + 64 * c
                bg[off + i:off + i + w] = qb[e0:e0 + w]
            else:
                og[off + i] = 1.0
            i += w
    out["wg_b"] = f(np.concatenate(blocks, axis=1))
    out["hmg"] = f(hmg.reshape(32, 128).T)
    out["bg"] = f(bg.reshape(32, 128).T)
    out["og"] = f(og.reshape(32, 128).T)
    out["wot_b"] = f(inputs["L0_attn_b_fc_w"].T)
    out["fcb"] = f(inputs["L0_attn_b_fc_b"].reshape(8, 128).T)

    out["w_in"] = f(inputs["L0_ffn_W_in"])
    out["w_int"] = f(inputs["L0_ffn_W_in"].T)
    out["w_out"] = f(inputs["L0_ffn_W_out"])
    out["b_in"] = f(inputs["L0_ffn_b_in"].reshape(32, 128).T)
    out["b_out"] = f(inputs["L0_ffn_b_out"].reshape(8, 128).T)
    return out



def emit_amax_combine(P, E, PIN, SAB, AMB, wait_sem_mark, qscale, scr=("AS1", "AS2")):
    mm, ev, vv, gd = P.sem("mm"), P.sem("ev"), P.sem("vv"), P.sem("gd")
    AS1, AS2 = E[scr[0]], E[scr[1]]
    psS = E["psS"]

    AIN = E["AIN"] if scr[0] == "AS1" else E["AINB"]

    def g(gp, PIN=PIN, AIN=AIN, pre=gd.n):
        gp.wait_ge(gd.h, pre)
        gp.dma_start(out=AIN[:], in_=PIN.rearrange("a b -> b a")).then_inc(gd.h, 16)
    P.on("gpsimd", g)
    gm = gd.inc(16)

    def dv(v, gm=gm):
        v.wait_ge(gd.h, gm)
        v.tensor_reduce(out=AS1[:], in_=AIN[:], axis=AX.X, op=A.max,
                        apply_absolute_value=True)
        v.drain()
        v.tensor_scalar(out=AS1[:], in0=AS1[:], scalar1=1e-8, scalar2=None, op0=A.max)
        v.drain()
        v.reciprocal(out=AS2[:], in_=AS1[:])
        v.drain()
        v.tensor_scalar(out=AS2[:], in0=AS2[:], scalar1=qscale, scalar2=None, op0=A.mult)
        v.drain().then_inc(vv.h, 1)
    P.on("vector", dv)
    m0 = vv.inc()

    prev = wait_sem_mark
    for (bsrc, dst) in ((AS2, SAB), (AS1, AMB)):
        def pe(t, bsrc=bsrc, m0=m0, prev=prev):
            t.wait_ge(vv.h, m0)
            _wait(t, prev)
            t.matmul(psS[0:128, 0:1], E["ONESF"][0:1, :], bsrc[0:1, 0:1],
                     start=True, stop=True).then_inc(mm.h, 1)
        P.on("tensor", pe)
        m1 = mm.inc()

        def sc(s, dst=dst, m1=m1):
            s.wait_ge(mm.h, m1)
            s.copy(out=dst[:], in_=psS[0:128, 0:1])
            s.drain().then_inc(ev.h, 1)
        P.on("scalar", sc)
        prev = (ev, ev.inc())
    return prev


def emit_quant_x(P, E, XLN, RQ, SAB, wait):
    vv = P.sem("vv")
    T1 = E["SQS"]

    def dv(v, XLN=XLN, RQ=RQ, SAB=SAB, wait=tuple(wait)):
        for w in wait:
            _wait(v, w)
        for tt in range(NT):
            v.tensor_scalar(out=T1[:], in0=XLN[:, 1024 * tt:1024 * (tt + 1)],
                            scalar1=SAB[:, 0:1], scalar2=C32, op0=A.mult, op1=A.add)
            v.drain()
            v.tensor_scalar(out=RQ[:, 1024 * tt:1024 * (tt + 1)], in0=T1[:],
                            scalar1=E["KC64"][:, 0:1], scalar2=E["DD"][:, 0:1],
                            op0=A.add, op1=A.add)
            v.drain()
        v.drain().then_inc(vv.h, 1)
    P.on("vector", dv)
    return (vv, vv.inc())


def emit_qin_prep(P, E, w_int, QIN, wait, rot_psT=None):
    mm, ev, vv, gd, gw, sd = (P.sem("mm"), P.sem("ev"), P.sem("vv"),
                              P.sem("gd"), P.sem("gw"), P.sem("sd"))
    WIT, QTC, QB = E["WIT"], E["QTC"], E["QB"]
    AC32, SCW = E["AC32"], E["SCW"]
    psT = E["psT"]
    rot_wit = Rot(2)
    if rot_psT is None:
        rot_psT = Rot(2)
    rot_qb = Rot(2)
    rot_qtc = Rot(2)
    last = None
    for j in range(NJ):
        uW, ws, ww = rot_wit.acquire()
        uQT, qs_j, wqt = rot_qtc.acquire()

        def g(gp, j=j, ws=ws, ww=ww, w_int=w_int, wait=wait, pre=gd.n):
            if j == 0 and wait is not None:
                _wait(gp, wait)
            gp.wait_ge(gd.h, pre)
            _wait(gp, ww)
            gp.dma_start(out=WIT[:, 1024 * ws:1024 * (ws + 1)],
                         in_=w_int[128 * j:128 * (j + 1), :]).then_inc(gd.h, 16)
        P.on("gpsimd", g)
        gm = gd.inc(16)

        def dv(v, j=j, ws=ws, gm=gm, qs_j=qs_j, wqt=wqt):
            v.wait_ge(gd.h, gm)
            _wait(v, wqt)
            wsl = WIT[:, 1024 * ws:1024 * (ws + 1)]
            v.tensor_reduce(out=AC32[:, j:j + 1], in_=wsl, axis=AX.X, op=A.max,
                            apply_absolute_value=True)
            v.drain()
            v.tensor_scalar(out=SCW[:, 0:1], in0=AC32[:, j:j + 1], scalar1=1e-8,
                            scalar2=None, op0=A.max)
            v.drain()
            v.reciprocal(out=SCW[:, 1:2], in_=SCW[:, 0:1])
            v.drain()
            v.tensor_scalar(out=SCW[:, 1:2], in0=SCW[:, 1:2], scalar1=QW,
                            scalar2=None, op0=A.mult)
            v.drain()
            v.tensor_scalar(out=E["T1W"][:], in0=wsl, scalar1=SCW[:, 1:2],
                            scalar2=C32, op0=A.mult, op1=A.add)
            v.drain()
            v.tensor_scalar(out=QTC[:, 1024 * qs_j:1024 * (qs_j + 1)],
                            in0=E["T1W"][:], scalar1=E["KC64"][:, 0:1],
                            scalar2=E["DD"][:, 0:1], op0=A.add, op1=A.add)
            v.drain().then_inc(vv.h, 1)
        P.on("vector", dv)
        m1 = vv.inc()
        rot_wit.release(uW, (vv, m1))

        for g4 in range(2):
            uT, ts_, wt = rot_psT.acquire()

            def pe(t, j=j, g4=g4, ts_=ts_, wt=wt, m1=m1, qs_j=qs_j):
                t.wait_ge(vv.h, m1)
                _wait(t, wt)
                for jj in range(4):
                    k = 4 * g4 + jj
                    ins = t.transpose(
                        psT[:, 1024 * ts_ + 128 * jj:1024 * ts_ + 128 * (jj + 1)],
                        QTC[:, 1024 * qs_j + 128 * k:1024 * qs_j + 128 * (k + 1)],
                        E["IDB"][:, :])
                ins.then_inc(mm.h, 1)
            P.on("tensor", pe)
            m2 = mm.inc()

            uQ, qs, wq = rot_qb.acquire()

            def sc(s, ts_=ts_, qs=qs, wq=wq, m2=m2):
                s.wait_ge(mm.h, m2)
                _wait(s, wq)
                s.copy(out=QB[:, 512 * qs:512 * (qs + 1)],
                       in_=psT[:, 1024 * ts_:1024 * ts_ + 512])
                s.drain().then_inc(ev.h, 1)
            P.on("scalar", sc)
            m3 = ev.inc()
            rot_psT.release(uT, (ev, m3))

            def g2(sy, j=j, g4=g4, qs=qs, m3=m3, QIN=QIN, pre=gw.n, sdn=sd.n):
                sy.wait_ge(sd.h, sdn)
                sy.wait_ge(gw.h, pre)
                sy.wait_ge(ev.h, m3)
                sy.dma_start(
                    out=QIN[j, 4 * g4:4 * (g4 + 1)].rearrange("k r c -> r k c"),
                    in_=QB[:, 512 * qs:512 * (qs + 1)].rearrange("p (k c) -> p k c", k=4)
                ).then_inc(gw.h, 16)
            P.on("sync", g2)
            m4 = gw.inc(16)
            rot_qb.release(uQ, (gw, m4))
            last = (gw, m4)
        rot_qtc.release(uQT, (mm, mm.n))
    return last


def emit_qout_prep(P, E, w_out, QOUT, wait):
    vv, gd, gw, sd = P.sem("vv"), P.sem("gd"), P.sem("gw"), P.sem("sd")
    WIT, QTC = E["WIT"], E["QTC"]
    AR32, SCW = E["AR32"], E["SCW"]
    rot_wit = Rot(2)
    rot_qtc = Rot(2)
    last = None
    for h in range(NJ):
        uW, ws, ww = rot_wit.acquire()

        def g(gp, h=h, ws=ws, ww=ww, w_out=w_out, wait=wait, pre=gd.n):
            if h == 0 and wait is not None:
                _wait(gp, wait)
            gp.wait_ge(gd.h, pre)
            _wait(gp, ww)
            gp.dma_start(out=WIT[:, 1024 * ws:1024 * (ws + 1)],
                         in_=w_out[128 * h:128 * (h + 1), :]).then_inc(gd.h, 16)
        P.on("gpsimd", g)
        gm = gd.inc(16)

        uQ, qs, wq = rot_qtc.acquire()

        def dv(v, h=h, ws=ws, qs=qs, wq=wq, gm=gm):
            v.wait_ge(gd.h, gm)
            _wait(v, wq)
            wsl = WIT[:, 1024 * ws:1024 * (ws + 1)]
            v.tensor_reduce(out=AR32[:, h:h + 1], in_=wsl, axis=AX.X, op=A.max,
                            apply_absolute_value=True)
            v.drain()
            v.tensor_scalar(out=SCW[:, 0:1], in0=AR32[:, h:h + 1], scalar1=1e-8,
                            scalar2=None, op0=A.max)
            v.drain()
            v.reciprocal(out=SCW[:, 1:2], in_=SCW[:, 0:1])
            v.drain()
            v.tensor_scalar(out=SCW[:, 1:2], in0=SCW[:, 1:2], scalar1=QW,
                            scalar2=None, op0=A.mult)
            v.drain()
            v.tensor_scalar(out=E["T1W"][:], in0=wsl, scalar1=SCW[:, 1:2],
                            scalar2=C32, op0=A.mult, op1=A.add)
            v.drain()
            v.tensor_scalar(out=QTC[:, 1024 * qs:1024 * (qs + 1)], in0=E["T1W"][:],
                            scalar1=E["KC64"][:, 0:1], scalar2=E["DD"][:, 0:1],
                            op0=A.add, op1=A.add)
            v.drain().then_inc(vv.h, 1)
        P.on("vector", dv)
        m1 = vv.inc()
        rot_wit.release(uW, (vv, m1))

        def g2(sy, h=h, qs=qs, m1=m1, QOUT=QOUT, pre=gw.n, sdn=sd.n):
            sy.wait_ge(sd.h, sdn)
            sy.wait_ge(gw.h, pre)
            sy.wait_ge(vv.h, m1)
            sy.dma_start(out=QOUT[128 * h:128 * (h + 1), :],
                         in_=QTC[:, 1024 * qs:1024 * (qs + 1)]).then_inc(gw.h, 16)
        P.on("sync", g2)
        m2 = gw.inc(16)
        rot_qtc.release(uQ, (gw, m2))
        last = (gw, m2)
    return last


def emit_cscale(P, E, SRC32, AMXB, DST32, wait):
    vv = P.sem("vv")

    def dv(v, SRC32=SRC32, AMXB=AMXB, DST32=DST32, wait=tuple(wait)):
        for w in wait:
            _wait(v, w)
        v.tensor_scalar(out=DST32[:], in0=SRC32[:], scalar1=AMXB[:, 0:1],
                        scalar2=1.0 / (QW * QA), op0=A.mult, op1=A.mult)
        v.drain().then_inc(vv.h, 1)
    P.on("vector", dv)
    return (vv, vv.inc())


def emit_mm1(P, E, QIN, RT, mode, tok_off, tok_w, HQ, waits):
    mm, ev, vv, gd = P.sem("mm"), P.sem("ev"), P.sem("vv"), P.sem("gd")
    psS = E["psS"]
    QW1, HIDC, T1H = E["QW1"], E["HIDC"], E["T1H"]
    CJ, BI = E["CJ"], E["BI"]
    rot_qw = Rot(2)
    rot_ps1 = Rot(2)
    rot_hid = Rot(2)
    nn = tok_w // 512
    last = None
    for j in range(NJ):
        uW, ws, ww = rot_qw.acquire()

        def g(gp, j=j, ws=ws, ww=ww, QIN=QIN, waits=tuple(waits), pre=gd.n):
            if j == 0:
                for w in waits:
                    _wait(gp, w)
            gp.wait_ge(gd.h, pre)
            _wait(gp, ww)
            gp.dma_start(out=QW1[:, 1024 * ws:1024 * (ws + 1)].rearrange(
                "p (k c) -> p k c", k=ND),
                in_=QIN[j].rearrange("k r c -> r k c")).then_inc(gd.h, 16)
        P.on("gpsimd", g)
        gm = gd.inc(16)

        for n in range(nn):
            u1, s1_, w1 = rot_ps1.acquire()

            def pe(t, j=j, n=n, ws=ws, s1_=s1_, w1=w1, gm=gm, RT=RT,
                   tok_off=tok_off, waits=tuple(waits)):
                t.wait_ge(gd.h, gm)
                if j == 0 and n == 0:
                    for w in waits:
                        _wait(t, w)
                _wait(t, w1)
                for k in range(ND):
                    ins = t.matmul(psS[:, 512 * s1_:512 * (s1_ + 1)],
                                   QW1[:, 1024 * ws + 128 * k:1024 * ws + 128 * (k + 1)],
                                   RT[:, 2048 * k + tok_off + 512 * n:
                                      2048 * k + tok_off + 512 * (n + 1)],
                                   start=(k == 0), stop=(k == ND - 1))
                ins.then_inc(mm.h, 1)
            P.on("tensor", pe)
            m1 = mm.inc()

            uH, hs, wh = rot_hid.acquire()

            def sc(s, j=j, s1_=s1_, hs=hs, wh=wh, m1=m1):
                s.wait_ge(mm.h, m1)
                _wait(s, wh)
                s.activation(out=HIDC[:, 512 * hs:512 * (hs + 1)],
                             in_=psS[:, 512 * s1_:512 * (s1_ + 1)],
                             func=AF.Gelu, scale=CJ[:, j:j + 1], bias=BI[:, j:j + 1])
                s.drain().then_inc(ev.h, 1)
            P.on("scalar", sc)
            m2 = ev.inc()
            rot_ps1.release(u1, (ev, m2))

            if mode == "probe":
                def dv(v, j=j, n=n, hs=hs, m2=m2):
                    v.wait_ge(ev.h, m2)
                    v.tensor_reduce(out=E["HMM"][:, 4 * j + n:4 * j + n + 1],
                                    in_=HIDC[:, 512 * hs:512 * (hs + 1)],
                                    axis=AX.X, op=A.max, apply_absolute_value=True)
                    v.drain().then_inc(vv.h, 1)
                P.on("vector", dv)
            else:
                def dv(v, j=j, n=n, hs=hs, m2=m2, HQ=HQ):
                    v.wait_ge(ev.h, m2)
                    hsl = HIDC[:, 512 * hs:512 * (hs + 1)]
                    v.tensor_scalar(out=T1H[:], in0=hsl, scalar1=E["SHB"][:, 0:1],
                                    scalar2=C32, op0=A.mult, op1=A.add)
                    v.drain()
                    v.tensor_scalar(out=T1H[:], in0=T1H[:], scalar1=E["KC64"][:, 0:1],
                                    scalar2=E["DD"][:, 0:1], op0=A.add, op1=A.add)
                    v.drain()
                    v.tensor_scalar(out=HQ[:, 1024 * j + 512 * n:1024 * j + 512 * (n + 1)],
                                    in0=T1H[:], scalar1=E["C2J"][:, j:j + 1],
                                    scalar2=None, op0=A.mult)
                    v.drain().then_inc(vv.h, 1)
                P.on("vector", dv)
            m3 = vv.inc()
            rot_hid.release(uH, (vv, m3))
            last = (vv, m3)
        rot_qw.release(uW, (mm, mm.n))
    return last


def emit_colmax_final(P, E, SRC, ncols, OUT11, wait):
    mm, ev, vv = P.sem("mm"), P.sem("ev"), P.sem("vv")

    def d1(v, SRC=SRC, ncols=ncols, wait=wait):
        _wait(v, wait)
        v.tensor_reduce(out=E["AMR"][:], in_=SRC[:, 0:ncols], axis=AX.X, op=A.max,
                        apply_absolute_value=True)
        v.drain().then_inc(vv.h, 1)
    P.on("vector", d1)
    m0 = vv.inc()

    def pe(t, m0=m0):
        t.wait_ge(vv.h, m0)
        t.transpose(E["psS"][0:1, 0:128], E["AMR"][:], E["IDF"][:, :]).then_inc(mm.h, 1)
    P.on("tensor", pe)
    m1 = mm.inc()

    def sc(s, m1=m1):
        s.wait_ge(mm.h, m1)
        s.copy(out=E["TR1"][:], in_=E["psS"][0:1, 0:128])
        s.drain().then_inc(ev.h, 1)
    P.on("scalar", sc)
    m2 = ev.inc()

    def d2(v, m2=m2, OUT11=OUT11):
        v.wait_ge(ev.h, m2)
        v.tensor_reduce(out=OUT11[0:1, 0:1], in_=E["TR1"][:], axis=AX.X, op=A.max,
                        apply_absolute_value=True)
        v.drain().then_inc(vv.h, 1)
    P.on("vector", d2)
    return (vv, vv.inc())


def _ffn_tiles(nc, E):
    E["AIN"] = sb(nc, "AIN", [1, NCORES], F32)
    E["AINB"] = E["SU"][0:1, 0:NCORES]
    E["AS1"] = sb(nc, "AS1", [1, 1], F32)
    E["AS2"] = sb(nc, "AS2", [1, 1], F32)
    E["AS1B"] = E["SQ"][0:1, 0:1]
    E["AS2B"] = E["SQ"][0:1, 1:2]
    E["SAB"] = sb(nc, "SAB", [128, 1], F32)
    E["AMB"] = sb(nc, "AMB", [128, 1], F32)
    E["WIT"] = sb(nc, "WIT", [128, 2048], F32)
    E["T1W"] = sb(nc, "T1W", [128, 1024], F32)
    E["QTC2"] = None
    E["AC32"] = sb(nc, "AC32", [128, NJ], F32)
    E["AR32"] = sb(nc, "AR32", [128, NJ], F32)
    E["SCW"] = sb(nc, "SCW", [128, 2], F32)
    E["CJ"] = sb(nc, "CJ", [128, NJ], F32)
    E["BI"] = sb(nc, "BI", [128, NJ], F32)
    E["QW1"] = sb(nc, "QW1", [128, 2048], BF16)


def build_probe():
    nc = bass.Bass("TRN2", target_bir_lowering=False)
    xin = nc.dram_tensor("xin", [T, D], F32, kind="ExternalInput").ap()
    pin = nc.dram_tensor("pin", [NCORES, 1], F32, kind="ExternalInput").ap()
    w_int = nc.dram_tensor("w_int", [FF, D], F32, kind="ExternalInput").ap()
    b_in = nc.dram_tensor("b_in", [128, NJ], F32, kind="ExternalInput").ap()
    ident = nc.dram_tensor("ident", [128, 128], F32, kind="ExternalInput").ap()
    pout = nc.dram_tensor("pout", [1, 1], F32, kind="ExternalOutput").ap()
    QIN = nc.dram_tensor("qin_scr", [NJ, ND, 128, 128], BF16, kind="Internal").ap()

    P = Prog(nc)
    E = {}
    _common_tiles(nc, E, need_attn=False)
    _ffn_tiles(nc, E)
    E["X"] = sb(nc, "X", [128, NT * 1024], F32)
    E["XLN"] = sb(nc, "XLN", [128, NT * 1024], BF16)
    E["RQ"] = sb(nc, "RQ", [128, NT * 1024], BF16)
    E["XLT"] = sb(nc, "XLT", [128, ND * 2048], BF16)
    E["HMM"] = sb(nc, "HMM", [128, 128], F32)
    E["QTC"] = sb(nc, "QTC", [128, 2048], BF16)
    E["QB"] = sb(nc, "QB", [128, 1024], BF16)
    E["HIDC"] = sb(nc, "HIDC", [128, 1024], BF16)
    E["T1H"] = E["T1W"][:, 0:512]
    E["P11"] = sb(nc, "P11", [1, 1], F32)

    sd, gd, st = P.sem("sd"), P.sem("gd"), P.sem("st")

    def ldx(sy, xin=xin):
        sy.dma_start(out=E["X"][:, :].rearrange("p (t d) -> p t d", t=NT),
                     in_=xin.rearrange("(t p) d -> p t d", p=128)).then_inc(sd.h, 16)
        sy.dma_start(out=E["BI"][:], in_=b_in).then_inc(sd.h, 16)
    P.on("sync", ldx)
    x_mark = sd.inc(32)

    (cg, cgn), cvv = _emit_const_loads(P, E, {"ident": ident})
    P.on("tensor", lambda t, cg=cg, cgn=cgn, cvv=cvv: (
        t.wait_ge(cg.h, cgn), t.wait_ge(cvv[0].h, cvv[1])))
    P.on("scalar", lambda s, xm=x_mark: s.wait_ge(sd.h, xm))
    kap = emit_kappa(P, E)

    amx = emit_amax_combine(P, E, pin, E["SAB"], E["AMB"], None, QA)
    ln = emit_ln(P, E, E["X"], E["XLN"], [(sd, x_mark)])
    qx = emit_quant_x(P, E, E["XLN"], E["RQ"], E["SAB"],
                      [ln, amx, (P.sems["vv"], kap[1])])
    rot_psT = Rot(2)
    rt = emit_tm2fm(P, E, E["RQ"], E["XLT"], qx, rot_psT)
    qp = emit_qin_prep(P, E, w_int, QIN, None, rot_psT=rot_psT)
    cj = emit_cscale(P, E, E["AC32"], E["AMB"], E["CJ"], [amx, qp])
    h = emit_mm1(P, E, QIN, E["XLT"], "probe", 0, T, None,
                 [rt, cj, (sd, x_mark)])
    pm = emit_colmax_final(P, E, E["HMM"], 128, E["P11"], h)

    gw = P.sem("gw")

    def stx(sy, pm=pm, gwn=gw.n):
        sy.wait_ge(gw.h, gwn)
        _wait(sy, pm)
        sy.dma_start(out=pout, in_=E["P11"][:]).then_inc(st.h, 16)
        sy.wait_ge(st.h, 16)
    P.on("sync", stx)
    P.attach()
    return nc


def build_main(final):
    nc = bass.Bass("TRN2", target_bir_lowering=False)
    xin = nc.dram_tensor("xin", [T, D], F32, kind="ExternalInput").ap()
    pin = nc.dram_tensor("pin", [NCORES, 1], F32, kind="ExternalInput").ap()
    phin = nc.dram_tensor("phin", [NCORES, 1], F32, kind="ExternalInput").ap()
    w_int = nc.dram_tensor("w_int", [FF, D], F32, kind="ExternalInput").ap()
    w_out = nc.dram_tensor("w_out", [FF, D], F32, kind="ExternalInput").ap()
    b_in = nc.dram_tensor("b_in", [128, NJ], F32, kind="ExternalInput").ap()
    b_out = nc.dram_tensor("b_out", [128, ND], F32, kind="ExternalInput").ap()
    ident = nc.dram_tensor("ident", [128, 128], F32, kind="ExternalInput").ap()
    if not final:
        wg_b = nc.dram_tensor("wg_b", [D, NPAIR * 512], F32, kind="ExternalInput").ap()
        hmg = nc.dram_tensor("hmg", [128, 32], F32, kind="ExternalInput").ap()
        bgt = nc.dram_tensor("bgt", [128, 32], F32, kind="ExternalInput").ap()
        ogt = nc.dram_tensor("ogt", [128, 32], F32, kind="ExternalInput").ap()
        wot_b = nc.dram_tensor("wot_b", [D, D], F32, kind="ExternalInput").ap()
        fcb = nc.dram_tensor("fcb", [128, ND], F32, kind="ExternalInput").ap()
        pout = nc.dram_tensor("pout", [1, 1], F32, kind="ExternalOutput").ap()
    xout = nc.dram_tensor("xout", [T, D], F32, kind="ExternalOutput").ap()
    QIN = nc.dram_tensor("qin_scr", [NJ, ND, 128, 128], BF16, kind="Internal").ap()
    QOUT = nc.dram_tensor("qout_scr", [FF, D], BF16, kind="Internal").ap()
    MROW = [nc.dram_tensor(f"mrow{i}", [16, 128], F32, kind="Internal").ap()
            for i in range(2)]

    P = Prog(nc)
    E = {}
    _common_tiles(nc, E, need_attn=False, skip_tr1=True)
    _ffn_tiles(nc, E)
    E["X"] = sb(nc, "X", [128, NT * 1024], F32)
    BIGB = sb(nc, "BIGB", [128, 2 * NT * 1024], BF16)
    E["XLN"] = BIGB[:, 0:16384]
    E["RQ"] = BIGB[:, 16384:32768]
    E["HQ"] = BIGB
    E["AFOUT"] = BIGB[:, 0:16384]
    if not final:
        E["QH0"] = BIGB[0:65, 16384 + 0 * 2048:16384 + 1 * 2048]
        E["QH1"] = BIGB[0:65, 16384 + 1 * 2048:16384 + 2 * 2048]
        E["KH0"] = BIGB[0:65, 16384 + 2 * 2048:16384 + 3 * 2048]
        E["KH1"] = BIGB[0:65, 16384 + 3 * 2048:16384 + 4 * 2048]
        E["V65"] = BIGB[0:65, 16384 + 4 * 2048:16384 + 6 * 2048]
        E["WPR"] = sb(nc, "WPR", [128, 8192], BF16)
        E["ZF"] = E["WIT"][32:33, 0:2048]
        E["ZR"] = E["WIT"][0:1, 0:2048]
        E["OF"] = E["WIT"][64:128, 0:2048]
        E["M16"] = sb(nc, "M16", [128, 16], F32)
        E["MC4"] = sb(nc, "MC4", [128, 4], F32)
        E["MT"] = sb(nc, "MT", [16, 128], F32)
        E["TR1"] = E["MT"][0:1, 0:128]
        E["HMG"] = sb(nc, "HMG", [128, 32], F32)
        E["BG"] = sb(nc, "BG", [128, 32], F32)
        E["OG"] = sb(nc, "OG", [128, 32], F32)
        E["HB"] = sb(nc, "HB", [128, 32], F32)
        E["FCB"] = sb(nc, "FCB", [128, ND], F32)
    E["XLT"] = sb(nc, "XLT", [128, ND * 2048], BF16)
    SHRB = sb(nc, "SHRB", [128, 4160], BF16)
    E["QTC"] = SHRB[:, 0:2048]
    E["QB"] = SHRB[:, 2048:3072]
    if not final:
        E["VT"] = SHRB
        E["PT"] = E["QW1"][:, 0:2048]
    E["HIDC"] = SHRB[:, 3072:4096]
    E["T1H"] = E["T1W"][:, 0:512]
    E["SHB"] = sb(nc, "SHB", [128, 1], F32)
    E["AMHB"] = sb(nc, "AMHB", [128, 1], F32)
    E["C2J"] = E["AC32"]
    E["BO"] = sb(nc, "BO", [128, ND], F32)
    E["QW2"] = E["WPR"][:, 0:8192] if not final else sb(nc, "QW2", [128, 8192], BF16)
    E["PRF"] = E["SQS"]
    E["P11"] = E["AS2"]

    sd, gd, st = P.sem("sd"), P.sem("gd"), P.sem("st")

    def ldx(sy, xin=xin):
        sy.dma_start(out=E["X"][:, :].rearrange("p (t d) -> p t d", t=NT),
                     in_=xin.rearrange("(t p) d -> p t d", p=128)).then_inc(sd.h, 16)
        sy.dma_start(out=E["BI"][:], in_=b_in).then_inc(sd.h, 16)
        sy.dma_start(out=E["BO"][:], in_=b_out).then_inc(sd.h, 16)
        if not final:
            sy.dma_start(out=E["HMG"][:], in_=hmg).then_inc(sd.h, 16)
            sy.dma_start(out=E["BG"][:], in_=bgt).then_inc(sd.h, 16)
            sy.dma_start(out=E["OG"][:], in_=ogt).then_inc(sd.h, 16)
            sy.dma_start(out=E["FCB"][:], in_=fcb).then_inc(sd.h, 16)
    P.on("sync", ldx)
    x_mark = sd.inc(16 * (7 if not final else 3))

    (cg, cgn), cvv = _emit_const_loads(P, E, {"ident": ident}, attn_b=True)
    P.on("tensor", lambda t, cg=cg, cgn=cgn, cvv=cvv: (
        t.wait_ge(cg.h, cgn), t.wait_ge(cvv[0].h, cvv[1])))
    P.on("scalar", lambda s, xm=x_mark: s.wait_ge(sd.h, xm))
    kap = emit_kappa(P, E)
    vvs = P.sems["vv"]

    if not final:
        def hbv(v, x_mark=x_mark):
            v.wait_ge(sd.h, x_mark)
            v.tensor_tensor(out=E["HB"][:], in0=E["HMG"][:], in1=E["BG"][:], op=A.mult)
            v.drain()
            v.tensor_tensor(out=E["HB"][:], in0=E["HB"][:], in1=E["OG"][:], op=A.add)
            v.drain().then_inc(vvs.h, 1)
        P.on("vector", hbv)
        hb_mark = vvs.inc()
        P.on("tensor", lambda t, n=hb_mark: t.wait_ge(vvs.h, n))
        P.on("scalar", lambda s, n=hb_mark: s.wait_ge(vvs.h, n))

    amx = emit_amax_combine(P, E, pin, E["SAB"], E["AMB"], None, QA)
    amh = emit_amax_combine(P, E, phin, E["SHB"], E["AMHB"], amx, QA, scr=("AS1B", "AS2B"))
    ln1 = emit_ln(P, E, E["X"], E["XLN"], [(sd, x_mark)])
    qx = emit_quant_x(P, E, E["XLN"], E["RQ"], E["SAB"],
                      [ln1, amx, (vvs, kap[1])])
    rot_psT = Rot(2)
    rt = emit_tm2fm(P, E, E["RQ"], E["XLT"], qx, rot_psT)
    qp = emit_qin_prep(P, E, w_int, QIN, None, rot_psT=rot_psT)
    qo = emit_qout_prep(P, E, w_out, QOUT, qp)
    cj = emit_cscale(P, E, E["AC32"], E["AMB"], E["CJ"], [amx, qp])
    c2 = emit_cscale(P, E, E["AR32"], E["AMHB"], E["C2J"], [amh, qo])

    E["_tw"] = 1024
    res_marks = []
    for half in range(2):
        w1 = [rt, cj, c2] + ([res_marks[0]] if half == 1 else [])
        hq = emit_mm1(P, E, QIN, E["XLT"], "full", 1024 * half, 1024, E["HQ"], w1)
        rm = emit_mm2_residual(P, E, QOUT, E["HQ"], E["X"], E["BO"],
                               half, [hq, qo])
        res_marks.append(rm)
    ffn_done = res_marks[-1]

    if DBG.get("tap3"):
        dbg = nc.dram_tensor("dbg", [128, 128], F32, kind="ExternalOutput").ap()
        dsem = P.sem("dsem")

        def dtap(g, ffn_done=ffn_done):
            _wait(g, ffn_done)
            ctx = P.nc.allow_non_contiguous_dma(reason="debug tap")
            ctx.__enter__()
            g.dma_start(out=dbg[:, 0:1], in_=E["SAB"][:]).then_inc(dsem.h, 16)
            g.dma_start(out=dbg[:, 1:2], in_=E["SHB"][:]).then_inc(dsem.h, 16)
            g.dma_start(out=dbg[:, 2:3], in_=E["AMB"][:]).then_inc(dsem.h, 16)
            g.dma_start(out=dbg[:, 3:4], in_=E["AMHB"][:]).then_inc(dsem.h, 16)
            g.dma_start(out=dbg[:, 4:36], in_=E["CJ"][:]).then_inc(dsem.h, 16)
            g.dma_start(out=dbg[:, 36:68], in_=E["C2J"][:]).then_inc(dsem.h, 16)
            g.dma_start(out=dbg[:, 68:100], in_=E["AR32"][:]).then_inc(dsem.h, 16)
            g.wait_ge(dsem.h, 112)
        P.on("gpsimd", dtap)

    if final:
        gw = P.sem("gw")

        def stx(sy, ffn_done=ffn_done, gwn=gw.n):
            sy.wait_ge(gw.h, gwn)
            _wait(sy, ffn_done)
            sy.dma_start(out=xout.rearrange("(t p) d -> p t d", p=128),
                         in_=E["X"][:, :].rearrange("p (t d) -> p t d", t=NT)
                         ).then_inc(st.h, 16)
            sy.wait_ge(st.h, 16)
        P.on("sync", stx)
        P.attach()
        return nc

    ln2 = emit_ln(P, E, E["X"], E["XLN"], [ffn_done])
    rot_psT2 = Rot(2)
    xlt2 = emit_tm2fm(P, E, E["XLN"], E["XLT"], ln2, rot_psT2)
    af = emit_attention(P, E, "b", wg_b, E["XLT"], E["AFOUT"], xlt2,
                        extras={"HMG": E["HMG"], "HB": E["HB"], "MROW": MROW})
    res = emit_proj_residual(P, E, wot_b, E["AFOUT"], E["X"], af,
                             WOt=E["WPR"], BIAS=E["FCB"])
    ln3 = emit_ln(P, E, E["X"], E["XLN"], [res])
    pm = emit_amax_partial(P, E, E["XLN"], E["P11"], ln3)

    gw = P.sem("gw")

    def stx(sy, pm=pm, res=res, gwn=gw.n):
        sy.wait_ge(gw.h, gwn)
        _wait(sy, res)
        sy.dma_start(out=xout.rearrange("(t p) d -> p t d", p=128),
                     in_=E["X"][:, :].rearrange("p (t d) -> p t d", t=NT)
                     ).then_inc(st.h, 16)
        _wait(sy, pm)
        sy.dma_start(out=pout, in_=E["P11"][:]).then_inc(st.h, 16)
        sy.wait_ge(st.h, 32)
    P.on("sync", stx)
    P.attach()
    return nc


def emit_mm2_residual(P, E, QOUT, HQ, X, BO, half, waits):
    mm, ev, vv, gd = P.sem("mm"), P.sem("ev"), P.sem("vv"), P.sem("gd")
    psS, psO = E["psS"], E["psO"]
    QW2, PRF = E["QW2"], E["PRF"]
    rot_qw = Rot(2)
    rot_ps1 = Rot(2)
    rot_prf = Rot(2)
    rot_ps2 = Rot(4)
    last = None
    for m in range(ND):
        uW, ws, ww = rot_qw.acquire()

        def g(gp, m=m, ws=ws, ww=ww, QOUT=QOUT, waits=tuple(waits), pre=gd.n):
            if m == 0:
                for w in waits:
                    _wait(gp, w)
            gp.wait_ge(gd.h, pre)
            _wait(gp, ww)
            for k in range(NJ):
                gp.dma_start(out=QW2[:, 4096 * ws + 128 * k:4096 * ws + 128 * (k + 1)],
                             in_=QOUT[128 * k:128 * (k + 1), 128 * m:128 * (m + 1)]
                             ).then_inc(gd.h, 16)
        P.on("gpsimd", g)
        gm = gd.inc(16 * NJ)

        for n in range(2):
            u1, s1_, w1 = rot_ps1.acquire()

            def pe(t, m=m, n=n, ws=ws, s1_=s1_, w1=w1, gm=gm, HQ=HQ,
                   waits=tuple(waits)):
                t.wait_ge(gd.h, gm)
                if m == 0 and n == 0:
                    for w in waits:
                        _wait(t, w)
                _wait(t, w1)
                for k in range(NJ):
                    ins = t.matmul(psS[:, 512 * s1_:512 * (s1_ + 1)],
                                   QW2[:, 4096 * ws + 128 * k:4096 * ws + 128 * (k + 1)],
                                   HQ[:, 1024 * k + 512 * n:1024 * k + 512 * (n + 1)],
                                   start=(k == 0), stop=(k == NJ - 1))
                ins.then_inc(mm.h, 1)
            P.on("tensor", pe)
            m1 = mm.inc()

            uP, sp, wp = rot_prf.acquire()

            def sc(s, m=m, s1_=s1_, sp=sp, wp=wp, m1=m1):
                s.wait_ge(mm.h, m1)
                _wait(s, wp)
                s.activation(out=PRF[:, 512 * sp:512 * (sp + 1)],
                             in_=psS[:, 512 * s1_:512 * (s1_ + 1)],
                             func=AF.Identity, bias=BO[:, m:m + 1], scale=1.0)
                s.drain().then_inc(ev.h, 1)
            P.on("scalar", sc)
            m2 = ev.inc()
            rot_ps1.release(u1, (ev, m2))

            u2, s2_, w2 = rot_ps2.acquire()

            def pe2(t, sp=sp, s2_=s2_, w2=w2, m2=m2):
                t.wait_ge(ev.h, m2)
                _wait(t, w2)
                for j in range(4):
                    ins = t.transpose(psO[:, 512 * s2_ + 128 * j:512 * s2_ + 128 * (j + 1)],
                                      PRF[:, 512 * sp + 128 * j:512 * sp + 128 * (j + 1)],
                                      E["IDF"][:, :])
                ins.then_inc(mm.h, 1)
            P.on("tensor", pe2)
            m3 = mm.inc()

            def dv(v, m=m, n=n, half=half, s2_=s2_, m3=m3, X=X):
                v.wait_ge(mm.h, m3)
                for j in range(4):
                    tt = 8 * half + 4 * n + j
                    xsl = X[:, 1024 * tt + 128 * m:1024 * tt + 128 * (m + 1)]
                    v.tensor_tensor(out=xsl,
                                    in0=psO[:, 512 * s2_ + 128 * j:512 * s2_ + 128 * (j + 1)],
                                    in1=xsl, op=A.add)
                v.drain().then_inc(vv.h, 1)
            P.on("vector", dv)
            m4 = vv.inc()
            rot_prf.release(uP, (mm, m3))
            rot_ps2.release(u2, (vv, m4))
            last = (vv, m4)
        rot_qw.release(uW, (mm, mm.n))
    return last


def kernel(**inputs):
    host = _prep_host(inputs)
    x_full = inputs["L0_attn_a_x"].astype(np.float32, copy=False)
    cores = list(range(NCORES))

    nc1 = build_L1()
    maps = [{"x": np.ascontiguousarray(x_full[c]), "wg_a": host["wg_a"],
             "wot_a": host["wot_a"], "ident": host["ident"],
             "cmask": host["cmask"]} for c in cores]
    r1 = run_bass_kernel_spmd(nc1, maps, core_ids=cores)
    x1 = [np.ascontiguousarray(np.asarray(r["x1"], dtype=np.float32)) for r in r1.results]
    P1 = np.ascontiguousarray(np.stack(
        [np.asarray(r["p1"], dtype=np.float32).reshape(()) for r in r1.results]
    ).reshape(NCORES, 1))

    ncp = build_probe()
    maps = [{"xin": x1[c], "pin": P1, "w_int": host["w_int"],
             "b_in": host["b_in"], "ident": host["ident"]} for c in cores]
    r2 = run_bass_kernel_spmd(ncp, maps, core_ids=cores)
    P2 = np.ascontiguousarray(np.stack(
        [np.asarray(r["pout"], dtype=np.float32).reshape(()) for r in r2.results]
    ).reshape(NCORES, 1))

    nc3 = build_main(final=False)
    maps = [{"xin": x1[c], "pin": P1, "phin": P2, "w_int": host["w_int"],
             "w_out": host["w_out"], "b_in": host["b_in"], "b_out": host["b_out"],
             "ident": host["ident"], "wg_b": host["wg_b"], "hmg": host["hmg"],
             "bgt": host["bg"], "ogt": host["og"], "wot_b": host["wot_b"],
             "fcb": host["fcb"]} for c in cores]
    r3 = run_bass_kernel_spmd(nc3, maps, core_ids=cores)
    x3 = [np.ascontiguousarray(np.asarray(r["xout"], dtype=np.float32)) for r in r3.results]
    P3 = np.ascontiguousarray(np.stack(
        [np.asarray(r["pout"], dtype=np.float32).reshape(()) for r in r3.results]
    ).reshape(NCORES, 1))

    maps = [{"xin": x3[c], "pin": P3, "w_int": host["w_int"],
             "b_in": host["b_in"], "ident": host["ident"]} for c in cores]
    r4 = run_bass_kernel_spmd(ncp, maps, core_ids=cores)
    P4 = np.ascontiguousarray(np.stack(
        [np.asarray(r["pout"], dtype=np.float32).reshape(()) for r in r4.results]
    ).reshape(NCORES, 1))

    nc5 = build_main(final=True)
    maps = [{"xin": x3[c], "pin": P3, "phin": P4, "w_int": host["w_int"],
             "w_out": host["w_out"], "b_in": host["b_in"], "b_out": host["b_out"],
             "ident": host["ident"]} for c in cores]
    r5 = run_bass_kernel_spmd(nc5, maps, core_ids=cores)
    out = np.stack([np.asarray(r["xout"], dtype=np.float32) for r in r5.results])
    return out
